# version 22
# baseline (speedup 1.0000x reference)
import numpy as np
import ml_dtypes
import concourse.bass as bass
import concourse.mybir as mybir
from concourse.bass_utils import run_bass_kernel_spmd
from contextlib import ExitStack

F32 = mybir.dt.float32
BF16 = mybir.dt.bfloat16
I32 = mybir.dt.int32
U8 = mybir.dt.uint8
AF = mybir.ActivationFunctionType
ALU = mybir.AluOpType
AX = mybir.AxisListType

D = 1024
NH = 16
HD = 64
NIC = 8
NCORES = 8
EPS = 1e-5
FP8MAX = 448.0
VELT_K = float(1.5 * 2.0 ** 20)
VELT_MIN = 24576.0

ENGS = ("sync", "tensor", "vector", "scalar", "gpsimd")


class Plan:
    def __init__(self):
        self.stream = {e: [] for e in ENGS}
        self.cnt = {}
        self.lastwait = {e: {} for e in ENGS}

    def sem(self, name):
        self.cnt[name] = 0

    def op(self, eng, fn, waits=(), inc=None, amt=1):
        w = []
        for t in waits:
            if t is None:
                continue
            s, v = t
            if self.lastwait[eng].get(s, 0) >= v:
                continue
            self.lastwait[eng][s] = v
            w.append((s, v))
        tok = None
        if inc is not None:
            self.cnt[inc] += amt
            tok = (inc, self.cnt[inc])
        self.stream[eng].append((fn, w, inc, amt))
        return tok


def _rope_tables(T):
    pos = np.arange(T, dtype=np.float64)[:, None]
    dim_pair = np.arange(0, HD, 2, dtype=np.float64)[None, :]
    theta = 10000.0 ** (-dim_pair / HD)
    freqs = pos * theta
    rope = np.concatenate([np.cos(freqs), np.sin(freqs)], axis=-1)
    ctab = rope[:, 0::2]
    stab = rope[:, 1::2]
    p = np.arange(128)
    C = ctab[:, p % 32].T
    sgn = np.where((p % 64) < 32, -1.0, 1.0)
    S = (stab[:, p % 32] * sgn[None, :]).T
    return C.astype(ml_dtypes.bfloat16), S.astype(ml_dtypes.bfloat16)


def build(T):
    NT = T // 128
    KC = T // 128
    QB = min(512, T)
    NQ = T // QB
    TH = min(1024, T)
    NHALF = T // TH
    S2W = 2 * QB

    nc = bass.Bass("TRN2", target_bir_lowering=False, debug=False,
                   dynamic_dma_scratch_size=4096)

    x_d = nc.dram_tensor("x", [T, D], F32, kind="ExternalInput")
    wq_d = nc.dram_tensor("wq", [128, 8 * D], F32, kind="ExternalInput")
    wk_d = nc.dram_tensor("wk", [128, 8 * D], F32, kind="ExternalInput")
    wv_d = nc.dram_tensor("wv", [128, 8 * D], F32, kind="ExternalInput")
    wo_d = nc.dram_tensor("wo", [128, 8 * D], F32, kind="ExternalInput")
    wf_d = nc.dram_tensor("wf", [128, 8 * D], F32, kind="ExternalInput")
    bq_d = nc.dram_tensor("bq", [128, 8], F32, kind="ExternalInput")
    bk_d = nc.dram_tensor("bk", [128, 8], F32, kind="ExternalInput")
    bv_d = nc.dram_tensor("bv", [128, 8], F32, kind="ExternalInput")
    bo_d = nc.dram_tensor("bo", [128, D], F32, kind="ExternalInput")
    fb_d = nc.dram_tensor("fb", [128, D], F32, kind="ExternalInput")
    out_d = nc.dram_tensor("out", [T, D], F32, kind="ExternalOutput")
    cc_in = nc.dram_tensor("cc_in", [1, 1], F32)
    cc_out = nc.dram_tensor("cc_out", [1, 1], F32)

    Cnp, Snp = _rope_tables(T)
    C_t = nc.inline_tensor(np.ascontiguousarray(Cnp), name="C_t")
    S_t = nc.inline_tensor(np.ascontiguousarray(Snp), name="S_t")
    id_t = nc.inline_tensor(np.eye(128, dtype=ml_dtypes.bfloat16), name="id_t")
    blkd = np.zeros((128, 33), dtype=ml_dtypes.bfloat16)
    blkd[0:64, 0] = 1.0
    blkd[64:128, 32] = 1.0
    blkd_t = nc.inline_tensor(blkd, name="blkd_t")
    blkA = np.zeros((1, 128), dtype=ml_dtypes.bfloat16)
    blkA[0, 0:64] = 1.0
    blkA_t = nc.inline_tensor(blkA, name="blkA_t")
    blkB = np.zeros((1, 128), dtype=ml_dtypes.bfloat16)
    blkB[0, 64:128] = 1.0
    blkB_t = nc.inline_tensor(blkB, name="blkB_t")
    blk64_t = nc.inline_tensor(np.ones((1, 64), ml_dtypes.bfloat16),
                               name="blk64_t")
    ones128b_t = nc.inline_tensor(np.ones((1, 128), ml_dtypes.bfloat16),
                                  name="ones128b_t")
    ones128_t = nc.inline_tensor(np.ones((1, 128), np.float32), name="ones128_t")
    c448_t = nc.inline_tensor(np.full((1, 1), FP8MAX, np.float32), name="c448_t")

    P = Plan()
    DMA_SEMS = ("sd", "sx0", "sx1", "so0", "so1", "sw0", "sw1", "gd", "gc")
    for s in ("pe", "dve", "act", "gp", "cc") + DMA_SEMS:
        P.sem(s)

    qT_o = 0
    kT_o = qT_o + 2 * 8 * T
    v_o = kT_o + 2 * 8 * T
    PT_o = v_o + 2 * NT * (NH * 65)
    wbuf_o = PT_o + 2 * 2 * KC * QB
    ln1T_o = wbuf_o + 2 * 8 * D
    AR = ln1T_o + 2 * 8 * T
    if T == 2048:
        y_o = 0
        wffq_o = 4 * 8 * T
        wroll_o = wffq_o + 2 * 8 * D
        ln2qT_o = PT_o
        oT_o = ln1T_o
        assert wroll_o + 2 * 4 * D <= PT_o
        assert 2 * 8 * T <= 2 * 2 * KC * QB
    else:
        y_o = AR
        wffq_o = y_o + 4 * 8 * T
        wroll_o = wffq_o + 2 * 8 * D
        ln2qT_o = wroll_o + 2 * 4 * D
        oT_o = ln2qT_o + 2 * 8 * T
        AR = oT_o + 2 * 8 * T

    with ExitStack() as ctx:
        e = ctx.enter_context
        arena = e(nc.sbuf_tensor("arena", [128, AR], U8))

        def view(off, nelem, dtype, nb):
            return arena[:, off:off + nelem * nb].bitcast(dtype)

        qT = view(qT_o, 8 * T, BF16, 2)
        kT = view(kT_o, 8 * T, BF16, 2)
        vtok = view(v_o, NT * NH * 65, BF16, 2)
        PT = view(PT_o, 2 * KC * QB, BF16, 2)
        wbuf = view(wbuf_o, 8 * D, BF16, 2)
        ln1T = view(ln1T_o, 8 * T, BF16, 2)
        yv = view(y_o, 8 * T, F32, 4)
        wffq = view(wffq_o, 8 * D, BF16, 2)
        wroll = [view(wroll_o, D, F32, 4), view(wroll_o + 4 * D, D, F32, 4)]
        ln2qT = view(ln2qT_o, 8 * T, BF16, 2)
        oT = view(oT_o, 8 * T, BF16, 2)

        Csb = e(nc.sbuf_tensor("Csb", [128, T], BF16))
        Ssb = e(nc.sbuf_tensor("Ssb", [128, T], BF16))
        idsb = e(nc.sbuf_tensor("idsb", [128, 128], BF16))
        blkdsb = e(nc.sbuf_tensor("blkdsb", [128, 33], BF16))
        blkAsb = e(nc.sbuf_tensor("blkAsb", [1, 128], BF16))
        blkBsb = e(nc.sbuf_tensor("blkBsb", [1, 128], BF16))
        blk64sb = e(nc.sbuf_tensor("blk64sb", [1, 64], BF16))
        ones128b = e(nc.sbuf_tensor("ones128b", [1, 128], BF16))
        bosr = e(nc.sbuf_tensor("bosr", [1, D], BF16))
        r2f = e(nc.sbuf_tensor("r2f", [1, 512], F32))
        negmr = e(nc.sbuf_tensor("negmr", [128, 1], F32))
        negm2 = e(nc.sbuf_tensor("negm2", [128, 1], F32))
        negm3 = e(nc.sbuf_tensor("negm3", [128, 1], F32))
        ones128 = e(nc.sbuf_tensor("ones128", [1, 128], F32))
        c448 = e(nc.sbuf_tensor("c448", [1, 1], F32))
        bqsb = e(nc.sbuf_tensor("bqsb", [128, 8], F32))
        bksb = e(nc.sbuf_tensor("bksb", [128, 8], F32))
        bvsb = e(nc.sbuf_tensor("bvsb", [128, 8], F32))
        if T == 2048:
            fbsb = view(oT_o, D, F32, 4)
        else:
            fbsb = e(nc.sbuf_tensor("fbsb", [128, D], F32))[:, :]
        xb = [e(nc.sbuf_tensor("xb0", [128, D], F32)),
              e(nc.sbuf_tensor("xb1", [128, D], F32))]
        lnc = e(nc.sbuf_tensor("lnc", [128, D], BF16))
        qbh = e(nc.sbuf_tensor("qbh", [128, TH], BF16))
        qsw = e(nc.sbuf_tensor("qsw", [128, TH], BF16))
        r1t = e(nc.sbuf_tensor("r1t", [128, TH], BF16))
        r2t = e(nc.sbuf_tensor("r2t", [128, TH], BF16))
        if T == 2048:
            vb = view(PT_o, T, BF16, 2)
            vb2 = view(PT_o + 2 * T, T, BF16, 2)
            qk_sw = view(PT_o + 2 * T, T, BF16, 2)
            qk_r1 = view(PT_o + 4 * T, T, BF16, 2)
            qk_r2 = view(PT_o + 6 * T, T, BF16, 2)
            qk_b = [view(PT_o + 8 * T, T, BF16, 2),
                    view(PT_o + 10 * T, T, BF16, 2)]
        else:
            vb = e(nc.sbuf_tensor("vb", [128, T], BF16))[:, :]
            vb2 = e(nc.sbuf_tensor("vb2", [128, T], BF16))[:, :]
            qk_sw = e(nc.sbuf_tensor("qk_sw", [128, T], BF16))[:, :]
            qk_r1 = e(nc.sbuf_tensor("qk_r1", [128, T], BF16))[:, :]
            qk_r2 = e(nc.sbuf_tensor("qk_r2", [128, T], BF16))[:, :]
            qk_b = [e(nc.sbuf_tensor("qk_b0", [128, T], BF16))[:, :],
                    e(nc.sbuf_tensor("qk_b1", [128, T], BF16))[:, :]]
        stat = e(nc.sbuf_tensor("stat", [128, 12], F32))
        mv = e(nc.sbuf_tensor("mv", [128, 2], F32))
        sdt = e(nc.sbuf_tensor("sdt", [128, 1], F32))
        rstd = e(nc.sbuf_tensor("rstd", [128, 1], F32))
        m2c = e(nc.sbuf_tensor("m2c", [33, 1], F32))
        mq2 = e(nc.sbuf_tensor("mq2", [1, NH], F32))
        mk2 = e(nc.sbuf_tensor("mk2", [1, NH], F32))
        mrow = e(nc.sbuf_tensor("mrow", [1, NH], F32))
        mpair = e(nc.sbuf_tensor("mpair", [1, NH // 2], F32))
        mh_bc = e(nc.sbuf_tensor("mh_bc", [128, NH // 2], F32))
        r2a = e(nc.sbuf_tensor("r2a", [1, QB], BF16))
        r2bb = e(nc.sbuf_tensor("r2bb", [1, QB], BF16))
        rfb = e(nc.sbuf_tensor("rfb", [128, QB], BF16))
        if T == 2048:
            vq3 = Ssb.ap().bitcast(F32)
            lnb = Csb.ap().bitcast(F32)
            zq2 = view(oT_o + 4 * D, D, F32, 4)
            cq2 = view(oT_o + 8 * D, D, F32, 4)
            vq32 = view(oT_o + 12 * D, D, F32, 4)
            lq2 = view(oT_o + 16 * D, D, BF16, 2)
            ffnb2 = view(oT_o + 16 * D + 2 * D, D, BF16, 2)
            lnb2 = view(oT_o + 16 * D + 4 * D, D, F32, 4)
        else:
            vq3 = e(nc.sbuf_tensor("vq3", [128, D], F32))[:, :]
            lnb = e(nc.sbuf_tensor("lnb", [128, D], F32))[:, :]
            zq2 = e(nc.sbuf_tensor("zq2", [128, D], F32))[:, :]
            cq2 = e(nc.sbuf_tensor("cq2", [128, D], F32))[:, :]
            vq32 = e(nc.sbuf_tensor("vq32", [128, D], F32))[:, :]
            lq2 = e(nc.sbuf_tensor("lq2", [128, D], BF16))[:, :]
            ffnb2 = e(nc.sbuf_tensor("ffnb2", [128, D], BF16))[:, :]
            lnb2 = e(nc.sbuf_tensor("lnb2", [128, D], F32))[:, :]
        attn = vq3
        lncm = e(nc.sbuf_tensor("lncm", [128, NT], F32))
        lncr = e(nc.sbuf_tensor("lncr", [128, NT], F32))
        rs2 = e(nc.sbuf_tensor("rs2", [128, NT], F32))
        amx = [e(nc.sbuf_tensor("amx0", [128, 1], F32)),
               e(nc.sbuf_tensor("amx1", [128, 1], F32))]
        wmx = [e(nc.sbuf_tensor("wmx0", [128, 1], F32)),
               e(nc.sbuf_tensor("wmx1", [128, 1], F32))]
        u1 = e(nc.sbuf_tensor("u1", [128, 1], F32))
        u2 = e(nc.sbuf_tensor("u2", [128, 1], F32))
        gu1 = e(nc.sbuf_tensor("gu1", [128, 1], F32))
        gu2 = e(nc.sbuf_tensor("gu2", [128, 1], F32))
        gtrt = e(nc.sbuf_tensor("gtrt", [128, 1], F32))
        trt = e(nc.sbuf_tensor("trt", [128, 1], F32))
        pad32 = e(nc.sbuf_tensor("pad32", [128, 32], F32))
        tr32 = e(nc.sbuf_tensor("tr32", [128, 32], F32))
        rowv = e(nc.sbuf_tensor("rowv", [1, 4], F32))
        zq = xb[0]
        cq = xb[1]
        lq = e(nc.sbuf_tensor("lq", [128, D], BF16))
        ffnb = e(nc.sbuf_tensor("ffnb", [128, D], BF16))
        if T == 2048:
            outb = [view(wbuf_o, D, F32, 4), view(wbuf_o + 4 * D, D, F32, 4)]
        else:
            outb = [e(nc.sbuf_tensor("outb0", [128, D], F32))[:, :],
                    e(nc.sbuf_tensor("outb1", [128, D], F32))[:, :]]
        scl = e(nc.sbuf_tensor("scl", [1, 8], F32))
        wscl = e(nc.sbuf_tensor("wscl", [1, 8], F32))
        dqt = e(nc.sbuf_tensor("dqt", [1, 1], F32))
        ccs = e(nc.sbuf_tensor("ccs", [1, 1], F32))
        epsb = e(nc.sbuf_tensor("epsb", [128, 1], F32))
        xs_bc = e(nc.sbuf_tensor("xs_bc", [128, 1], F32))
        ws_bc = e(nc.sbuf_tensor("ws_bc", [128, 1], F32))
        dq_bc = e(nc.sbuf_tensor("dq_bc", [128, 1], F32))
        zerob = e(nc.sbuf_tensor("zerob", [128, 1], F32))
        lnstT = [e(nc.sbuf_tensor("lnstT0", [128, 8], F32)),
                 e(nc.sbuf_tensor("lnstT1", [128, 8], F32)),
                 e(nc.sbuf_tensor("lnstT2", [128, 8], F32))]
        lnstS = [e(nc.sbuf_tensor("lnstS0", [128, 8], F32)),
                 e(nc.sbuf_tensor("lnstS1", [128, 8], F32))]
        lnstD = [e(nc.sbuf_tensor("lnstD0", [128, 8], F32)),
                 e(nc.sbuf_tensor("lnstD1", [128, 8], F32))]
        gu1b = e(nc.sbuf_tensor("gu1b", [128, 1], F32))
        gu2b = e(nc.sbuf_tensor("gu2b", [128, 1], F32))
        gtrtb = e(nc.sbuf_tensor("gtrtb", [128, 1], F32))
        if T == 2048:
            zqW = view(PT_o, D, F32, 4)
            cqW = view(PT_o + 4 * D, D, F32, 4)
            vqW = view(PT_o + 8 * D, D, F32, 4)
        else:
            zqW = e(nc.sbuf_tensor("zqW", [128, D], F32))[:, :]
            cqW = e(nc.sbuf_tensor("cqW", [128, D], F32))[:, :]
            vqW = e(nc.sbuf_tensor("vqW", [128, D], F32))[:, :]
        if T == 2048:
            scT = [view(PT_o + 12 * 1024, D, BF16, 2),
                   view(PT_o + 12 * 1024 + 2 * D, D, BF16, 2),
                   view(PT_o + 12 * 1024 + 8 * D, D, BF16, 2)]
            scS = [view(PT_o + 12 * 1024 + 4 * D, D, BF16, 2),
                   view(PT_o + 12 * 1024 + 6 * D, D, BF16, 2)]
        else:
            scT = [e(nc.sbuf_tensor("scT0", [128, D], BF16))[:, :],
                   e(nc.sbuf_tensor("scT1", [128, D], BF16))[:, :],
                   e(nc.sbuf_tensor("scT2", [128, D], BF16))[:, :]]
            scS = [e(nc.sbuf_tensor("scS0", [128, D], BF16))[:, :],
                   e(nc.sbuf_tensor("scS1", [128, D], BF16))[:, :]]

        ps32a = e(nc.psum_tensor("ps32a", [128, 2048], F32))
        ps32b = e(nc.psum_tensor("ps32b", [128, 2048], F32))
        psbf = ps32b[:, 0:512].bitcast(BF16)

        sems = {}
        for s in ("pe", "dve", "act", "gp", "cc") + DMA_SEMS:
            sems[s] = e(nc.semaphore(s))

        def dma(eng, o, i, waits=(), q="sd"):
            pre = (q, P.cnt[q]) if P.cnt[q] > 0 else None
            return P.op(eng, lambda en, o=o, i=i: en.dma_start(out=o, in_=i),
                        waits=list(waits) + [pre], inc=q, amt=16)

        def dve(fn, waits=(), inc="dve"):
            return P.op("vector", fn, waits=waits, inc=inc)

        def act(fn, waits=(), inc="act"):
            return P.op("scalar", fn, waits=waits, inc=inc)

        def pe(fn, waits=(), inc="pe"):
            return P.op("tensor", fn, waits=waits, inc=inc)

        def pe_noinc(fn, waits=()):
            return P.op("tensor", fn, waits=waits, inc=None)

        def gp(fn, waits=(), inc="gp"):
            return P.op("gpsimd", fn, waits=waits, inc=inc)

        tok_eps = P.op("vector", lambda v: v.memset(epsb[:, :], EPS), inc="dve")
        t_consts = []
        for dst, src in ((Csb, C_t), (Ssb, S_t), (idsb, id_t), (blkdsb, blkd_t),
                         (blkAsb, blkA_t), (blkBsb, blkB_t), (blk64sb, blk64_t),
                         (ones128b, ones128b_t),
                         (ones128, ones128_t), (c448, c448_t),
                         (bqsb, bq_d), (bksb, bk_d), (bvsb, bv_d)):
            t_consts.append(dma("sync", dst[:, :], src[:, :]))
        tok_consts = t_consts[-1]
        tok_bosr = dma("gpsimd", bosr[0:1, :], bo_d[0:1, :], q="gd")

        ln_norm_tok = [None, None]
        prev_t6p1 = None
        ev1_tok = None
        ln1T_toks = []
        for tc in range(NT):
            par = tc % 2
            tx = dma("sync", xb[par][:, :], x_d[128 * tc:128 * (tc + 1), :],
                     waits=[ln_norm_tok[par]], q="sx%d" % par)
            t1 = dve(lambda v, b=xb[par]: v.bn_stats(stat[:, 0:6], b[:, 0:512]),
                     waits=[tx])
            t2 = dve(lambda v, b=xb[par]: v.bn_stats(stat[:, 6:12], b[:, 512:1024]),
                     waits=[tx])
            t3 = dve(lambda v: v.bn_aggr(mv[:, :], stat[:, :]), waits=[t1, t2])
            t4 = act(lambda s: s.activation(sdt[:, :], mv[:, 1:2], AF.Sqrt, bias=epsb[:, 0:1]),
                     waits=[t3, tok_eps])
            t5 = dve(lambda v: v.reciprocal(rstd[:, :], sdt[:, :]),
                     waits=[t4, prev_t6p1])
            t5b = dve(lambda v: v.tensor_scalar(
                negmr[:, :], mv[:, 0:1], rstd[:, 0:1], -1.0,
                ALU.mult, ALU.mult), waits=[t5])
            t6 = act(lambda s, b=xb[par]: s.activation(
                lnc[:, :], b[:, :], AF.Identity, bias=negmr[:, 0:1],
                scale=rstd[:, 0:1]), waits=[t5b])
            prev_t6p1 = t6
            ln_norm_tok[par] = t6
            ttr = None
            for j in range(8):
                ttr = pe(lambda t, j=j: t.transpose(
                    psbf[:, 128 * j:128 * (j + 1)], lnc[:, 128 * j:128 * (j + 1)],
                    idsb[:, :]), waits=[t6, ev1_tok] if j == 0 else [])
            ot = ln1T.rearrange("p (j t) -> p j t", j=8)[:, :, 128 * tc:128 * (tc + 1)]
            it = psbf.rearrange("p (j t) -> p j t", j=8)
            ev1_tok = dve(lambda v, ot=ot, it=it: v.tensor_copy(ot, it), waits=[ttr])
            ln1T_toks.append(ev1_tok)
        tok_ln1T = ln1T_toks[-1]

        def vtok_dst(j0, jn, oc, b):
            a = vtok.rearrange("p (tt x) -> p tt x", tt=NT)[
                :, j0:j0 + jn, 130 * oc:130 * oc + 130]
            a = a.rearrange("p tt (h i) -> p tt h i", h=2)
            return a[:, :, :, 32 * b:32 * b + 32]

        def vtok_src(jn, b):
            a = psbf.rearrange("p (t hh j b) -> p t hh j b",
                                    t=8, hh=2, j=32, b=2)
            return a[:, 0:jn, :, :, b]

        proj_state = {"evac": [None, None], "vtr": None,
                      "vb_war": [None, None], "pend_tr": None,
                      "e3": [None, None], "esw_war": [None, None],
                      "e4": None, "e5": None, "half": 0}

        def emit_proj(w_d, b_sb, dest_kind, wbuf_war):
            wtok = dma("gpsimd", wbuf[:, :], w_d[:, :], waits=[wbuf_war], q="gd")
            evac_tok = proj_state["evac"]
            write_toks = []
            mm = None
            for oc in range(NIC):
                oc_e5 = []
                for h2 in range(NHALF):
                    m = oc * NHALF + h2
                    pv = ps32a[:, (m % 2) * 1024:(m % 2) * 1024 + TH]
                    tcb = h2 * TH
                    for n in range((TH + 511) // 512):
                        nw = min(512, TH - 512 * n)
                        for ic in range(NIC):
                            w0 = []
                            if oc == 0 and h2 == 0 and n == 0 and ic == 0:
                                w0 = [wtok, tok_ln1T]
                            if n == 0 and ic == 0:
                                w0 = w0 + [evac_tok[m % 2]]
                            mm = P.op("tensor", lambda t, pv=pv, n=n, nw=nw, ic=ic,
                                      oc=oc, tcb=tcb: t.matmul(
                                          pv[:, 512 * n:512 * n + nw],
                                          wbuf[:, 1024 * ic + 128 * oc:1024 * ic + 128 * (oc + 1)],
                                          ln1T[:, T * ic + tcb + 512 * n:T * ic + tcb + 512 * n + nw],
                                          start=(ic == 0), stop=(ic == NIC - 1)),
                                      waits=w0,
                                      inc="pe" if ic == NIC - 1 else None)
                    hpar = proj_state["half"] % 2
                    proj_state["half"] += 1
                    qb_t = qbh if hpar == 0 else lq
                    e1 = act(lambda s, pv=pv, b=b_sb, oc=oc, qb_t=qb_t:
                             s.activation(
                                 qb_t[:, 0:TH], pv, AF.Identity,
                                 bias=b[:, oc:oc + 1]),
                             waits=[mm, proj_state["e3"][hpar],
                                    proj_state["esw_war"][hpar]])
                    evac_tok[m % 2] = e1
                    esw = None
                    for si, (do, di) in enumerate(
                            ((0, 32), (32, 0), (64, 96), (96, 64))):
                        esw = P.op("gpsimd", lambda g, do=do, di=di, qb_t=qb_t:
                                   g.tensor_copy(
                                       qsw[do:do + 32, 0:TH],
                                       qb_t[di:di + 32, 0:TH]),
                                   waits=[e1] + ([proj_state["e4"]]
                                                 if si == 0 else []),
                                   inc="cc", amt=1)
                    proj_state["esw_war"][hpar] = esw
                    e3 = dve(lambda v, tcb=tcb, qb_t=qb_t: v.tensor_mul(
                        r1t[:, 0:TH], qb_t[:, 0:TH], Csb[:, tcb:tcb + TH]),
                        waits=[e1, proj_state["e5"]])
                    proj_state["e3"][hpar] = e3
                    e4 = dve(lambda v, tcb=tcb: v.tensor_mul(
                        r2t[:, 0:TH], qsw[:, 0:TH], Ssb[:, tcb:tcb + TH]),
                        waits=[esw])
                    proj_state["e4"] = e4
                    if dest_kind == "q":
                        dst = qT[:, T * oc + tcb:T * oc + tcb + TH]
                    elif dest_kind == "k":
                        dst = kT[:, T * oc + tcb:T * oc + tcb + TH]
                    else:
                        dst = (vb if oc % 2 == 0 else vb2)[:, tcb:tcb + TH]
                    e5 = dve(lambda v, dst=dst: v.tensor_add(
                        dst, r1t[:, 0:TH], r2t[:, 0:TH]),
                        waits=[e3, e4] + ([proj_state["vb_war"][oc % 2]]
                                          if dest_kind == "v" and h2 == 0 else []))
                    proj_state["e5"] = e5
                    oc_e5.append(e5)
                    write_toks.append(e5)
                    if dest_kind == "v" and h2 == 0 and proj_state["pend_tr"]:
                        proj_state["pend_tr"]()
                        proj_state["pend_tr"] = None
                if dest_kind == "v":
                    def do_tr(oc=oc, e5_tok=oc_e5[-1]):
                        vbp = vb if oc % 2 == 0 else vb2
                        ntile = T // 128
                        ttr_last = None
                        for batch in range((ntile + 7) // 8):
                            j0 = batch * 8
                            jn = min(8, ntile - j0)
                            ttr = None
                            for j in range(j0, j0 + jn):
                                ttr = pe(lambda t, j=j, vbp=vbp: t.transpose(
                                    psbf[:, 128 * (j % 8):128 * (j % 8) + 128],
                                    vbp[:, 128 * j:128 * j + 128], idsb[:, :]),
                                    waits=([e5_tok, proj_state["vtr"]]
                                           if j == j0 else []))
                            for b in range(2):
                                proj_state["vtr"] = dve(
                                    lambda v, b=b, j0=j0, jn=jn, oc=oc:
                                    v.tensor_copy(vtok_dst(j0, jn, oc, b),
                                                  vtok_src(jn, b)),
                                    waits=[ttr] if b == 0 else [])
                            ttr_last = ttr
                            write_toks.append(proj_state["vtr"])
                        proj_state["vb_war"][oc % 2] = ttr_last
                    proj_state["pend_tr"] = do_tr
            if dest_kind == "v" and proj_state["pend_tr"]:
                proj_state["pend_tr"]()
                proj_state["pend_tr"] = None
            return mm, write_toks, wtok

        vmm, v_toks, _ = emit_proj(wv_d, bvsb, "v", tok_consts)

        NW = (T + 511) // 512
        qk_state = {"evac": [None, None], "e3": [None, None], "e5": None,
                    "e4": None, "oc": 0}

        def emit_proj_fc(w_d, b_sb, dest, wbuf_war):
            wtok = dma("gpsimd", wbuf[:, :], w_d[:, :], waits=[wbuf_war], q="gd")
            last_mm = None
            toks = []
            for oc in range(NIC):
                par = qk_state["oc"] % 2
                qk_state["oc"] += 1
                pv = ps32a[:, 0:T] if par == 0 else ps32b[:, 0:T]
                mm = None
                for n in range(NW):
                    nw = min(512, T - 512 * n)
                    for ic in range(NIC):
                        w0 = []
                        if n == 0 and ic == 0:
                            w0 = [qk_state["evac"][par], wtok]
                            if qk_state["oc"] <= 2:
                                w0 += [v_toks[-1], proj_state["vtr"], tok_ln1T]
                        mm = P.op("tensor", lambda t, pv=pv, n=n, nw=nw, ic=ic,
                                  oc=oc: t.matmul(
                                      pv[:, 512 * n:512 * n + nw],
                                      wbuf[:, 1024 * ic + 128 * oc:1024 * ic + 128 * (oc + 1)],
                                      ln1T[:, T * ic + 512 * n:T * ic + 512 * n + nw],
                                      start=(ic == 0), stop=(ic == NIC - 1)),
                                  waits=w0,
                                  inc="pe" if ic == NIC - 1 else None)
                qb_t = qk_b[par]
                e1 = act(lambda s, pv=pv, b=b_sb, oc=oc, qb_t=qb_t: s.activation(
                    qb_t[:, :], pv, AF.Identity, bias=b[:, oc:oc + 1]),
                    waits=[mm, qk_state["e3"][par]])
                qk_state["evac"][par] = e1
                esw = None
                for si, (do, di) in enumerate(
                        ((0, 32), (32, 0), (64, 96), (96, 64))):
                    esw = dve(lambda v, do=do, di=di, qb_t=qb_t: v.tensor_copy(
                        qk_sw[do:do + 32, :], qb_t[di:di + 32, :]),
                        waits=[e1] + ([qk_state["e4"]] if si == 0 else []))
                e3 = dve(lambda v, qb_t=qb_t: v.tensor_mul(
                    qk_r1[:, :], qb_t[:, :], Csb[:, :]),
                    waits=[e1, qk_state["e5"]])
                qk_state["e3"][par] = e3
                e4 = dve(lambda v: v.tensor_mul(
                    qk_r2[:, :], qk_sw[:, :], Ssb[:, :]), waits=[esw])
                qk_state["e4"] = e4
                dst = (qT if dest == "q" else kT)[:, T * oc:T * (oc + 1)]
                e5 = dve(lambda v, dst=dst: v.tensor_add(
                    dst, qk_r1[:, :], qk_r2[:, :]), waits=[e3, e4])
                qk_state["e5"] = e5
                toks.append(e5)
            return last_mm if last_mm else mm, toks, wtok

        USE_FC = False
        if USE_FC:
            qmm, q_toks, _ = emit_proj_fc(wq_d, bqsb, "q", vmm)
            kmm, k_toks, _ = emit_proj_fc(wk_d, bksb, "k", qmm)
        else:
            qmm, q_toks, _ = emit_proj(wq_d, bqsb, "q", vmm)
            kmm, k_toks, _ = emit_proj(wk_d, bksb, "k", qmm)
        tok_vones = dve(lambda v: v.memset(
            vtok.rearrange("p (tt h i) -> p tt h i", tt=NT, h=NH)[:, :, :, 64:65],
            -1.0), waits=[])
        tok_wo = dma("gpsimd", wbuf[:, :], wo_d[:, :], waits=[kmm], q="gd")

        tok_q_all = q_toks[-1]
        tok_k_all = k_toks[-1]
        vb_war = proj_state["vb_war"][0]
        stat_war = proj_state["evac"][0]
        last_stat_red = None
        mrow_tok = None
        for (src, dstrow, wtok_all) in ((qT, mq2, tok_q_all), (kT, mk2, tok_k_all)):
            for oc in range(NIC):
                tsq = act(lambda s, src=src, oc=oc: s.activation(
                    vb[:, 0:T], src[:, T * oc:T * (oc + 1)], AF.Square),
                    waits=[wtok_all, v_toks[-1], vb_war])
                mmq = None
                for n in range((T + 511) // 512):
                    nw = min(512, T - 512 * n)
                    mmq = pe(lambda t, n=n, nw=nw: t.matmul(
                        ps32a[0:33, 512 * n:512 * n + nw],
                        blkdsb[:, :], vb[:, 512 * n:512 * n + nw],
                        start=True, stop=True),
                        waits=[tsq] + (([qk_state["evac"][0],
                                         qk_state["evac"][1],
                                         proj_state["evac"][0],
                                         proj_state["evac"][1]])
                                       if last_stat_red is None
                                       else [last_stat_red]))
                vb_war = mmq
                last_stat_red = dve(lambda v: v.tensor_reduce(
                    m2c[:, :], ps32a[0:33, 0:T], AX.X, ALU.max),
                    waits=[mmq, mrow_tok])
                mrow_tok = dve(lambda v, dstrow=dstrow, oc=oc: v.tensor_copy(
                    dstrow[0:1, 2 * oc:2 * oc + 1], m2c[0:1, 0:1]),
                    waits=[last_stat_red])
                mrow_tok = dve(lambda v, dstrow=dstrow, oc=oc: v.tensor_copy(
                    dstrow[0:1, 2 * oc + 1:2 * oc + 2], m2c[32:33, 0:1]),
                    waits=[last_stat_red])
        tprod = dve(lambda v: v.tensor_mul(mrow[:, :], mq2[:, :], mk2[:, :]),
                    waits=[mrow_tok])
        tsqr = act(lambda s: s.activation(mq2[:, :], mrow[:, :], AF.Sqrt),
                   waits=[tprod])
        tneg = dve(lambda v: v.tensor_scalar(
            mrow[:, :], mq2[:, :], -0.125, -1.0, ALU.mult, ALU.add), waits=[tsqr])
        tpair = dve(lambda v: v.tensor_tensor(
            mpair[:, :], mrow[0:1, 0:NH:2], mrow[0:1, 1:NH:2], ALU.min),
            waits=[tneg])
        tmh_mm = pe(lambda t: t.matmul(
            ps32a[0:128, 0:8], ones128[:, :], mpair[:, :], start=True, stop=True),
            waits=[tpair, last_stat_red])
        tok_mh = dve(lambda v: v.tensor_copy(mh_bc[:, :], ps32a[0:128, 0:8]),
                     waits=[tmh_mm])

        S2 = [ps32a[:, 0:S2W], ps32a[:, 1024:1024 + S2W]]
        av_off = [0, 512, 1024, 1536]
        o_last = None
        rope_hist = {-1: [tok_mh], -2: [tok_mh]}
        avb_prev = {}
        s2_free = {0: tok_mh, 1: tok_mh}
        last_exp = tok_mh
        prev_rfm = None
        prev_ow = None
        pending = None
        gi = 0

        def emit_epilogue_b(p):
            nonlocal prev_rfm, prev_ow, o_last
            offA, offB = p["offA"], p["offB"]
            pe_noinc(lambda t, offA=offA: t.matmul(
                ps32b[0:64, offA:offA + QB], blk64sb[:, :], r2a[0:1, 0:QB],
                start=True, stop=True),
                waits=[p["rc1"], p["rc2"], p["evA"], p["evB"]])
            rfm = pe(lambda t, offA=offA: t.matmul(
                ps32b[64:128, offA:offA + QB], blk64sb[:, :], r2bb[0:1, 0:QB],
                start=True, stop=True, tile_position=(0, 64)), waits=[])
            prev_rfm = rfm
            erf = dve(lambda v, offA=offA: v.tensor_copy(
                rfb[:, 0:QB], ps32b[0:128, offA:offA + QB]),
                waits=[rfm, prev_ow])
            esw = None
            for (do, di) in ((0, 32), (32, 0), (64, 96), (96, 64)):
                esw = dve(lambda v, do=do, di=di: v.tensor_copy(
                    qsw[do:do + 32, 0:QB], qbh[di:di + 32, 0:QB]),
                    waits=[p["evA"], p["evB"]])
            m1 = dve(lambda v, qc0=p["qc0"]: v.tensor_mul(
                r1t[:, 0:QB], qbh[:, 0:QB], Csb[:, qc0:qc0 + QB]),
                waits=[p["evA"], p["evB"]])
            m2 = dve(lambda v, qc0=p["qc0"]: v.tensor_mul(
                r2t[:, 0:QB], qsw[:, 0:QB], Ssb[:, qc0:qc0 + QB]),
                waits=[esw])
            m3 = dve(lambda v: v.tensor_add(qsw[:, 0:QB], r1t[:, 0:QB],
                                            r2t[:, 0:QB]), waits=[m1, m2])
            ow = dve(lambda v, hp=p["hp"], qc0=p["qc0"]: v.tensor_mul(
                oT[:, T * hp + qc0:T * hp + qc0 + QB], qsw[:, 0:QB],
                rfb[:, 0:QB]), waits=[m3, erf])
            rope_hist[p["gi"]] = [p["evA"], p["evB"], p["rc1"], p["rc2"],
                                  ow, erf]
            prev_ow = ow
            o_last = ow

        for qb in range(NQ):
            qc0 = QB * qb
            for hp in range(NH // 2):
                par = gi % 2
                offA = av_off[2 * par]
                offB = av_off[2 * par + 1]
                avA = ps32b[0:65, offA:offA + QB]
                avB = ps32b[0:65, offB:offB + QB]
                new_exp = {}
                new_avb = {}
                av_args = None
                for kc in range(KC):
                    s2 = S2[kc % 2]
                    w0 = [s2_free[kc % 2]]
                    if gi == 0 and kc == 0:
                        w0 += [tok_mh, tok_q_all, tok_k_all]
                    pe_noinc(lambda t, s2=s2, hp=hp, kc=kc, qc0=qc0: t.matmul(
                        s2[:, 0:QB],
                        kT[0:64, T * hp + 128 * kc:T * hp + 128 * (kc + 1)],
                        qT[0:64, T * hp + qc0:T * hp + qc0 + QB],
                        start=True, stop=True, tile_position=(0, 0)),
                        waits=w0)
                    smm = pe(lambda t, s2=s2, hp=hp, kc=kc, qc0=qc0: t.matmul(
                        s2[:, QB:2 * QB],
                        kT[64:128, T * hp + 128 * kc:T * hp + 128 * (kc + 1)],
                        qT[64:128, T * hp + qc0:T * hp + qc0 + QB],
                        start=True, stop=True, tile_position=(64, 0)),
                        waits=[])
                    pt_out = PT.rearrange("p (h x) -> p h x", h=2)[
                        :, :, QB * kc:QB * (kc + 1)]
                    s2r = s2.rearrange("p (h x) -> p h x", h=2)
                    et = act(lambda s, pt_out=pt_out, s2r=s2r, hp=hp: s.activation(
                        pt_out, s2r, AF.Exp, bias=mh_bc[:, hp:hp + 1], scale=0.125),
                        waits=[smm, avb_prev.get(kc)])
                    new_exp[kc] = et
                    s2_free[kc % 2] = et

                    if kc == min(6, KC - 1) and pending is not None:
                        emit_epilogue_b(pending)
                        pending = None

                    def emit_av(kav):
                        avw = [new_exp[kav]]
                        if kav == 0:
                            avw = avw + rope_hist[gi - 2]
                        pe_noinc(lambda t, kav=kav, hp=hp, avA=avA: t.matmul(
                            avA, vtok[:, (65 * NH) * kav + 65 * (2 * hp):(65 * NH) * kav + 65 * (2 * hp) + 65],
                            PT[:, QB * kav:QB * (kav + 1)],
                            start=(kav == 0), stop=(kav == KC - 1)), waits=avw)
                        return P.op("tensor", lambda t, kav=kav, hp=hp, avB=avB:
                                    t.matmul(
                                        avB, vtok[:, (65 * NH) * kav + 65 * (2 * hp + 1):(65 * NH) * kav + 65 * (2 * hp + 1) + 65],
                                        PT[:, KC * QB + QB * kav:KC * QB + QB * (kav + 1)],
                                        start=(kav == 0), stop=(kav == KC - 1)),
                                    waits=[], inc="pe")
                    if kc >= 1:
                        new_avb[kc - 1] = emit_av(kc - 1)
                new_avb[KC - 1] = emit_av(KC - 1)
                av_tokB = new_avb[KC - 1]
                avb_prev = new_avb
                last_exp = new_exp[KC - 1]
                evA = dve(lambda v, avA=avA: v.tensor_copy(
                    qbh[0:64, 0:QB], avA[0:64, 0:QB]),
                    waits=[av_tokB, prev_rfm, prev_ow])
                evB = dve(lambda v, avB=avB: v.tensor_copy(
                    qbh[64:128, 0:QB], avB[0:64, 0:QB]), waits=[av_tokB])
                rc1 = dve(lambda v, avA=avA: v.reciprocal(
                    r2a[0:1, 0:QB], avA[64:65, 0:QB]), waits=[av_tokB])
                rc2 = dve(lambda v, avB=avB: v.reciprocal(
                    r2bb[0:1, 0:QB], avB[64:65, 0:QB]), waits=[av_tokB])
                pending = {"gi": gi, "hp": hp, "qc0": qc0, "offA": offA,
                           "offB": offB, "rc1": rc1, "rc2": rc2,
                           "evA": evA, "evB": evB}
                gi += 1
        emit_epilogue_b(pending)
        pending = None
        prev_rf_evac = None

        tok_pad = dve(lambda v: v.memset(pad32[:, :], 0.0), waits=[])
        tok_zerob = dve(lambda v: v.memset(zerob[:, :], 0.0), waits=[])

        def maxtree(src, toks):
            tok = dve(lambda v, src=src: v.tensor_copy(pad32[:, 0:1], src[:, 0:1]),
                      waits=list(toks) + [tok_pad])
            tok = dve(lambda v: v.transpose(tr32[:, :], pad32[:, :]), waits=[tok])
            for j, row in enumerate((0, 32, 64, 96)):
                tok = dve(lambda v, j=j, row=row: v.reduce_max(
                    rowv[0:1, j:j + 1], tr32[row:row + 1, 0:32], AX.X),
                    waits=[tok])
            tok = dve(lambda v: v.reduce_max(trt[0:1, 0:1], rowv[0:1, 0:4], AX.X),
                      waits=[tok])
            return tok

        RD = 1.0 / D

        def stats_math(C, pA, pB, mean_out=None):
            mo = mean_out if mean_out is not None else C[:, 2:3]
            m1 = dve(lambda v, C=C, mo=mo: v.tensor_scalar_mul(
                mo, C[:, 0:1], RD), waits=[pA])
            m2 = dve(lambda v, C=C, mo=mo: v.tensor_mul(
                C[:, 4:5], mo, mo), waits=[m1])
            m3 = dve(lambda v, C=C: v.tensor_scalar_mul(
                C[:, 3:4], C[:, 1:2], RD), waits=[pB])
            m4 = dve(lambda v, C=C: v.tensor_sub(
                C[:, 3:4], C[:, 3:4], C[:, 4:5]), waits=[m2, m3])
            return m1, m4

        def velt(dst, srcf32, waits, cqx=None, vq3x=None):
            cqa = cqx if cqx is not None else cq.ap()[:, :]
            vqa = vq3x if vq3x is not None else vq3
            t1 = dve(lambda v, s=srcf32, cqa=cqa: v.tensor_single_scalar(
                cqa.bitcast(I32), s.bitcast(I32), 0x7F800000,
                ALU.bitwise_and), waits=waits)
            t2 = dve(lambda v, cqa=cqa, vqa=vqa: v.tensor_scalar(
                vqa, cqa, VELT_K, VELT_MIN, ALU.mult, ALU.max),
                waits=[t1])
            t3 = dve(lambda v, s=srcf32, cqa=cqa, vqa=vqa: v.tensor_tensor(
                cqa, s, vqa, ALU.add), waits=[t2])
            t4 = dve(lambda v, dst=dst, cqa=cqa, vqa=vqa: v.tensor_tensor(
                dst, cqa, vqa, ALU.subtract), waits=[t3])
            return t4

        wmx_tok = None
        wroll_war = [None, None]
        velt_war = None
        wq_toks = []
        tokw = None
        tok_ws = None
        tok_wsi = None
        twsb = None
        bc_mm2 = None

        def emit_wstep(i):
            nonlocal wmx_tok, velt_war, tokw, tok_ws, tok_wsi, twsb, bc_mm2
            if i < NIC:
                wc = i
                wpar = wc % 2
                tw = dma("sync", wroll[wpar][:, :],
                         wf_d[:, 1024 * wc:1024 * (wc + 1)],
                         waits=[wroll_war[wpar]] + ([k_toks[-1], o_last,
                                                    v_toks[-1]]
                                                   if wc <= 1 else []),
                         q="sw%d" % wpar)
                rw = dve(lambda v, wpar=wpar: v.reduce_max(
                    u1[:, :], wroll[wpar][:, :], AX.X,
                    apply_absolute_value=True),
                    waits=[tw, wmx_tok])
                wroll_war[wpar] = rw
                if wc == 0:
                    wmx_tok = dve(lambda v: v.tensor_copy(wmx[0][:, :],
                                                          u1[:, :]),
                                  waits=[rw])
                else:
                    wmx_tok = dve(lambda v, wc=wc: v.tensor_max(
                        wmx[wc % 2][:, :], wmx[1 - wc % 2][:, :], u1[:, :]),
                        waits=[rw, wmx_tok])
            else:
                wc = i - NIC
                wpar = wc % 2
                if wc == 0:
                    wmx_fin = wmx[(NIC - 1) % 2]
                    tokw = maxtree(wmx_fin, [wmx_tok])
                    tokw = dve(lambda v: v.tensor_scalar_max(
                        wscl[0:1, 1:2], trt[0:1, 0:1], 1e-12), waits=[tokw])
                    tokw = dve(lambda v: v.reciprocal(wscl[0:1, 5:6],
                                                      wscl[0:1, 1:2]),
                               waits=[tokw])
                    tokw = dve(lambda v: v.tensor_scalar_mul(
                        wscl[0:1, 2:3], wscl[0:1, 5:6], FP8MAX), waits=[tokw])
                    tok_ws = dve(lambda v: v.tensor_scalar_min(
                        wscl[0:1, 3:4], wscl[0:1, 2:3], FP8MAX), waits=[tokw])
                    tok_wsi = dve(lambda v: v.reciprocal(wscl[0:1, 4:5],
                                                         wscl[0:1, 3:4]),
                                  waits=[tok_ws])
                    bc_mm2 = pe(lambda t: t.matmul(
                        ps32b[0:128, 1540:1541], ones128[:, :], wscl[0:1, 3:4],
                        start=True, stop=True), waits=[tok_ws, o_last])
                    twsb = dve(lambda v: v.tensor_copy(ws_bc[:, :],
                                                       ps32b[0:128, 1540:1541]),
                               waits=[bc_mm2])
                tw = dma("sync", wroll[wpar][:, :],
                         wf_d[:, 1024 * wc:1024 * (wc + 1)],
                         waits=[wroll_war[wpar]], q="sw%d" % wpar)
                tz = dve(lambda v, wpar=wpar: v.tensor_scalar_mul(
                    zqW, wroll[wpar][:, :], ws_bc[:, 0:1]),
                    waits=[tw, twsb, velt_war])
                wroll_war[wpar] = tz
                tq = velt(wffq[:, 1024 * wc:1024 * (wc + 1)], zqW, [tz],
                          cqx=cqW, vq3x=vqW)
                velt_war = tq
                wq_toks.append(tq)

        y_toks = []
        prev_q3 = None
        oproj_evac = [None, None, None]
        amx_tok = None
        sA_war = [None, None]
        gu_war = [None, None]
        out_war = [k_toks[-1], o_last]
        x2_tok = [ln_norm_tok[0], ln_norm_tok[1]]
        p4s = {"s5": None, "mm": None}

        pvs3 = [ps32a[:, 0:1024], ps32a[:, 1024:2048], ps32b[:, 0:1024]]

        def emit_p4_head(tc):
            par = tc % 2
            p3i = tc % 3
            tx = dma("sync", xb[par][:, :], x_d[128 * tc:128 * (tc + 1), :],
                     waits=[x2_tok[par]], q="sx%d" % par)
            pv = pvs3[p3i]
            mm = None
            for n in range(2):
                for dc in range(NIC):
                    w0 = []
                    if tc == 0 and n == 0 and dc == 0:
                        w0 = [last_exp, tok_wo, o_last, prev_rf_evac]
                    if tc == 2 and n == 0 and dc == 0:
                        w0 = [o_last]
                    if n == 0 and dc == 0:
                        w0 = w0 + [oproj_evac[p3i]]
                    mm = P.op("tensor", lambda t, pv=pv, n=n, dc=dc, tc=tc: t.matmul(
                        pv[:, 512 * n:512 * n + 512],
                        oT[:, T * dc + 128 * tc:T * dc + 128 * (tc + 1)],
                        wbuf[:, 1024 * dc + 512 * n:1024 * dc + 512 * n + 512],
                        start=(dc == 0), stop=False),
                        waits=w0, inc=None)
                mm = P.op("tensor", lambda t, pv=pv, n=n: t.matmul(
                    pv[:, 512 * n:512 * n + 512], ones128b[:, :],
                    bosr[0:1, 512 * n:512 * n + 512],
                    start=False, stop=True),
                    waits=[tok_bosr] if tc == 0 and n == 0 else [],
                    inc="pe")
            p4s["mm"] = mm
            C = lnstT[p3i]
            pA = act(lambda s, pv=pv, C=C, p3i=p3i: s.activation(
                scT[p3i], pv, AF.Identity,
                bias=zerob[:, 0:1], accum_out=C[:, 0:1]),
                waits=[mm, tok_zerob] + ([o_last] if tc <= 1 else []))
            pB = act(lambda s, pv=pv, C=C, p3i=p3i: s.activation(
                scT[p3i], pv, AF.Square,
                accum_out=C[:, 1:2]), waits=[pA])
            _, m4 = stats_math(C, pA, pB)
            t4 = act(lambda s, C=C: s.activation(C[:, 5:6], C[:, 3:4], AF.Sqrt,
                                                 bias=epsb[:, 0:1]),
                     waits=[m4])
            t5 = dve(lambda v, C=C: v.reciprocal(C[:, 6:7], C[:, 5:6]),
                     waits=[t4])
            t5b = dve(lambda v, C=C: v.tensor_scalar(
                C[:, 7:8], C[:, 2:3], C[:, 6:7], -1.0,
                ALU.mult, ALU.mult), waits=[t5])
            t6 = act(lambda s, pv=pv, C=C: s.activation(
                lnb[:, :], pv, AF.Identity, bias=C[:, 7:8],
                scale=C[:, 6:7]), waits=[t5b])
            oproj_evac[p3i] = t6
            ty = dve(lambda v, tc=tc, par=par: v.tensor_add(
                yv[:, D * tc:D * (tc + 1)], xb[par][:, :], lnb[:, :]),
                waits=[t6, tx] + (out_war if tc <= 1 else []))
            x2_tok[par] = ty
            y_toks.append(ty)

        def emit_p4_tail(tc):
            nonlocal amx_tok, prev_q3
            par = tc % 2
            ty = y_toks[tc]
            Cs = lnstS[par]
            pAs = act(lambda s, tc=tc, Cs=Cs, par=par: s.activation(
                scS[par], yv[:, D * tc:D * (tc + 1)], AF.Identity,
                bias=zerob[:, 0:1], accum_out=Cs[:, 0:1]),
                waits=[ty, sA_war[par]])
            pBs = act(lambda s, tc=tc, Cs=Cs, par=par: s.activation(
                scS[par], yv[:, D * tc:D * (tc + 1)], AF.Square,
                accum_out=Cs[:, 1:2]), waits=[pAs])
            sA_war[par] = pBs
            sm, m4s = stats_math(Cs, pAs, pBs,
                                 mean_out=lncm[:, tc:tc + 1])
            s4 = act(lambda s, Cs=Cs: s.activation(
                Cs[:, 5:6], Cs[:, 3:4], AF.Sqrt, bias=epsb[:, 0:1]),
                waits=[m4s])
            s5 = dve(lambda v, tc=tc, Cs=Cs: v.reciprocal(
                lncr[:, tc:tc + 1], Cs[:, 5:6]), waits=[s4])
            p4s["s5"] = s5
            guP, gu2P, gtP = ([gu1, gu2, gtrt] if par == 0
                              else [gu1b, gu2b, gtrtb])
            r1 = dve(lambda v, tc=tc, guP=guP: v.reduce_max(
                guP[:, :], yv[:, D * tc:D * (tc + 1)], AX.X),
                waits=[ty, gu_war[par]])
            r2_ = dve(lambda v, tc=tc, gu2P=gu2P: v.reduce_max(
                gu2P[:, :], yv[:, D * tc:D * (tc + 1)], AX.X, negate=True),
                waits=[ty])
            q1 = dve(lambda v, tc=tc, guP=guP: v.tensor_scalar(
                guP[:, :], guP[:, :], lncm[:, tc:tc + 1], lncr[:, tc:tc + 1],
                ALU.subtract, ALU.mult), waits=[r1, sm, s5])
            q2 = dve(lambda v, tc=tc, gu2P=gu2P: v.tensor_scalar(
                gu2P[:, :], gu2P[:, :], lncm[:, tc:tc + 1], lncr[:, tc:tc + 1],
                ALU.add, ALU.mult), waits=[r2_, sm, s5])
            q3 = dve(lambda v, guP=guP, gu2P=gu2P, gtP=gtP: v.tensor_max(
                gtP[:, :], guP[:, :], gu2P[:, :]), waits=[q1, q2])
            gu_war[par] = q3
            prev_q3 = q3
            if tc == 0:
                amx_tok = dve(lambda v, gtP=gtP: v.tensor_copy(
                    amx[0][:, :], gtP[:, :]), waits=[q3])
            else:
                amx_tok = dve(lambda v, tc=tc, gtP=gtP: v.tensor_max(
                    amx[tc % 2][:, :], amx[1 - tc % 2][:, :], gtP[:, :]),
                    waits=[q3, amx_tok])
            gu_war[par] = amx_tok

        for tc in range(NT):
            emit_p4_head(tc)
            if tc >= 1:
                emit_p4_tail(tc - 1)
            if tc < 2 * NIC:
                emit_wstep(tc)
        emit_p4_tail(NT - 1)
        for i in range(min(NT, 2 * NIC), 2 * NIC):
            emit_wstep(i)
        amx_fin = amx[(NT - 1) % 2]
        tok_fb = dma("sync", fbsb, fb_d[:, :], waits=[p4s["mm"]])

        tok = maxtree(amx_fin, [wmx_tok, amx_tok, tokw])
        tok = dve(lambda v: v.tensor_copy(scl[0:1, 0:1], trt[0:1, 0:1]),
                  waits=[tok])
        tgd = dma("gpsimd", cc_in[0:1, 0:1], scl[0:1, 0:1], waits=[tok], q="gc")
        tcc = P.op("gpsimd", lambda g: g.collective_compute(
            "AllReduce", ALU.max, replica_groups=[list(range(NCORES))],
            ins=[cc_in.ap().opt()], outs=[cc_out.ap().opt()]),
            waits=[tgd], inc="cc", amt=1)
        tgd2 = dma("gpsimd", ccs[0:1, 0:1], cc_out[0:1, 0:1], waits=[tcc], q="gc")
        tok = dve(lambda v: v.tensor_scalar_max(scl[0:1, 1:2], ccs[0:1, 0:1],
                                                1e-12), waits=[tgd2])
        tok = dve(lambda v: v.reciprocal(scl[0:1, 5:6], scl[0:1, 1:2]),
                  waits=[tok])
        tok = dve(lambda v: v.tensor_scalar_mul(scl[0:1, 2:3], scl[0:1, 5:6],
                                                FP8MAX), waits=[tok])
        tok_xs = dve(lambda v: v.tensor_scalar_min(scl[0:1, 3:4], scl[0:1, 2:3],
                                                   FP8MAX), waits=[tok])
        tok_xsi = dve(lambda v: v.reciprocal(scl[0:1, 4:5], scl[0:1, 3:4]),
                      waits=[tok_xs])
        tok_dq = dve(lambda v: v.tensor_mul(dqt[:, :], scl[0:1, 4:5],
                                            wscl[0:1, 4:5]),
                     waits=[tok_xsi, tok_wsi])
        bc_mm1 = pe(lambda t: t.matmul(ps32b[0:128, 1541:1542], ones128[:, :],
                                       scl[0:1, 3:4], start=True, stop=True),
                    waits=[tok_xs, bc_mm2])
        bc_mm3 = pe(lambda t: t.matmul(ps32b[0:128, 1542:1543], ones128[:, :],
                                       dqt[:, :], start=True, stop=True),
                    waits=[tok_dq, bc_mm1])
        txsb = dve(lambda v: v.tensor_copy(xs_bc[:, :], ps32b[0:128, 1541:1542]),
                   waits=[bc_mm1])
        tdqb = dve(lambda v: v.tensor_copy(dq_bc[:, :], ps32b[0:128, 1542:1543]),
                   waits=[bc_mm3])
        trs2 = dve(lambda v: v.tensor_scalar_mul(rs2[:, 0:NT], lncr[:, 0:NT],
                                                 xs_bc[:, 0:1]),
                   waits=[txsb, p4s["s5"]])
        tfbr = dve(lambda v: v.tensor_scalar(
            bosr[0:1, :], fbsb[0:1, :], scl[0:1, 3:4], wscl[0:1, 3:4],
            ALU.mult, ALU.mult), waits=[tok_fb, tok_xs, tok_ws])

        qsets = [(zq.ap()[:, :], cq.ap()[:, :], vq3, lq.ap()[:, :],
                  ffnb.ap()[:, :], lnb, negm2),
                 (zq2, cq2, vq32, lq2, ffnb2, lnb2, negm3)]
        p6 = {"ev6": [None, None],
              "tvq_war": [y_toks[NT - 2], y_toks[NT - 1]],
              "d6_war": [None, None], "ffn_evac": [None, None],
              "out_dma": [None, None],
              "ln2q_war": [o_last, wq_toks[-1], oproj_evac[0],
                           oproj_evac[1], oproj_evac[2],
                           sA_war[0], sA_war[1]],
              "mm": [None, None], "d7_war": [None, None],
              "pB_war": [None, None]}
        final_toks = []

        def emit_quant(tc):
            par = tc % 2
            zqp, cqp, vq3p, lqp, ffnp, lnbp, ngp = qsets[par]
            pb = dve(lambda v, tc=tc, ngp=ngp: v.tensor_scalar(
                ngp[:, 0:1], lncm[:, tc:tc + 1],
                rs2[:, tc:tc + 1], -1.0, ALU.mult, ALU.mult),
                waits=[trs2, p6["tvq_war"][par]])
            tvq = act(lambda s, tc=tc, zqp=zqp, ngp=ngp: s.activation(
                zqp, yv[:, D * tc:D * (tc + 1)], AF.Identity,
                bias=ngp[:, 0:1], scale=rs2[:, tc:tc + 1]),
                waits=[pb, y_toks[tc]])
            tq = velt(lqp, zqp, [tvq], cqx=cqp, vq3x=vq3p)
            p6["tvq_war"][par] = tq
            ttr = None
            for j in range(8):
                ttr = pe(lambda t, j=j, lqp=lqp: t.transpose(
                    psbf[:, 128 * j:128 * (j + 1)], lqp[:, 128 * j:128 * (j + 1)],
                    idsb[:, :]),
                    waits=[tq, p6["ev6"][1 - par]] + p6["ln2q_war"]
                    if j == 0 else [])
                p6["ln2q_war"] = []
            ot = ln2qT.rearrange("p (j t) -> p j t", j=8)[
                :, :, 128 * tc:128 * (tc + 1)]
            it = psbf.rearrange("p (j t) -> p j t", j=8)
            p6["ev6"][par] = dve(lambda v, ot=ot, it=it: v.tensor_copy(ot, it),
                                 waits=[ttr])
            return p6["ev6"][par]

        def emit_ffn_mm(tc):
            par = tc % 2
            pv = ps32a[:, 1024 * par:1024 * par + 1024]
            mm = None
            for n in range(2):
                for ic in range(NIC):
                    w0 = []
                    if tc == 0 and n == 0 and ic == 0:
                        w0 = [tdqb, txsb, wq_toks[-1]]
                    if n == 0 and ic == 0:
                        w0 = w0 + [p6["ffn_evac"][par], ev_list[tc]]
                    mm = P.op("tensor", lambda t, pv=pv, n=n, ic=ic, tc=tc:
                              t.matmul(
                                  pv[:, 512 * n:512 * n + 512],
                                  ln2qT[:, T * ic + 128 * tc:T * ic + 128 * (tc + 1)],
                                  wffq[:, 1024 * ic + 512 * n:1024 * ic + 512 * n + 512],
                                  start=(ic == 0), stop=False),
                              waits=w0, inc=None)
                mm = P.op("tensor", lambda t, pv=pv, n=n: t.matmul(
                    pv[:, 512 * n:512 * n + 512], ones128b[:, :],
                    bosr[0:1, 512 * n:512 * n + 512],
                    start=False, stop=True),
                    waits=[tfbr] if tc == 0 and n == 0 else [],
                    inc="pe")
            p6["mm"][par] = mm

        def emit_ffn_rest(tc):
            par = tc % 2
            zqp, cqp, vq3p, lqp, ffnp, lnbp, ngp = qsets[par]
            pv = ps32a[:, 1024 * par:1024 * par + 1024]
            Cd = lnstD[par]
            f1 = act(lambda s, pv=pv, ffnp=ffnp, Cd=Cd: s.activation(
                ffnp, pv, AF.Identity, bias=zerob[:, 0:1],
                scale=dq_bc[:, 0:1], accum_out=Cd[:, 0:1]),
                waits=[p6["mm"][par], tdqb, tok_zerob, p6["d6_war"][par]])
            p6["ffn_evac"][par] = f1
            pBd = act(lambda s, ffnp=ffnp, Cd=Cd, par=par: s.activation(
                wroll[par][:, :], ffnp, AF.Square,
                accum_out=Cd[:, 1:2]), waits=[f1, p6["pB_war"][par]])
            p6["pB_war"][par] = pBd
            _, m4d = stats_math(Cd, f1, pBd)
            d4 = act(lambda s, Cd=Cd: s.activation(
                Cd[:, 5:6], Cd[:, 3:4], AF.Sqrt, bias=epsb[:, 0:1]),
                waits=[m4d])
            d5 = dve(lambda v, Cd=Cd: v.reciprocal(Cd[:, 6:7], Cd[:, 5:6]),
                     waits=[d4])
            d5b = dve(lambda v, Cd=Cd: v.tensor_scalar(
                Cd[:, 7:8], Cd[:, 2:3], Cd[:, 6:7], -1.0,
                ALU.mult, ALU.mult), waits=[d5])
            d6 = act(lambda s, ffnp=ffnp, lnbp=lnbp, Cd=Cd: s.activation(
                lnbp, ffnp, AF.Identity, bias=Cd[:, 7:8],
                scale=Cd[:, 6:7]), waits=[d5b, p6["d7_war"][par]])
            p6["d6_war"][par] = d6

        def emit_ffn_out(tc):
            par = tc % 2
            zqp, cqp, vq3p, lqp, ffnp, lnbp, ngp = qsets[par]
            d7 = dve(lambda v, tc=tc, par=par, lnbp=lnbp: v.tensor_add(
                outb[par][:, :], yv[:, D * tc:D * (tc + 1)], lnbp),
                waits=[p6["d6_war"][par], p6["out_dma"][par]])
            p6["d7_war"][par] = d7
            od = dma("sync", out_d[128 * tc:128 * (tc + 1), :], outb[par][:, :],
                     waits=[d7], q="so%d" % par)
            p6["out_dma"][par] = od
            final_toks.append(od)

        ev_list = {}
        ev_list[0] = emit_quant(0)
        for tc in range(1, NT):
            emit_ffn_mm(tc - 1)
            ev_list[tc] = emit_quant(tc)
            emit_ffn_rest(tc - 1)
            if tc >= 2:
                emit_ffn_out(tc - 2)
        emit_ffn_mm(NT - 1)
        emit_ffn_rest(NT - 1)
        emit_ffn_out(NT - 2)
        emit_ffn_out(NT - 1)

        P.op("sync", lambda s: s.nop(),
             waits=[final_toks[-1], final_toks[-2] if NT > 1 else None])

        with nc.allow_low_precision(reason="bf16 operands by design (blessed)"), \
                nc.Block() as block:
            def make_runner(ename):
                def runner(en):
                    for fn, w, inc, amt in P.stream[ename]:
                        for s, v in w:
                            en.wait_ge(sems[s], v)
                        ins = fn(en)
                        if inc is not None and ins is not None:
                            ins.then_inc(sems[inc], amt)
                return runner
            block.sync(make_runner("sync"))
            block.tensor(make_runner("tensor"))
            block.vector(make_runner("vector"))
            block.scalar(make_runner("scalar"))
            block.gpsimd(make_runner("gpsimd"))
    return nc


def _host_prep(L0_attn_x, L0_attn_Wq, L0_attn_bq, L0_attn_Wk, L0_attn_bk,
               L0_attn_Wv, L0_attn_bv, L0_attn_Wo, L0_attn_bo,
               L0_ffn_weight, L0_ffn_bias, T):
    perm = np.concatenate([np.concatenate([np.arange(64 * h, 64 * (h + 1), 2),
                                           np.arange(64 * h + 1, 64 * (h + 1), 2)])
                           for h in range(NH)])

    def lay_lhsT(Wp):
        A = np.ascontiguousarray(Wp.T)
        return np.ascontiguousarray(
            A.reshape(8, 128, D).transpose(1, 0, 2).reshape(128, 8 * D))

    def lay_rhs(W):
        A = np.ascontiguousarray(W.T)
        return np.ascontiguousarray(
            A.reshape(8, 128, D).transpose(1, 0, 2).reshape(128, 8 * D))

    wq = lay_lhsT(np.asarray(L0_attn_Wq)[perm, :])
    wk = lay_lhsT(np.asarray(L0_attn_Wk)[perm, :])
    wv = lay_lhsT(np.asarray(L0_attn_Wv)[perm, :])
    wo = lay_rhs(np.asarray(L0_attn_Wo))
    wf = lay_rhs(np.asarray(L0_ffn_weight))
    bq = np.ascontiguousarray(np.asarray(L0_attn_bq)[perm].reshape(8, 128).T)
    bk = np.ascontiguousarray(np.asarray(L0_attn_bk)[perm].reshape(8, 128).T)
    bv = np.ascontiguousarray(np.asarray(L0_attn_bv)[perm].reshape(8, 128).T)
    bo = np.ascontiguousarray(np.broadcast_to(np.asarray(L0_attn_bo), (128, D)))
    fb = np.ascontiguousarray(np.broadcast_to(np.asarray(L0_ffn_bias), (128, D)))
    x = np.asarray(L0_attn_x)
    in_maps = []
    for b in range(NCORES):
        in_maps.append({
            "x": np.ascontiguousarray(x[b, :T]), "wq": wq, "wk": wk, "wv": wv,
            "wo": wo, "wf": wf, "bq": bq, "bk": bk, "bv": bv, "bo": bo, "fb": fb,
        })
    return in_maps


def kernel(L0_attn_x, L0_attn_Wq, L0_attn_bq, L0_attn_Wk, L0_attn_bk,
           L0_attn_Wv, L0_attn_bv, L0_attn_Wo, L0_attn_bo,
           L0_ffn_weight, L0_ffn_bias):
    T = np.asarray(L0_attn_x).shape[1]
    in_maps = _host_prep(L0_attn_x, L0_attn_Wq, L0_attn_bq, L0_attn_Wk,
                         L0_attn_bk, L0_attn_Wv, L0_attn_bv, L0_attn_Wo,
                         L0_attn_bo, L0_ffn_weight, L0_ffn_bias, T)
    nc = build(T)
    res = run_bass_kernel_spmd(nc, in_maps, core_ids=list(range(NCORES)))
    out = np.stack([r["out"] for r in res.results], axis=0)
    return out.astype(np.float32)


# version 23
# speedup vs baseline: 1.3066x; 1.3066x over previous
import numpy as np
import ml_dtypes
import concourse.bass as bass
import concourse.mybir as mybir
from concourse.bass_utils import run_bass_kernel_spmd
from contextlib import ExitStack

F32 = mybir.dt.float32
BF16 = mybir.dt.bfloat16
I32 = mybir.dt.int32
U8 = mybir.dt.uint8
AF = mybir.ActivationFunctionType
ALU = mybir.AluOpType
AX = mybir.AxisListType

D = 1024
NH = 16
HD = 64
NIC = 8
NCORES = 8
EPS = 1e-5
FP8MAX = 448.0
VELT_K = float(1.5 * 2.0 ** 20)
VELT_MIN = 24576.0

ENGS = ("sync", "tensor", "vector", "scalar", "gpsimd")


class Plan:
    def __init__(self):
        self.stream = {e: [] for e in ENGS}
        self.cnt = {}
        self.lastwait = {e: {} for e in ENGS}

    def sem(self, name):
        self.cnt[name] = 0

    def op(self, eng, fn, waits=(), inc=None, amt=1):
        w = []
        for t in waits:
            if t is None:
                continue
            s, v = t
            if self.lastwait[eng].get(s, 0) >= v:
                continue
            self.lastwait[eng][s] = v
            w.append((s, v))
        tok = None
        if inc is not None:
            self.cnt[inc] += amt
            tok = (inc, self.cnt[inc])
        self.stream[eng].append((fn, w, inc, amt))
        return tok


def _rope_tables(T):
    pos = np.arange(T, dtype=np.float64)[:, None]
    dim_pair = np.arange(0, HD, 2, dtype=np.float64)[None, :]
    theta = 10000.0 ** (-dim_pair / HD)
    freqs = pos * theta
    rope = np.concatenate([np.cos(freqs), np.sin(freqs)], axis=-1)
    ctab = rope[:, 0::2]
    stab = rope[:, 1::2]
    p = np.arange(128)
    C = ctab[:, p % 32].T
    sgn = np.where((p % 64) < 32, -1.0, 1.0)
    S = (stab[:, p % 32] * sgn[None, :]).T
    return C.astype(ml_dtypes.bfloat16), S.astype(ml_dtypes.bfloat16)


def build(T):
    NT = T // 128
    KC = T // 128
    QB = min(512, T)
    NQ = T // QB
    TH = min(1024, T)
    NHALF = T // TH
    S2W = 2 * QB

    nc = bass.Bass("TRN2", target_bir_lowering=False, debug=False,
                   dynamic_dma_scratch_size=4096)

    x_d = nc.dram_tensor("x", [T, D], F32, kind="ExternalInput")
    wq_d = nc.dram_tensor("wq", [128, 8 * D], F32, kind="ExternalInput")
    wk_d = nc.dram_tensor("wk", [128, 8 * D], F32, kind="ExternalInput")
    wv_d = nc.dram_tensor("wv", [128, 8 * D], F32, kind="ExternalInput")
    wo_d = nc.dram_tensor("wo", [128, 8 * D], F32, kind="ExternalInput")
    wf_d = nc.dram_tensor("wf", [128, 8 * D], F32, kind="ExternalInput")
    bq_d = nc.dram_tensor("bq", [128, 8], F32, kind="ExternalInput")
    bk_d = nc.dram_tensor("bk", [128, 8], F32, kind="ExternalInput")
    bv_d = nc.dram_tensor("bv", [128, 8], F32, kind="ExternalInput")
    bo_d = nc.dram_tensor("bo", [128, D], F32, kind="ExternalInput")
    fb_d = nc.dram_tensor("fb", [128, D], F32, kind="ExternalInput")
    out_d = nc.dram_tensor("out", [T, D], F32, kind="ExternalOutput")
    cc_in = nc.dram_tensor("cc_in", [1, 1], F32)
    cc_out = nc.dram_tensor("cc_out", [1, 1], F32)

    Cnp, Snp = _rope_tables(T)
    C_t = nc.inline_tensor(np.ascontiguousarray(Cnp), name="C_t")
    S_t = nc.inline_tensor(np.ascontiguousarray(Snp), name="S_t")
    id_t = nc.inline_tensor(np.eye(128, dtype=ml_dtypes.bfloat16), name="id_t")
    blkd = np.zeros((128, 33), dtype=ml_dtypes.bfloat16)
    blkd[0:64, 0] = 1.0
    blkd[64:128, 32] = 1.0
    blkd_t = nc.inline_tensor(blkd, name="blkd_t")
    blkA = np.zeros((1, 128), dtype=ml_dtypes.bfloat16)
    blkA[0, 0:64] = 1.0
    blkA_t = nc.inline_tensor(blkA, name="blkA_t")
    blkB = np.zeros((1, 128), dtype=ml_dtypes.bfloat16)
    blkB[0, 64:128] = 1.0
    blkB_t = nc.inline_tensor(blkB, name="blkB_t")
    blk64_t = nc.inline_tensor(np.ones((1, 64), ml_dtypes.bfloat16),
                               name="blk64_t")
    ones128b_t = nc.inline_tensor(np.ones((1, 128), ml_dtypes.bfloat16),
                                  name="ones128b_t")
    ones128_t = nc.inline_tensor(np.ones((1, 128), np.float32), name="ones128_t")
    c448_t = nc.inline_tensor(np.full((1, 1), FP8MAX, np.float32), name="c448_t")

    P = Plan()
    DMA_SEMS = ("sd", "sx0", "sx1", "so0", "so1", "sw0", "sw1", "gd", "gc")
    for s in ("pe", "dve", "act", "gp", "cc") + DMA_SEMS:
        P.sem(s)

    qT_o = 0
    kT_o = qT_o + 2 * 8 * T
    v_o = kT_o + 2 * 8 * T
    PT_o = v_o + 2 * NT * (NH * 65)
    wbuf_o = PT_o + 2 * 2 * KC * QB
    ln1T_o = wbuf_o + 2 * 8 * D
    AR = ln1T_o + 2 * 8 * T
    if T == 2048:
        y_o = 0
        wffq_o = 4 * 8 * T
        wroll_o = wffq_o + 2 * 8 * D
        ln2qT_o = PT_o
        oT_o = ln1T_o
        assert wroll_o + 2 * 4 * D <= PT_o
        assert 2 * 8 * T <= 2 * 2 * KC * QB
    else:
        y_o = AR
        wffq_o = y_o + 4 * 8 * T
        wroll_o = wffq_o + 2 * 8 * D
        ln2qT_o = wroll_o + 2 * 4 * D
        oT_o = ln2qT_o + 2 * 8 * T
        AR = oT_o + 2 * 8 * T

    with ExitStack() as ctx:
        e = ctx.enter_context
        arena = e(nc.sbuf_tensor("arena", [128, AR], U8))

        def view(off, nelem, dtype, nb):
            return arena[:, off:off + nelem * nb].bitcast(dtype)

        qT = view(qT_o, 8 * T, BF16, 2)
        kT = view(kT_o, 8 * T, BF16, 2)
        vtok = view(v_o, NT * NH * 65, BF16, 2)
        PT = view(PT_o, 2 * KC * QB, BF16, 2)
        wbuf = view(wbuf_o, 8 * D, BF16, 2)
        ln1T = view(ln1T_o, 8 * T, BF16, 2)
        yv = view(y_o, 8 * T, F32, 4)
        wffq = view(wffq_o, 8 * D, BF16, 2)
        wroll = [view(wroll_o, D, F32, 4), view(wroll_o + 4 * D, D, F32, 4)]
        ln2qT = view(ln2qT_o, 8 * T, BF16, 2)
        oT = view(oT_o, 8 * T, BF16, 2)

        Csb = e(nc.sbuf_tensor("Csb", [128, T], BF16))
        Ssb = e(nc.sbuf_tensor("Ssb", [128, T], BF16))
        idsb = e(nc.sbuf_tensor("idsb", [128, 128], BF16))
        blkdsb = e(nc.sbuf_tensor("blkdsb", [128, 33], BF16))
        blkAsb = e(nc.sbuf_tensor("blkAsb", [1, 128], BF16))
        blkBsb = e(nc.sbuf_tensor("blkBsb", [1, 128], BF16))
        blk64sb = e(nc.sbuf_tensor("blk64sb", [1, 64], BF16))
        ones128b = e(nc.sbuf_tensor("ones128b", [1, 128], BF16))
        bosr = e(nc.sbuf_tensor("bosr", [1, D], BF16))
        r2f = e(nc.sbuf_tensor("r2f", [1, 512], F32))
        negmr = e(nc.sbuf_tensor("negmr", [128, 1], F32))
        negm2 = e(nc.sbuf_tensor("negm2", [128, 1], F32))
        negm3 = e(nc.sbuf_tensor("negm3", [128, 1], F32))
        ones128 = e(nc.sbuf_tensor("ones128", [1, 128], F32))
        c448 = e(nc.sbuf_tensor("c448", [1, 1], F32))
        bqsb = e(nc.sbuf_tensor("bqsb", [128, 8], F32))
        bksb = e(nc.sbuf_tensor("bksb", [128, 8], F32))
        bvsb = e(nc.sbuf_tensor("bvsb", [128, 8], F32))
        if T == 2048:
            fbsb = view(oT_o, D, F32, 4)
        else:
            fbsb = e(nc.sbuf_tensor("fbsb", [128, D], F32))[:, :]
        xb = [e(nc.sbuf_tensor("xb0", [128, D], F32)),
              e(nc.sbuf_tensor("xb1", [128, D], F32))]
        lnc = e(nc.sbuf_tensor("lnc", [128, D], BF16))
        qbh = e(nc.sbuf_tensor("qbh", [128, TH], BF16))
        qsw = e(nc.sbuf_tensor("qsw", [128, TH], BF16))
        r1t = e(nc.sbuf_tensor("r1t", [128, TH], BF16))
        r2t = e(nc.sbuf_tensor("r2t", [128, TH], BF16))
        if T == 2048:
            vb = view(PT_o, T, BF16, 2)
            vb2 = view(PT_o + 2 * T, T, BF16, 2)
            qk_sw = view(PT_o + 2 * T, T, BF16, 2)
            qk_r1 = view(PT_o + 4 * T, T, BF16, 2)
            qk_r2 = view(PT_o + 6 * T, T, BF16, 2)
            qk_b = [view(PT_o + 8 * T, T, BF16, 2),
                    view(PT_o + 10 * T, T, BF16, 2)]
        else:
            vb = e(nc.sbuf_tensor("vb", [128, T], BF16))[:, :]
            vb2 = e(nc.sbuf_tensor("vb2", [128, T], BF16))[:, :]
            qk_sw = e(nc.sbuf_tensor("qk_sw", [128, T], BF16))[:, :]
            qk_r1 = e(nc.sbuf_tensor("qk_r1", [128, T], BF16))[:, :]
            qk_r2 = e(nc.sbuf_tensor("qk_r2", [128, T], BF16))[:, :]
            qk_b = [e(nc.sbuf_tensor("qk_b0", [128, T], BF16))[:, :],
                    e(nc.sbuf_tensor("qk_b1", [128, T], BF16))[:, :]]
        stat = e(nc.sbuf_tensor("stat", [128, 12], F32))
        mv = e(nc.sbuf_tensor("mv", [128, 2], F32))
        sdt = e(nc.sbuf_tensor("sdt", [128, 1], F32))
        rstd = e(nc.sbuf_tensor("rstd", [128, 1], F32))
        m2c = e(nc.sbuf_tensor("m2c", [33, 1], F32))
        mq2 = e(nc.sbuf_tensor("mq2", [1, NH], F32))
        mk2 = e(nc.sbuf_tensor("mk2", [1, NH], F32))
        mrow = e(nc.sbuf_tensor("mrow", [1, NH], F32))
        mpair = e(nc.sbuf_tensor("mpair", [1, NH // 2], F32))
        mh_bc = e(nc.sbuf_tensor("mh_bc", [128, NH // 2], F32))
        r2a = e(nc.sbuf_tensor("r2a", [1, QB], BF16))
        r2bb = e(nc.sbuf_tensor("r2bb", [1, QB], BF16))
        rfb = e(nc.sbuf_tensor("rfb", [128, QB], BF16))
        if T == 2048:
            vq3 = Ssb.ap().bitcast(F32)
            lnb = Csb.ap().bitcast(F32)
            zq2 = view(oT_o + 4 * D, D, F32, 4)
            cq2 = view(oT_o + 8 * D, D, F32, 4)
            vq32 = view(oT_o + 12 * D, D, F32, 4)
            lq2 = view(oT_o + 16 * D, D, BF16, 2)
            ffnb2 = view(oT_o + 16 * D + 2 * D, D, BF16, 2)
            lnb2 = view(oT_o + 16 * D + 4 * D, D, F32, 4)
        else:
            vq3 = e(nc.sbuf_tensor("vq3", [128, D], F32))[:, :]
            lnb = e(nc.sbuf_tensor("lnb", [128, D], F32))[:, :]
            zq2 = e(nc.sbuf_tensor("zq2", [128, D], F32))[:, :]
            cq2 = e(nc.sbuf_tensor("cq2", [128, D], F32))[:, :]
            vq32 = e(nc.sbuf_tensor("vq32", [128, D], F32))[:, :]
            lq2 = e(nc.sbuf_tensor("lq2", [128, D], BF16))[:, :]
            ffnb2 = e(nc.sbuf_tensor("ffnb2", [128, D], BF16))[:, :]
            lnb2 = e(nc.sbuf_tensor("lnb2", [128, D], F32))[:, :]
        attn = vq3
        lncm = e(nc.sbuf_tensor("lncm", [128, NT], F32))
        lncr = e(nc.sbuf_tensor("lncr", [128, NT], F32))
        rs2 = e(nc.sbuf_tensor("rs2", [128, NT], F32))
        amx = [e(nc.sbuf_tensor("amx0", [128, 1], F32)),
               e(nc.sbuf_tensor("amx1", [128, 1], F32))]
        wmx = [e(nc.sbuf_tensor("wmx0", [128, 1], F32)),
               e(nc.sbuf_tensor("wmx1", [128, 1], F32))]
        u1 = e(nc.sbuf_tensor("u1", [128, 1], F32))
        u2 = e(nc.sbuf_tensor("u2", [128, 1], F32))
        gu1 = e(nc.sbuf_tensor("gu1", [128, 1], F32))
        gu2 = e(nc.sbuf_tensor("gu2", [128, 1], F32))
        gtrt = e(nc.sbuf_tensor("gtrt", [128, 1], F32))
        trt = e(nc.sbuf_tensor("trt", [128, 1], F32))
        pad32 = e(nc.sbuf_tensor("pad32", [128, 32], F32))
        tr32 = e(nc.sbuf_tensor("tr32", [128, 32], F32))
        rowv = e(nc.sbuf_tensor("rowv", [1, 4], F32))
        zq = xb[0]
        cq = xb[1]
        lq = e(nc.sbuf_tensor("lq", [128, D], BF16))
        ffnb = e(nc.sbuf_tensor("ffnb", [128, D], BF16))
        if T == 2048:
            outb = [view(wbuf_o, D, F32, 4), view(wbuf_o + 4 * D, D, F32, 4)]
        else:
            outb = [e(nc.sbuf_tensor("outb0", [128, D], F32))[:, :],
                    e(nc.sbuf_tensor("outb1", [128, D], F32))[:, :]]
        scl = e(nc.sbuf_tensor("scl", [1, 8], F32))
        wscl = e(nc.sbuf_tensor("wscl", [1, 8], F32))
        dqt = e(nc.sbuf_tensor("dqt", [1, 1], F32))
        ccs = e(nc.sbuf_tensor("ccs", [1, 1], F32))
        epsb = e(nc.sbuf_tensor("epsb", [128, 1], F32))
        xs_bc = e(nc.sbuf_tensor("xs_bc", [128, 1], F32))
        ws_bc = e(nc.sbuf_tensor("ws_bc", [128, 1], F32))
        dq_bc = e(nc.sbuf_tensor("dq_bc", [128, 1], F32))
        zerob = e(nc.sbuf_tensor("zerob", [128, 1], F32))
        lnstT = [e(nc.sbuf_tensor("lnstT0", [128, 8], F32)),
                 e(nc.sbuf_tensor("lnstT1", [128, 8], F32)),
                 e(nc.sbuf_tensor("lnstT2", [128, 8], F32))]
        lnstS = [e(nc.sbuf_tensor("lnstS0", [128, 8], F32)),
                 e(nc.sbuf_tensor("lnstS1", [128, 8], F32))]
        lnstD = [e(nc.sbuf_tensor("lnstD0", [128, 8], F32)),
                 e(nc.sbuf_tensor("lnstD1", [128, 8], F32))]
        gu1b = e(nc.sbuf_tensor("gu1b", [128, 1], F32))
        gu2b = e(nc.sbuf_tensor("gu2b", [128, 1], F32))
        gtrtb = e(nc.sbuf_tensor("gtrtb", [128, 1], F32))
        if T == 2048:
            zqW = view(PT_o, D, F32, 4)
            cqW = view(PT_o + 4 * D, D, F32, 4)
            vqW = view(PT_o + 8 * D, D, F32, 4)
        else:
            zqW = e(nc.sbuf_tensor("zqW", [128, D], F32))[:, :]
            cqW = e(nc.sbuf_tensor("cqW", [128, D], F32))[:, :]
            vqW = e(nc.sbuf_tensor("vqW", [128, D], F32))[:, :]
        if T == 2048:
            scT = [view(PT_o + 12 * 1024, D, BF16, 2),
                   view(PT_o + 12 * 1024 + 2 * D, D, BF16, 2),
                   view(PT_o + 12 * 1024 + 8 * D, D, BF16, 2)]
            scS = [view(PT_o + 12 * 1024 + 4 * D, D, BF16, 2),
                   view(PT_o + 12 * 1024 + 6 * D, D, BF16, 2)]
        else:
            scT = [e(nc.sbuf_tensor("scT0", [128, D], BF16))[:, :],
                   e(nc.sbuf_tensor("scT1", [128, D], BF16))[:, :],
                   e(nc.sbuf_tensor("scT2", [128, D], BF16))[:, :]]
            scS = [e(nc.sbuf_tensor("scS0", [128, D], BF16))[:, :],
                   e(nc.sbuf_tensor("scS1", [128, D], BF16))[:, :]]

        ps32a = e(nc.psum_tensor("ps32a", [128, 2048], F32))
        ps32b = e(nc.psum_tensor("ps32b", [128, 2048], F32))
        psbf = ps32b[:, 0:512].bitcast(BF16)

        sems = {}
        for s in ("pe", "dve", "act", "gp", "cc") + DMA_SEMS:
            sems[s] = e(nc.semaphore(s))

        def dma(eng, o, i, waits=(), q="sd"):
            pre = (q, P.cnt[q]) if P.cnt[q] > 0 else None
            return P.op(eng, lambda en, o=o, i=i: en.dma_start(out=o, in_=i),
                        waits=list(waits) + [pre], inc=q, amt=16)

        def dve(fn, waits=(), inc="dve"):
            return P.op("vector", fn, waits=waits, inc=inc)

        def act(fn, waits=(), inc="act"):
            return P.op("scalar", fn, waits=waits, inc=inc)

        def pe(fn, waits=(), inc="pe"):
            return P.op("tensor", fn, waits=waits, inc=inc)

        def pe_noinc(fn, waits=()):
            return P.op("tensor", fn, waits=waits, inc=None)

        def gp(fn, waits=(), inc="gp"):
            return P.op("gpsimd", fn, waits=waits, inc=inc)

        tok_eps = P.op("vector", lambda v: v.memset(epsb[:, :], EPS), inc="dve")
        t_consts = []
        for dst, src in ((Csb, C_t), (Ssb, S_t), (idsb, id_t), (blkdsb, blkd_t),
                         (blkAsb, blkA_t), (blkBsb, blkB_t), (blk64sb, blk64_t),
                         (ones128b, ones128b_t),
                         (ones128, ones128_t), (c448, c448_t),
                         (bqsb, bq_d), (bksb, bk_d), (bvsb, bv_d)):
            t_consts.append(dma("sync", dst[:, :], src[:, :]))
        tok_consts = t_consts[-1]
        tok_bosr = dma("gpsimd", bosr[0:1, :], bo_d[0:1, :], q="gd")

        ln_norm_tok = [None, None]
        prev_t6p1 = None
        ev1_tok = None
        ln1T_toks = []
        for tc in range(NT):
            par = tc % 2
            tx = dma("sync", xb[par][:, :], x_d[128 * tc:128 * (tc + 1), :],
                     waits=[ln_norm_tok[par]], q="sx%d" % par)
            t1 = dve(lambda v, b=xb[par]: v.bn_stats(stat[:, 0:6], b[:, 0:512]),
                     waits=[tx])
            t2 = dve(lambda v, b=xb[par]: v.bn_stats(stat[:, 6:12], b[:, 512:1024]),
                     waits=[tx])
            t3 = dve(lambda v: v.bn_aggr(mv[:, :], stat[:, :]), waits=[t1, t2])
            t4 = act(lambda s: s.activation(sdt[:, :], mv[:, 1:2], AF.Sqrt, bias=epsb[:, 0:1]),
                     waits=[t3, tok_eps])
            t5 = dve(lambda v: v.reciprocal(rstd[:, :], sdt[:, :]),
                     waits=[t4, prev_t6p1])
            t5b = dve(lambda v: v.tensor_scalar(
                negmr[:, :], mv[:, 0:1], rstd[:, 0:1], -1.0,
                ALU.mult, ALU.mult), waits=[t5])
            t6 = act(lambda s, b=xb[par]: s.activation(
                lnc[:, :], b[:, :], AF.Identity, bias=negmr[:, 0:1],
                scale=rstd[:, 0:1]), waits=[t5b])
            prev_t6p1 = t6
            ln_norm_tok[par] = t6
            ttr = None
            for j in range(8):
                ttr = pe(lambda t, j=j: t.transpose(
                    psbf[:, 128 * j:128 * (j + 1)], lnc[:, 128 * j:128 * (j + 1)],
                    idsb[:, :]), waits=[t6, ev1_tok] if j == 0 else [])
            ot = ln1T.rearrange("p (j t) -> p j t", j=8)[:, :, 128 * tc:128 * (tc + 1)]
            it = psbf.rearrange("p (j t) -> p j t", j=8)
            ev1_tok = dve(lambda v, ot=ot, it=it: v.tensor_copy(ot, it), waits=[ttr])
            ln1T_toks.append(ev1_tok)
        tok_ln1T = ln1T_toks[-1]

        def vtok_dst(j0, jn, oc, b):
            a = vtok.rearrange("p (tt x) -> p tt x", tt=NT)[
                :, j0:j0 + jn, 130 * oc:130 * oc + 130]
            a = a.rearrange("p tt (h i) -> p tt h i", h=2)
            return a[:, :, :, 32 * b:32 * b + 32]

        def vtok_src(jn, b):
            a = psbf.rearrange("p (t hh j b) -> p t hh j b",
                                    t=8, hh=2, j=32, b=2)
            return a[:, 0:jn, :, :, b]

        proj_state = {"evac": [None, None], "vtr": None,
                      "vb_war": [None, None], "pend_tr": None,
                      "e3": [None, None], "e4": None, "e5": None, "half": 0}

        def emit_proj(w_d, b_sb, dest_kind, wbuf_war):
            wtok = dma("gpsimd", wbuf[:, :], w_d[:, :], waits=[wbuf_war], q="gd")
            evac_tok = proj_state["evac"]
            write_toks = []
            mm = None
            for oc in range(NIC):
                oc_e5 = []
                for h2 in range(NHALF):
                    m = oc * NHALF + h2
                    pv = ps32a[:, (m % 2) * 1024:(m % 2) * 1024 + TH]
                    tcb = h2 * TH
                    for n in range((TH + 511) // 512):
                        nw = min(512, TH - 512 * n)
                        for ic in range(NIC):
                            w0 = []
                            if oc == 0 and h2 == 0 and n == 0 and ic == 0:
                                w0 = [wtok, tok_ln1T]
                            if n == 0 and ic == 0:
                                w0 = w0 + [evac_tok[m % 2]]
                            mm = P.op("tensor", lambda t, pv=pv, n=n, nw=nw, ic=ic,
                                      oc=oc, tcb=tcb: t.matmul(
                                          pv[:, 512 * n:512 * n + nw],
                                          wbuf[:, 1024 * ic + 128 * oc:1024 * ic + 128 * (oc + 1)],
                                          ln1T[:, T * ic + tcb + 512 * n:T * ic + tcb + 512 * n + nw],
                                          start=(ic == 0), stop=(ic == NIC - 1)),
                                      waits=w0,
                                      inc="pe" if ic == NIC - 1 else None)
                    hpar = proj_state["half"] % 2
                    proj_state["half"] += 1
                    qb_t = qbh if hpar == 0 else lq
                    e1 = act(lambda s, pv=pv, b=b_sb, oc=oc, qb_t=qb_t:
                             s.activation(
                                 qb_t[:, 0:TH], pv, AF.Identity,
                                 bias=b[:, oc:oc + 1]),
                             waits=[mm, proj_state["e3"][hpar]])
                    evac_tok[m % 2] = e1
                    esw = None
                    for si, (do, di) in enumerate(
                            ((0, 32), (32, 0), (64, 96), (96, 64))):
                        esw = dve(lambda v, do=do, di=di, qb_t=qb_t: v.tensor_copy(
                            qsw[do:do + 32, 0:TH], qb_t[di:di + 32, 0:TH]),
                            waits=[e1] + ([proj_state["e4"]] if si == 0 else []))
                    e3 = dve(lambda v, tcb=tcb, qb_t=qb_t: v.tensor_mul(
                        r1t[:, 0:TH], qb_t[:, 0:TH], Csb[:, tcb:tcb + TH]),
                        waits=[e1, proj_state["e5"]])
                    proj_state["e3"][hpar] = e3
                    e4 = dve(lambda v, tcb=tcb: v.tensor_mul(
                        r2t[:, 0:TH], qsw[:, 0:TH], Ssb[:, tcb:tcb + TH]),
                        waits=[esw])
                    proj_state["e4"] = e4
                    if dest_kind == "q":
                        dst = qT[:, T * oc + tcb:T * oc + tcb + TH]
                    elif dest_kind == "k":
                        dst = kT[:, T * oc + tcb:T * oc + tcb + TH]
                    else:
                        dst = (vb if oc % 2 == 0 else vb2)[:, tcb:tcb + TH]
                    e5 = dve(lambda v, dst=dst: v.tensor_add(
                        dst, r1t[:, 0:TH], r2t[:, 0:TH]),
                        waits=[e3, e4] + ([proj_state["vb_war"][oc % 2]]
                                          if dest_kind == "v" and h2 == 0 else []))
                    proj_state["e5"] = e5
                    oc_e5.append(e5)
                    write_toks.append(e5)
                    if dest_kind == "v" and h2 == 0 and proj_state["pend_tr"]:
                        proj_state["pend_tr"]()
                        proj_state["pend_tr"] = None
                if dest_kind == "v":
                    def do_tr(oc=oc, e5_tok=oc_e5[-1]):
                        vbp = vb if oc % 2 == 0 else vb2
                        ntile = T // 128
                        ttr_last = None
                        for batch in range((ntile + 7) // 8):
                            j0 = batch * 8
                            jn = min(8, ntile - j0)
                            ttr = None
                            for j in range(j0, j0 + jn):
                                ttr = pe(lambda t, j=j, vbp=vbp: t.transpose(
                                    psbf[:, 128 * (j % 8):128 * (j % 8) + 128],
                                    vbp[:, 128 * j:128 * j + 128], idsb[:, :]),
                                    waits=([e5_tok, proj_state["vtr"]]
                                           if j == j0 else []))
                            for b in range(2):
                                proj_state["vtr"] = dve(
                                    lambda v, b=b, j0=j0, jn=jn, oc=oc:
                                    v.tensor_copy(vtok_dst(j0, jn, oc, b),
                                                  vtok_src(jn, b)),
                                    waits=[ttr] if b == 0 else [])
                            ttr_last = ttr
                            write_toks.append(proj_state["vtr"])
                        proj_state["vb_war"][oc % 2] = ttr_last
                    proj_state["pend_tr"] = do_tr
            if dest_kind == "v" and proj_state["pend_tr"]:
                proj_state["pend_tr"]()
                proj_state["pend_tr"] = None
            return mm, write_toks, wtok

        vmm, v_toks, _ = emit_proj(wv_d, bvsb, "v", tok_consts)

        NW = (T + 511) // 512
        qk_state = {"evac": [None, None], "e3": [None, None], "e5": None,
                    "e4": None, "oc": 0}

        def emit_proj_fc(w_d, b_sb, dest, wbuf_war):
            wtok = dma("gpsimd", wbuf[:, :], w_d[:, :], waits=[wbuf_war], q="gd")
            last_mm = None
            toks = []
            for oc in range(NIC):
                par = qk_state["oc"] % 2
                qk_state["oc"] += 1
                pv = ps32a[:, 0:T] if par == 0 else ps32b[:, 0:T]
                mm = None
                for n in range(NW):
                    nw = min(512, T - 512 * n)
                    for ic in range(NIC):
                        w0 = []
                        if n == 0 and ic == 0:
                            w0 = [qk_state["evac"][par], wtok]
                            if qk_state["oc"] <= 2:
                                w0 += [v_toks[-1], proj_state["vtr"], tok_ln1T]
                        mm = P.op("tensor", lambda t, pv=pv, n=n, nw=nw, ic=ic,
                                  oc=oc: t.matmul(
                                      pv[:, 512 * n:512 * n + nw],
                                      wbuf[:, 1024 * ic + 128 * oc:1024 * ic + 128 * (oc + 1)],
                                      ln1T[:, T * ic + 512 * n:T * ic + 512 * n + nw],
                                      start=(ic == 0), stop=(ic == NIC - 1)),
                                  waits=w0,
                                  inc="pe" if ic == NIC - 1 else None)
                qb_t = qk_b[par]
                e1 = act(lambda s, pv=pv, b=b_sb, oc=oc, qb_t=qb_t: s.activation(
                    qb_t[:, :], pv, AF.Identity, bias=b[:, oc:oc + 1]),
                    waits=[mm, qk_state["e3"][par]])
                qk_state["evac"][par] = e1
                esw = None
                for si, (do, di) in enumerate(
                        ((0, 32), (32, 0), (64, 96), (96, 64))):
                    esw = dve(lambda v, do=do, di=di, qb_t=qb_t: v.tensor_copy(
                        qk_sw[do:do + 32, :], qb_t[di:di + 32, :]),
                        waits=[e1] + ([qk_state["e4"]] if si == 0 else []))
                e3 = dve(lambda v, qb_t=qb_t: v.tensor_mul(
                    qk_r1[:, :], qb_t[:, :], Csb[:, :]),
                    waits=[e1, qk_state["e5"]])
                qk_state["e3"][par] = e3
                e4 = dve(lambda v: v.tensor_mul(
                    qk_r2[:, :], qk_sw[:, :], Ssb[:, :]), waits=[esw])
                qk_state["e4"] = e4
                dst = (qT if dest == "q" else kT)[:, T * oc:T * (oc + 1)]
                e5 = dve(lambda v, dst=dst: v.tensor_add(
                    dst, qk_r1[:, :], qk_r2[:, :]), waits=[e3, e4])
                qk_state["e5"] = e5
                toks.append(e5)
            return last_mm if last_mm else mm, toks, wtok

        USE_FC = False
        if USE_FC:
            qmm, q_toks, _ = emit_proj_fc(wq_d, bqsb, "q", vmm)
            kmm, k_toks, _ = emit_proj_fc(wk_d, bksb, "k", qmm)
        else:
            qmm, q_toks, _ = emit_proj(wq_d, bqsb, "q", vmm)
            kmm, k_toks, _ = emit_proj(wk_d, bksb, "k", qmm)
        tok_vones = dve(lambda v: v.memset(
            vtok.rearrange("p (tt h i) -> p tt h i", tt=NT, h=NH)[:, :, :, 64:65],
            -1.0), waits=[])
        tok_wo = dma("gpsimd", wbuf[:, :], wo_d[:, :], waits=[kmm], q="gd")

        tok_q_all = q_toks[-1]
        tok_k_all = k_toks[-1]
        vb_war = proj_state["vb_war"][0]
        stat_war = proj_state["evac"][0]
        last_stat_red = None
        mrow_tok = None
        for (src, dstrow, wtok_all) in ((qT, mq2, tok_q_all), (kT, mk2, tok_k_all)):
            for oc in range(NIC):
                tsq = act(lambda s, src=src, oc=oc: s.activation(
                    vb[:, 0:T], src[:, T * oc:T * (oc + 1)], AF.Square),
                    waits=[wtok_all, v_toks[-1], vb_war])
                mmq = None
                for n in range((T + 511) // 512):
                    nw = min(512, T - 512 * n)
                    mmq = pe(lambda t, n=n, nw=nw: t.matmul(
                        ps32a[0:33, 512 * n:512 * n + nw],
                        blkdsb[:, :], vb[:, 512 * n:512 * n + nw],
                        start=True, stop=True),
                        waits=[tsq] + (([qk_state["evac"][0],
                                         qk_state["evac"][1],
                                         proj_state["evac"][0],
                                         proj_state["evac"][1]])
                                       if last_stat_red is None
                                       else [last_stat_red]))
                vb_war = mmq
                last_stat_red = dve(lambda v: v.tensor_reduce(
                    m2c[:, :], ps32a[0:33, 0:T], AX.X, ALU.max),
                    waits=[mmq, mrow_tok])
                mrow_tok = dve(lambda v, dstrow=dstrow, oc=oc: v.tensor_copy(
                    dstrow[0:1, 2 * oc:2 * oc + 1], m2c[0:1, 0:1]),
                    waits=[last_stat_red])
                mrow_tok = dve(lambda v, dstrow=dstrow, oc=oc: v.tensor_copy(
                    dstrow[0:1, 2 * oc + 1:2 * oc + 2], m2c[32:33, 0:1]),
                    waits=[last_stat_red])
        tprod = dve(lambda v: v.tensor_mul(mrow[:, :], mq2[:, :], mk2[:, :]),
                    waits=[mrow_tok])
        tsqr = act(lambda s: s.activation(mq2[:, :], mrow[:, :], AF.Sqrt),
                   waits=[tprod])
        tneg = dve(lambda v: v.tensor_scalar(
            mrow[:, :], mq2[:, :], -0.125, -1.0, ALU.mult, ALU.add), waits=[tsqr])
        tpair = dve(lambda v: v.tensor_tensor(
            mpair[:, :], mrow[0:1, 0:NH:2], mrow[0:1, 1:NH:2], ALU.min),
            waits=[tneg])
        tmh_mm = pe(lambda t: t.matmul(
            ps32a[0:128, 0:8], ones128[:, :], mpair[:, :], start=True, stop=True),
            waits=[tpair, last_stat_red])
        tok_mh = dve(lambda v: v.tensor_copy(mh_bc[:, :], ps32a[0:128, 0:8]),
                     waits=[tmh_mm])

        S2 = [ps32a[:, 0:S2W], ps32a[:, 1024:1024 + S2W]]
        av_off = [0, 512, 1024, 1536]
        o_last = None
        rope_hist = {-1: [tok_mh], -2: [tok_mh]}
        avb_prev = {}
        s2_free = {0: tok_mh, 1: tok_mh}
        last_exp = tok_mh
        prev_rfm = None
        prev_ow = None
        pending = None
        gi = 0

        def emit_epilogue_b(p):
            nonlocal prev_rfm, prev_ow, o_last
            offA, offB = p["offA"], p["offB"]
            pe_noinc(lambda t, offA=offA: t.matmul(
                ps32b[0:64, offA:offA + QB], blk64sb[:, :], r2a[0:1, 0:QB],
                start=True, stop=True),
                waits=[p["rc1"], p["rc2"], p["evA"], p["evB"]])
            rfm = pe(lambda t, offA=offA: t.matmul(
                ps32b[64:128, offA:offA + QB], blk64sb[:, :], r2bb[0:1, 0:QB],
                start=True, stop=True, tile_position=(0, 64)), waits=[])
            prev_rfm = rfm
            erf = dve(lambda v, offA=offA: v.tensor_copy(
                rfb[:, 0:QB], ps32b[0:128, offA:offA + QB]),
                waits=[rfm, prev_ow])
            esw = None
            for (do, di) in ((0, 32), (32, 0), (64, 96), (96, 64)):
                esw = dve(lambda v, do=do, di=di: v.tensor_copy(
                    qsw[do:do + 32, 0:QB], qbh[di:di + 32, 0:QB]),
                    waits=[p["evA"], p["evB"]])
            m1 = dve(lambda v, qc0=p["qc0"]: v.tensor_mul(
                r1t[:, 0:QB], qbh[:, 0:QB], Csb[:, qc0:qc0 + QB]),
                waits=[p["evA"], p["evB"]])
            m2 = dve(lambda v, qc0=p["qc0"]: v.tensor_mul(
                r2t[:, 0:QB], qsw[:, 0:QB], Ssb[:, qc0:qc0 + QB]),
                waits=[esw])
            m3 = dve(lambda v: v.tensor_add(qsw[:, 0:QB], r1t[:, 0:QB],
                                            r2t[:, 0:QB]), waits=[m1, m2])
            ow = dve(lambda v, hp=p["hp"], qc0=p["qc0"]: v.tensor_mul(
                oT[:, T * hp + qc0:T * hp + qc0 + QB], qsw[:, 0:QB],
                rfb[:, 0:QB]), waits=[m3, erf])
            rope_hist[p["gi"]] = [p["evA"], p["evB"], p["rc1"], p["rc2"],
                                  ow, erf]
            prev_ow = ow
            o_last = ow

        for qb in range(NQ):
            qc0 = QB * qb
            for hp in range(NH // 2):
                par = gi % 2
                offA = av_off[2 * par]
                offB = av_off[2 * par + 1]
                avA = ps32b[0:65, offA:offA + QB]
                avB = ps32b[0:65, offB:offB + QB]
                new_exp = {}
                new_avb = {}
                av_args = None
                for kc in range(KC):
                    s2 = S2[kc % 2]
                    w0 = [s2_free[kc % 2]]
                    if gi == 0 and kc == 0:
                        w0 += [tok_mh, tok_q_all, tok_k_all]
                    pe_noinc(lambda t, s2=s2, hp=hp, kc=kc, qc0=qc0: t.matmul(
                        s2[:, 0:QB],
                        kT[0:64, T * hp + 128 * kc:T * hp + 128 * (kc + 1)],
                        qT[0:64, T * hp + qc0:T * hp + qc0 + QB],
                        start=True, stop=True, tile_position=(0, 0)),
                        waits=w0)
                    smm = pe(lambda t, s2=s2, hp=hp, kc=kc, qc0=qc0: t.matmul(
                        s2[:, QB:2 * QB],
                        kT[64:128, T * hp + 128 * kc:T * hp + 128 * (kc + 1)],
                        qT[64:128, T * hp + qc0:T * hp + qc0 + QB],
                        start=True, stop=True, tile_position=(64, 0)),
                        waits=[])
                    pt_out = PT.rearrange("p (h x) -> p h x", h=2)[
                        :, :, QB * kc:QB * (kc + 1)]
                    s2r = s2.rearrange("p (h x) -> p h x", h=2)
                    et = act(lambda s, pt_out=pt_out, s2r=s2r, hp=hp: s.activation(
                        pt_out, s2r, AF.Exp, bias=mh_bc[:, hp:hp + 1], scale=0.125),
                        waits=[smm, avb_prev.get(kc)])
                    new_exp[kc] = et
                    s2_free[kc % 2] = et

                    if kc == min(6, KC - 1) and pending is not None:
                        emit_epilogue_b(pending)
                        pending = None

                    def emit_av(kav):
                        avw = [new_exp[kav]]
                        if kav == 0:
                            avw = avw + rope_hist[gi - 2]
                        pe_noinc(lambda t, kav=kav, hp=hp, avA=avA: t.matmul(
                            avA, vtok[:, (65 * NH) * kav + 65 * (2 * hp):(65 * NH) * kav + 65 * (2 * hp) + 65],
                            PT[:, QB * kav:QB * (kav + 1)],
                            start=(kav == 0), stop=(kav == KC - 1)), waits=avw)
                        return P.op("tensor", lambda t, kav=kav, hp=hp, avB=avB:
                                    t.matmul(
                                        avB, vtok[:, (65 * NH) * kav + 65 * (2 * hp + 1):(65 * NH) * kav + 65 * (2 * hp + 1) + 65],
                                        PT[:, KC * QB + QB * kav:KC * QB + QB * (kav + 1)],
                                        start=(kav == 0), stop=(kav == KC - 1)),
                                    waits=[], inc="pe")
                    if kc >= 1:
                        new_avb[kc - 1] = emit_av(kc - 1)
                new_avb[KC - 1] = emit_av(KC - 1)
                av_tokB = new_avb[KC - 1]
                avb_prev = new_avb
                last_exp = new_exp[KC - 1]
                evA = dve(lambda v, avA=avA: v.tensor_copy(
                    qbh[0:64, 0:QB], avA[0:64, 0:QB]),
                    waits=[av_tokB, prev_rfm, prev_ow])
                evB = dve(lambda v, avB=avB: v.tensor_copy(
                    qbh[64:128, 0:QB], avB[0:64, 0:QB]), waits=[av_tokB])
                rc1 = dve(lambda v, avA=avA: v.reciprocal(
                    r2a[0:1, 0:QB], avA[64:65, 0:QB]), waits=[av_tokB])
                rc2 = dve(lambda v, avB=avB: v.reciprocal(
                    r2bb[0:1, 0:QB], avB[64:65, 0:QB]), waits=[av_tokB])
                pending = {"gi": gi, "hp": hp, "qc0": qc0, "offA": offA,
                           "offB": offB, "rc1": rc1, "rc2": rc2,
                           "evA": evA, "evB": evB}
                gi += 1
        emit_epilogue_b(pending)
        pending = None
        prev_rf_evac = None

        tok_pad = dve(lambda v: v.memset(pad32[:, :], 0.0), waits=[])
        tok_zerob = dve(lambda v: v.memset(zerob[:, :], 0.0), waits=[])

        def maxtree(src, toks):
            tok = dve(lambda v, src=src: v.tensor_copy(pad32[:, 0:1], src[:, 0:1]),
                      waits=list(toks) + [tok_pad])
            tok = dve(lambda v: v.transpose(tr32[:, :], pad32[:, :]), waits=[tok])
            for j, row in enumerate((0, 32, 64, 96)):
                tok = dve(lambda v, j=j, row=row: v.reduce_max(
                    rowv[0:1, j:j + 1], tr32[row:row + 1, 0:32], AX.X),
                    waits=[tok])
            tok = dve(lambda v: v.reduce_max(trt[0:1, 0:1], rowv[0:1, 0:4], AX.X),
                      waits=[tok])
            return tok

        RD = 1.0 / D

        def stats_math(C, pA, pB, mean_out=None):
            mo = mean_out if mean_out is not None else C[:, 2:3]
            m1 = dve(lambda v, C=C, mo=mo: v.tensor_scalar_mul(
                mo, C[:, 0:1], RD), waits=[pA])
            m2 = dve(lambda v, C=C, mo=mo: v.tensor_mul(
                C[:, 4:5], mo, mo), waits=[m1])
            m3 = dve(lambda v, C=C: v.tensor_scalar_mul(
                C[:, 3:4], C[:, 1:2], RD), waits=[pB])
            m4 = dve(lambda v, C=C: v.tensor_sub(
                C[:, 3:4], C[:, 3:4], C[:, 4:5]), waits=[m2, m3])
            return m1, m4

        def velt(dst, srcf32, waits, cqx=None, vq3x=None):
            cqa = cqx if cqx is not None else cq.ap()[:, :]
            vqa = vq3x if vq3x is not None else vq3
            t1 = dve(lambda v, s=srcf32, cqa=cqa: v.tensor_single_scalar(
                cqa.bitcast(I32), s.bitcast(I32), 0x7F800000,
                ALU.bitwise_and), waits=waits)
            t2 = dve(lambda v, cqa=cqa, vqa=vqa: v.tensor_scalar(
                vqa, cqa, VELT_K, VELT_MIN, ALU.mult, ALU.max),
                waits=[t1])
            t3 = dve(lambda v, s=srcf32, cqa=cqa, vqa=vqa: v.tensor_tensor(
                cqa, s, vqa, ALU.add), waits=[t2])
            t4 = dve(lambda v, dst=dst, cqa=cqa, vqa=vqa: v.tensor_tensor(
                dst, cqa, vqa, ALU.subtract), waits=[t3])
            return t4

        wmx_tok = None
        wroll_war = [None, None]
        velt_war = None
        wq_toks = []
        tokw = None
        tok_ws = None
        tok_wsi = None
        twsb = None
        bc_mm2 = None

        def emit_wstep(i):
            nonlocal wmx_tok, velt_war, tokw, tok_ws, tok_wsi, twsb, bc_mm2
            if i < NIC:
                wc = i
                wpar = wc % 2
                tw = dma("sync", wroll[wpar][:, :],
                         wf_d[:, 1024 * wc:1024 * (wc + 1)],
                         waits=[wroll_war[wpar]] + ([k_toks[-1], o_last,
                                                    v_toks[-1]]
                                                   if wc <= 1 else []),
                         q="sw%d" % wpar)
                rw = dve(lambda v, wpar=wpar: v.reduce_max(
                    u1[:, :], wroll[wpar][:, :], AX.X,
                    apply_absolute_value=True),
                    waits=[tw, wmx_tok])
                wroll_war[wpar] = rw
                if wc == 0:
                    wmx_tok = dve(lambda v: v.tensor_copy(wmx[0][:, :],
                                                          u1[:, :]),
                                  waits=[rw])
                else:
                    wmx_tok = dve(lambda v, wc=wc: v.tensor_max(
                        wmx[wc % 2][:, :], wmx[1 - wc % 2][:, :], u1[:, :]),
                        waits=[rw, wmx_tok])
            else:
                wc = i - NIC
                wpar = wc % 2
                if wc == 0:
                    wmx_fin = wmx[(NIC - 1) % 2]
                    tokw = maxtree(wmx_fin, [wmx_tok])
                    tokw = dve(lambda v: v.tensor_scalar_max(
                        wscl[0:1, 1:2], trt[0:1, 0:1], 1e-12), waits=[tokw])
                    tokw = dve(lambda v: v.reciprocal(wscl[0:1, 5:6],
                                                      wscl[0:1, 1:2]),
                               waits=[tokw])
                    tokw = dve(lambda v: v.tensor_scalar_mul(
                        wscl[0:1, 2:3], wscl[0:1, 5:6], FP8MAX), waits=[tokw])
                    tok_ws = dve(lambda v: v.tensor_scalar_min(
                        wscl[0:1, 3:4], wscl[0:1, 2:3], FP8MAX), waits=[tokw])
                    tok_wsi = dve(lambda v: v.reciprocal(wscl[0:1, 4:5],
                                                         wscl[0:1, 3:4]),
                                  waits=[tok_ws])
                    bc_mm2 = pe(lambda t: t.matmul(
                        ps32b[0:128, 1540:1541], ones128[:, :], wscl[0:1, 3:4],
                        start=True, stop=True), waits=[tok_ws, o_last])
                    twsb = dve(lambda v: v.tensor_copy(ws_bc[:, :],
                                                       ps32b[0:128, 1540:1541]),
                               waits=[bc_mm2])
                tw = dma("sync", wroll[wpar][:, :],
                         wf_d[:, 1024 * wc:1024 * (wc + 1)],
                         waits=[wroll_war[wpar]], q="sw%d" % wpar)
                tz = dve(lambda v, wpar=wpar: v.tensor_scalar_mul(
                    zqW, wroll[wpar][:, :], ws_bc[:, 0:1]),
                    waits=[tw, twsb, velt_war])
                wroll_war[wpar] = tz
                tq = velt(wffq[:, 1024 * wc:1024 * (wc + 1)], zqW, [tz],
                          cqx=cqW, vq3x=vqW)
                velt_war = tq
                wq_toks.append(tq)

        y_toks = []
        prev_q3 = None
        oproj_evac = [None, None, None]
        amx_tok = None
        sA_war = [None, None]
        gu_war = [None, None]
        out_war = [k_toks[-1], o_last]
        x2_tok = [ln_norm_tok[0], ln_norm_tok[1]]
        p4s = {"s5": None, "mm": None}

        pvs3 = [ps32a[:, 0:1024], ps32a[:, 1024:2048], ps32b[:, 0:1024]]

        def emit_p4_head(tc):
            par = tc % 2
            p3i = tc % 3
            tx = dma("sync", xb[par][:, :], x_d[128 * tc:128 * (tc + 1), :],
                     waits=[x2_tok[par]], q="sx%d" % par)
            pv = pvs3[p3i]
            mm = None
            for n in range(2):
                for dc in range(NIC):
                    w0 = []
                    if tc == 0 and n == 0 and dc == 0:
                        w0 = [last_exp, tok_wo, o_last, prev_rf_evac]
                    if tc == 2 and n == 0 and dc == 0:
                        w0 = [o_last]
                    if n == 0 and dc == 0:
                        w0 = w0 + [oproj_evac[p3i]]
                    mm = P.op("tensor", lambda t, pv=pv, n=n, dc=dc, tc=tc: t.matmul(
                        pv[:, 512 * n:512 * n + 512],
                        oT[:, T * dc + 128 * tc:T * dc + 128 * (tc + 1)],
                        wbuf[:, 1024 * dc + 512 * n:1024 * dc + 512 * n + 512],
                        start=(dc == 0), stop=False),
                        waits=w0, inc=None)
                mm = P.op("tensor", lambda t, pv=pv, n=n: t.matmul(
                    pv[:, 512 * n:512 * n + 512], ones128b[:, :],
                    bosr[0:1, 512 * n:512 * n + 512],
                    start=False, stop=True),
                    waits=[tok_bosr] if tc == 0 and n == 0 else [],
                    inc="pe")
            p4s["mm"] = mm
            C = lnstT[p3i]
            pA = act(lambda s, pv=pv, C=C, p3i=p3i: s.activation(
                scT[p3i], pv, AF.Identity,
                bias=zerob[:, 0:1], accum_out=C[:, 0:1]),
                waits=[mm, tok_zerob] + ([o_last] if tc <= 1 else []))
            pB = act(lambda s, pv=pv, C=C, p3i=p3i: s.activation(
                scT[p3i], pv, AF.Square,
                accum_out=C[:, 1:2]), waits=[pA])
            _, m4 = stats_math(C, pA, pB)
            t4 = act(lambda s, C=C: s.activation(C[:, 5:6], C[:, 3:4], AF.Sqrt,
                                                 bias=epsb[:, 0:1]),
                     waits=[m4])
            t5 = dve(lambda v, C=C: v.reciprocal(C[:, 6:7], C[:, 5:6]),
                     waits=[t4])
            t5b = dve(lambda v, C=C: v.tensor_scalar(
                C[:, 7:8], C[:, 2:3], C[:, 6:7], -1.0,
                ALU.mult, ALU.mult), waits=[t5])
            t6 = act(lambda s, pv=pv, C=C: s.activation(
                lnb[:, :], pv, AF.Identity, bias=C[:, 7:8],
                scale=C[:, 6:7]), waits=[t5b])
            oproj_evac[p3i] = t6
            ty = dve(lambda v, tc=tc, par=par: v.tensor_add(
                yv[:, D * tc:D * (tc + 1)], xb[par][:, :], lnb[:, :]),
                waits=[t6, tx] + (out_war if tc <= 1 else []))
            x2_tok[par] = ty
            y_toks.append(ty)

        def emit_p4_tail(tc):
            nonlocal amx_tok, prev_q3
            par = tc % 2
            ty = y_toks[tc]
            Cs = lnstS[par]
            pAs = act(lambda s, tc=tc, Cs=Cs, par=par: s.activation(
                scS[par], yv[:, D * tc:D * (tc + 1)], AF.Identity,
                bias=zerob[:, 0:1], accum_out=Cs[:, 0:1]),
                waits=[ty, sA_war[par]])
            pBs = act(lambda s, tc=tc, Cs=Cs, par=par: s.activation(
                scS[par], yv[:, D * tc:D * (tc + 1)], AF.Square,
                accum_out=Cs[:, 1:2]), waits=[pAs])
            sA_war[par] = pBs
            sm, m4s = stats_math(Cs, pAs, pBs,
                                 mean_out=lncm[:, tc:tc + 1])
            s4 = act(lambda s, Cs=Cs: s.activation(
                Cs[:, 5:6], Cs[:, 3:4], AF.Sqrt, bias=epsb[:, 0:1]),
                waits=[m4s])
            s5 = dve(lambda v, tc=tc, Cs=Cs: v.reciprocal(
                lncr[:, tc:tc + 1], Cs[:, 5:6]), waits=[s4])
            p4s["s5"] = s5
            guP, gu2P, gtP = ([gu1, gu2, gtrt] if par == 0
                              else [gu1b, gu2b, gtrtb])
            r1 = dve(lambda v, tc=tc, guP=guP: v.reduce_max(
                guP[:, :], yv[:, D * tc:D * (tc + 1)], AX.X),
                waits=[ty, gu_war[par]])
            r2_ = dve(lambda v, tc=tc, gu2P=gu2P: v.reduce_max(
                gu2P[:, :], yv[:, D * tc:D * (tc + 1)], AX.X, negate=True),
                waits=[ty])
            q1 = dve(lambda v, tc=tc, guP=guP: v.tensor_scalar(
                guP[:, :], guP[:, :], lncm[:, tc:tc + 1], lncr[:, tc:tc + 1],
                ALU.subtract, ALU.mult), waits=[r1, sm, s5])
            q2 = dve(lambda v, tc=tc, gu2P=gu2P: v.tensor_scalar(
                gu2P[:, :], gu2P[:, :], lncm[:, tc:tc + 1], lncr[:, tc:tc + 1],
                ALU.add, ALU.mult), waits=[r2_, sm, s5])
            q3 = dve(lambda v, guP=guP, gu2P=gu2P, gtP=gtP: v.tensor_max(
                gtP[:, :], guP[:, :], gu2P[:, :]), waits=[q1, q2])
            gu_war[par] = q3
            prev_q3 = q3
            if tc == 0:
                amx_tok = dve(lambda v, gtP=gtP: v.tensor_copy(
                    amx[0][:, :], gtP[:, :]), waits=[q3])
            else:
                amx_tok = dve(lambda v, tc=tc, gtP=gtP: v.tensor_max(
                    amx[tc % 2][:, :], amx[1 - tc % 2][:, :], gtP[:, :]),
                    waits=[q3, amx_tok])
            gu_war[par] = amx_tok

        if NT >= 12:
            wsched = {tc: [2 * tc, 2 * tc + 1] for tc in range(4)}
            for tc in range(4, 12):
                wsched[tc] = [tc + 4]
        else:
            wsched = {tc: [tc] for tc in range(NT)}
        n_wdone = sum(len(v) for v in wsched.values())
        for tc in range(NT):
            emit_p4_head(tc)
            if tc >= 1:
                emit_p4_tail(tc - 1)
            for i in wsched.get(tc, []):
                emit_wstep(i)
        emit_p4_tail(NT - 1)
        for i in range(n_wdone, 2 * NIC):
            emit_wstep(i)
        amx_fin = amx[(NT - 1) % 2]
        tok_fb = dma("sync", fbsb, fb_d[:, :], waits=[p4s["mm"]])

        tok = maxtree(amx_fin, [wmx_tok, amx_tok, tokw])
        tok = dve(lambda v: v.tensor_copy(scl[0:1, 0:1], trt[0:1, 0:1]),
                  waits=[tok])
        tgd = dma("gpsimd", cc_in[0:1, 0:1], scl[0:1, 0:1], waits=[tok], q="gc")
        tcc = P.op("gpsimd", lambda g: g.collective_compute(
            "AllReduce", ALU.max, replica_groups=[list(range(NCORES))],
            ins=[cc_in.ap().opt()], outs=[cc_out.ap().opt()]),
            waits=[tgd], inc="cc", amt=1)
        tgd2 = dma("gpsimd", ccs[0:1, 0:1], cc_out[0:1, 0:1], waits=[tcc], q="gc")
        tok = dve(lambda v: v.tensor_scalar_max(scl[0:1, 1:2], ccs[0:1, 0:1],
                                                1e-12), waits=[tgd2])
        tok = dve(lambda v: v.reciprocal(scl[0:1, 5:6], scl[0:1, 1:2]),
                  waits=[tok])
        tok = dve(lambda v: v.tensor_scalar_mul(scl[0:1, 2:3], scl[0:1, 5:6],
                                                FP8MAX), waits=[tok])
        tok_xs = dve(lambda v: v.tensor_scalar_min(scl[0:1, 3:4], scl[0:1, 2:3],
                                                   FP8MAX), waits=[tok])
        tok_xsi = dve(lambda v: v.reciprocal(scl[0:1, 4:5], scl[0:1, 3:4]),
                      waits=[tok_xs])
        tok_dq = dve(lambda v: v.tensor_mul(dqt[:, :], scl[0:1, 4:5],
                                            wscl[0:1, 4:5]),
                     waits=[tok_xsi, tok_wsi])
        bc_mm1 = pe(lambda t: t.matmul(ps32b[0:128, 1541:1542], ones128[:, :],
                                       scl[0:1, 3:4], start=True, stop=True),
                    waits=[tok_xs, bc_mm2])
        bc_mm3 = pe(lambda t: t.matmul(ps32b[0:128, 1542:1543], ones128[:, :],
                                       dqt[:, :], start=True, stop=True),
                    waits=[tok_dq, bc_mm1])
        txsb = dve(lambda v: v.tensor_copy(xs_bc[:, :], ps32b[0:128, 1541:1542]),
                   waits=[bc_mm1])
        tdqb = dve(lambda v: v.tensor_copy(dq_bc[:, :], ps32b[0:128, 1542:1543]),
                   waits=[bc_mm3])
        trs2 = dve(lambda v: v.tensor_scalar_mul(rs2[:, 0:NT], lncr[:, 0:NT],
                                                 xs_bc[:, 0:1]),
                   waits=[txsb, p4s["s5"]])
        tfbr = dve(lambda v: v.tensor_scalar(
            bosr[0:1, :], fbsb[0:1, :], scl[0:1, 3:4], wscl[0:1, 3:4],
            ALU.mult, ALU.mult), waits=[tok_fb, tok_xs, tok_ws])

        qsets = [(zq.ap()[:, :], cq.ap()[:, :], vq3, lq.ap()[:, :],
                  ffnb.ap()[:, :], lnb, negm2),
                 (zq2, cq2, vq32, lq2, ffnb2, lnb2, negm3)]
        p6 = {"ev6": [None, None],
              "tvq_war": [y_toks[NT - 2], y_toks[NT - 1]],
              "d6_war": [None, None], "ffn_evac": [None, None],
              "out_dma": [None, None],
              "ln2q_war": [o_last, wq_toks[-1], oproj_evac[0],
                           oproj_evac[1], oproj_evac[2],
                           sA_war[0], sA_war[1]],
              "mm": [None, None], "d7_war": [None, None],
              "pB_war": [None, None]}
        final_toks = []

        def emit_quant(tc):
            par = tc % 2
            zqp, cqp, vq3p, lqp, ffnp, lnbp, ngp = qsets[par]
            pb = dve(lambda v, tc=tc, ngp=ngp: v.tensor_scalar(
                ngp[:, 0:1], lncm[:, tc:tc + 1],
                rs2[:, tc:tc + 1], -1.0, ALU.mult, ALU.mult),
                waits=[trs2, p6["tvq_war"][par]])
            tvq = act(lambda s, tc=tc, zqp=zqp, ngp=ngp: s.activation(
                zqp, yv[:, D * tc:D * (tc + 1)], AF.Identity,
                bias=ngp[:, 0:1], scale=rs2[:, tc:tc + 1]),
                waits=[pb, y_toks[tc]])
            tq = velt(lqp, zqp, [tvq], cqx=cqp, vq3x=vq3p)
            p6["tvq_war"][par] = tq
            ttr = None
            for j in range(8):
                ttr = pe(lambda t, j=j, lqp=lqp: t.transpose(
                    psbf[:, 128 * j:128 * (j + 1)], lqp[:, 128 * j:128 * (j + 1)],
                    idsb[:, :]),
                    waits=[tq, p6["ev6"][1 - par]] + p6["ln2q_war"]
                    if j == 0 else [])
                p6["ln2q_war"] = []
            ot = ln2qT.rearrange("p (j t) -> p j t", j=8)[
                :, :, 128 * tc:128 * (tc + 1)]
            it = psbf.rearrange("p (j t) -> p j t", j=8)
            p6["ev6"][par] = dve(lambda v, ot=ot, it=it: v.tensor_copy(ot, it),
                                 waits=[ttr])
            return p6["ev6"][par]

        def emit_ffn_mm(tc):
            par = tc % 2
            pv = ps32a[:, 1024 * par:1024 * par + 1024]
            mm = None
            for n in range(2):
                for ic in range(NIC):
                    w0 = []
                    if tc == 0 and n == 0 and ic == 0:
                        w0 = [tdqb, txsb, wq_toks[-1]]
                    if n == 0 and ic == 0:
                        w0 = w0 + [p6["ffn_evac"][par], ev_list[tc]]
                    mm = P.op("tensor", lambda t, pv=pv, n=n, ic=ic, tc=tc:
                              t.matmul(
                                  pv[:, 512 * n:512 * n + 512],
                                  ln2qT[:, T * ic + 128 * tc:T * ic + 128 * (tc + 1)],
                                  wffq[:, 1024 * ic + 512 * n:1024 * ic + 512 * n + 512],
                                  start=(ic == 0), stop=False),
                              waits=w0, inc=None)
                mm = P.op("tensor", lambda t, pv=pv, n=n: t.matmul(
                    pv[:, 512 * n:512 * n + 512], ones128b[:, :],
                    bosr[0:1, 512 * n:512 * n + 512],
                    start=False, stop=True),
                    waits=[tfbr] if tc == 0 and n == 0 else [],
                    inc="pe")
            p6["mm"][par] = mm

        def emit_ffn_rest(tc):
            par = tc % 2
            zqp, cqp, vq3p, lqp, ffnp, lnbp, ngp = qsets[par]
            pv = ps32a[:, 1024 * par:1024 * par + 1024]
            Cd = lnstD[par]
            f1 = act(lambda s, pv=pv, ffnp=ffnp, Cd=Cd: s.activation(
                ffnp, pv, AF.Identity, bias=zerob[:, 0:1],
                scale=dq_bc[:, 0:1], accum_out=Cd[:, 0:1]),
                waits=[p6["mm"][par], tdqb, tok_zerob, p6["d6_war"][par]])
            p6["ffn_evac"][par] = f1
            pBd = act(lambda s, ffnp=ffnp, Cd=Cd, par=par: s.activation(
                wroll[par][:, :], ffnp, AF.Square,
                accum_out=Cd[:, 1:2]), waits=[f1, p6["pB_war"][par]])
            p6["pB_war"][par] = pBd
            _, m4d = stats_math(Cd, f1, pBd)
            d4 = act(lambda s, Cd=Cd: s.activation(
                Cd[:, 5:6], Cd[:, 3:4], AF.Sqrt, bias=epsb[:, 0:1]),
                waits=[m4d])
            d5 = dve(lambda v, Cd=Cd: v.reciprocal(Cd[:, 6:7], Cd[:, 5:6]),
                     waits=[d4])
            d5b = dve(lambda v, Cd=Cd: v.tensor_scalar(
                Cd[:, 7:8], Cd[:, 2:3], Cd[:, 6:7], -1.0,
                ALU.mult, ALU.mult), waits=[d5])
            d6 = act(lambda s, ffnp=ffnp, lnbp=lnbp, Cd=Cd: s.activation(
                lnbp, ffnp, AF.Identity, bias=Cd[:, 7:8],
                scale=Cd[:, 6:7]), waits=[d5b, p6["d7_war"][par]])
            p6["d6_war"][par] = d6

        def emit_ffn_out(tc):
            par = tc % 2
            zqp, cqp, vq3p, lqp, ffnp, lnbp, ngp = qsets[par]
            d7 = dve(lambda v, tc=tc, par=par, lnbp=lnbp: v.tensor_add(
                outb[par][:, :], yv[:, D * tc:D * (tc + 1)], lnbp),
                waits=[p6["d6_war"][par], p6["out_dma"][par]])
            p6["d7_war"][par] = d7
            od = dma("sync", out_d[128 * tc:128 * (tc + 1), :], outb[par][:, :],
                     waits=[d7], q="so%d" % par)
            p6["out_dma"][par] = od
            final_toks.append(od)

        ev_list = {}
        ev_list[0] = emit_quant(0)
        for tc in range(1, NT):
            emit_ffn_mm(tc - 1)
            ev_list[tc] = emit_quant(tc)
            emit_ffn_rest(tc - 1)
            if tc >= 2:
                emit_ffn_out(tc - 2)
        emit_ffn_mm(NT - 1)
        emit_ffn_rest(NT - 1)
        emit_ffn_out(NT - 2)
        emit_ffn_out(NT - 1)

        P.op("sync", lambda s: s.nop(),
             waits=[final_toks[-1], final_toks[-2] if NT > 1 else None])

        with nc.allow_low_precision(reason="bf16 operands by design (blessed)"), \
                nc.Block() as block:
            def make_runner(ename):
                def runner(en):
                    for fn, w, inc, amt in P.stream[ename]:
                        for s, v in w:
                            en.wait_ge(sems[s], v)
                        ins = fn(en)
                        if inc is not None and ins is not None:
                            ins.then_inc(sems[inc], amt)
                return runner
            block.sync(make_runner("sync"))
            block.tensor(make_runner("tensor"))
            block.vector(make_runner("vector"))
            block.scalar(make_runner("scalar"))
            block.gpsimd(make_runner("gpsimd"))
    return nc


def _host_prep(L0_attn_x, L0_attn_Wq, L0_attn_bq, L0_attn_Wk, L0_attn_bk,
               L0_attn_Wv, L0_attn_bv, L0_attn_Wo, L0_attn_bo,
               L0_ffn_weight, L0_ffn_bias, T):
    perm = np.concatenate([np.concatenate([np.arange(64 * h, 64 * (h + 1), 2),
                                           np.arange(64 * h + 1, 64 * (h + 1), 2)])
                           for h in range(NH)])

    def lay_lhsT(Wp):
        A = np.ascontiguousarray(Wp.T)
        return np.ascontiguousarray(
            A.reshape(8, 128, D).transpose(1, 0, 2).reshape(128, 8 * D))

    def lay_rhs(W):
        A = np.ascontiguousarray(W.T)
        return np.ascontiguousarray(
            A.reshape(8, 128, D).transpose(1, 0, 2).reshape(128, 8 * D))

    wq = lay_lhsT(np.asarray(L0_attn_Wq)[perm, :])
    wk = lay_lhsT(np.asarray(L0_attn_Wk)[perm, :])
    wv = lay_lhsT(np.asarray(L0_attn_Wv)[perm, :])
    wo = lay_rhs(np.asarray(L0_attn_Wo))
    wf = lay_rhs(np.asarray(L0_ffn_weight))
    bq = np.ascontiguousarray(np.asarray(L0_attn_bq)[perm].reshape(8, 128).T)
    bk = np.ascontiguousarray(np.asarray(L0_attn_bk)[perm].reshape(8, 128).T)
    bv = np.ascontiguousarray(np.asarray(L0_attn_bv)[perm].reshape(8, 128).T)
    bo = np.ascontiguousarray(np.broadcast_to(np.asarray(L0_attn_bo), (128, D)))
    fb = np.ascontiguousarray(np.broadcast_to(np.asarray(L0_ffn_bias), (128, D)))
    x = np.asarray(L0_attn_x)
    in_maps = []
    for b in range(NCORES):
        in_maps.append({
            "x": np.ascontiguousarray(x[b, :T]), "wq": wq, "wk": wk, "wv": wv,
            "wo": wo, "wf": wf, "bq": bq, "bk": bk, "bv": bv, "bo": bo, "fb": fb,
        })
    return in_maps


def kernel(L0_attn_x, L0_attn_Wq, L0_attn_bq, L0_attn_Wk, L0_attn_bk,
           L0_attn_Wv, L0_attn_bv, L0_attn_Wo, L0_attn_bo,
           L0_ffn_weight, L0_ffn_bias):
    T = np.asarray(L0_attn_x).shape[1]
    in_maps = _host_prep(L0_attn_x, L0_attn_Wq, L0_attn_bq, L0_attn_Wk,
                         L0_attn_bk, L0_attn_Wv, L0_attn_bv, L0_attn_Wo,
                         L0_attn_bo, L0_ffn_weight, L0_ffn_bias, T)
    nc = build(T)
    res = run_bass_kernel_spmd(nc, in_maps, core_ids=list(range(NCORES)))
    out = np.stack([r["out"] for r in res.results], axis=0)
    return out.astype(np.float32)


# version 24
# speedup vs baseline: 1.3586x; 1.0398x over previous
import numpy as np
import ml_dtypes
import concourse.bass as bass
import concourse.mybir as mybir
from concourse.bass_utils import run_bass_kernel_spmd
from contextlib import ExitStack

F32 = mybir.dt.float32
BF16 = mybir.dt.bfloat16
I32 = mybir.dt.int32
U8 = mybir.dt.uint8
AF = mybir.ActivationFunctionType
ALU = mybir.AluOpType
AX = mybir.AxisListType

D = 1024
NH = 16
HD = 64
NIC = 8
NCORES = 8
EPS = 1e-5
FP8MAX = 448.0
VELT_K = float(1.5 * 2.0 ** 20)
VELT_MIN = 24576.0

ENGS = ("sync", "tensor", "vector", "scalar", "gpsimd")


class Plan:
    def __init__(self):
        self.stream = {e: [] for e in ENGS}
        self.cnt = {}
        self.lastwait = {e: {} for e in ENGS}

    def sem(self, name):
        self.cnt[name] = 0

    def op(self, eng, fn, waits=(), inc=None, amt=1):
        w = []
        for t in waits:
            if t is None:
                continue
            s, v = t
            if self.lastwait[eng].get(s, 0) >= v:
                continue
            self.lastwait[eng][s] = v
            w.append((s, v))
        tok = None
        if inc is not None:
            self.cnt[inc] += amt
            tok = (inc, self.cnt[inc])
        self.stream[eng].append((fn, w, inc, amt))
        return tok


def _rope_tables(T):
    pos = np.arange(T, dtype=np.float64)[:, None]
    dim_pair = np.arange(0, HD, 2, dtype=np.float64)[None, :]
    theta = 10000.0 ** (-dim_pair / HD)
    freqs = pos * theta
    rope = np.concatenate([np.cos(freqs), np.sin(freqs)], axis=-1)
    ctab = rope[:, 0::2]
    stab = rope[:, 1::2]
    p = np.arange(128)
    C = ctab[:, p % 32].T
    sgn = np.where((p % 64) < 32, -1.0, 1.0)
    S = (stab[:, p % 32] * sgn[None, :]).T
    return C.astype(ml_dtypes.bfloat16), S.astype(ml_dtypes.bfloat16)


def build(T):
    NT = T // 128
    KC = T // 128
    QB = min(512, T)
    NQ = T // QB
    TH = min(1024, T)
    NHALF = T // TH
    S2W = 2 * QB

    nc = bass.Bass("TRN2", target_bir_lowering=False, debug=False,
                   dynamic_dma_scratch_size=4096)

    x_d = nc.dram_tensor("x", [T, D], F32, kind="ExternalInput")
    wq_d = nc.dram_tensor("wq", [128, 8 * D], F32, kind="ExternalInput")
    wk_d = nc.dram_tensor("wk", [128, 8 * D], F32, kind="ExternalInput")
    wv_d = nc.dram_tensor("wv", [128, 8 * D], F32, kind="ExternalInput")
    wo_d = nc.dram_tensor("wo", [128, 8 * D], F32, kind="ExternalInput")
    wf_d = nc.dram_tensor("wf", [128, 8 * D], F32, kind="ExternalInput")
    bq_d = nc.dram_tensor("bq", [128, 8], F32, kind="ExternalInput")
    bk_d = nc.dram_tensor("bk", [128, 8], F32, kind="ExternalInput")
    bv_d = nc.dram_tensor("bv", [128, 8], F32, kind="ExternalInput")
    bo_d = nc.dram_tensor("bo", [128, D], F32, kind="ExternalInput")
    fb_d = nc.dram_tensor("fb", [128, D], F32, kind="ExternalInput")
    out_d = nc.dram_tensor("out", [T, D], F32, kind="ExternalOutput")
    cc_in = nc.dram_tensor("cc_in", [1, 1], F32)
    cc_out = nc.dram_tensor("cc_out", [1, 1], F32)

    Cnp, Snp = _rope_tables(T)
    C_t = nc.inline_tensor(np.ascontiguousarray(Cnp), name="C_t")
    S_t = nc.inline_tensor(np.ascontiguousarray(Snp), name="S_t")
    id_t = nc.inline_tensor(np.eye(128, dtype=ml_dtypes.bfloat16), name="id_t")
    blkd = np.zeros((128, 33), dtype=ml_dtypes.bfloat16)
    blkd[0:64, 0] = 1.0
    blkd[64:128, 32] = 1.0
    blkd_t = nc.inline_tensor(blkd, name="blkd_t")
    blkA = np.zeros((1, 128), dtype=ml_dtypes.bfloat16)
    blkA[0, 0:64] = 1.0
    blkA_t = nc.inline_tensor(blkA, name="blkA_t")
    blkB = np.zeros((1, 128), dtype=ml_dtypes.bfloat16)
    blkB[0, 64:128] = 1.0
    blkB_t = nc.inline_tensor(blkB, name="blkB_t")
    blk64_t = nc.inline_tensor(np.ones((1, 64), ml_dtypes.bfloat16),
                               name="blk64_t")
    ones128b_t = nc.inline_tensor(np.ones((1, 128), ml_dtypes.bfloat16),
                                  name="ones128b_t")
    ones128_t = nc.inline_tensor(np.ones((1, 128), np.float32), name="ones128_t")
    c448_t = nc.inline_tensor(np.full((1, 1), FP8MAX, np.float32), name="c448_t")

    P = Plan()
    DMA_SEMS = ("sd", "sx0", "sx1", "so0", "so1", "sw0", "sw1", "gd", "gc")
    for s in ("pe", "dve", "act", "gp", "cc") + DMA_SEMS:
        P.sem(s)

    qT_o = 0
    kT_o = qT_o + 2 * 8 * T
    v_o = kT_o + 2 * 8 * T
    PT_o = v_o + 2 * NT * (NH * 65)
    wbuf_o = PT_o + 2 * 2 * KC * QB
    ln1T_o = wbuf_o + 2 * 8 * D
    AR = ln1T_o + 2 * 8 * T
    if T == 2048:
        y_o = 0
        wffq_o = 4 * 8 * T
        wroll_o = wffq_o + 2 * 8 * D
        ln2qT_o = PT_o
        oT_o = ln1T_o
        assert wroll_o + 2 * 4 * D <= PT_o
        assert 2 * 8 * T <= 2 * 2 * KC * QB
    else:
        y_o = AR
        wffq_o = y_o + 4 * 8 * T
        wroll_o = wffq_o + 2 * 8 * D
        ln2qT_o = wroll_o + 2 * 4 * D
        oT_o = ln2qT_o + 2 * 8 * T
        AR = oT_o + 2 * 8 * T

    with ExitStack() as ctx:
        e = ctx.enter_context
        arena = e(nc.sbuf_tensor("arena", [128, AR], U8))

        def view(off, nelem, dtype, nb):
            return arena[:, off:off + nelem * nb].bitcast(dtype)

        qT = view(qT_o, 8 * T, BF16, 2)
        kT = view(kT_o, 8 * T, BF16, 2)
        vtok = view(v_o, NT * NH * 65, BF16, 2)
        PT = view(PT_o, 2 * KC * QB, BF16, 2)
        wbuf = view(wbuf_o, 8 * D, BF16, 2)
        ln1T = view(ln1T_o, 8 * T, BF16, 2)
        yv = view(y_o, 8 * T, F32, 4)
        wffq = view(wffq_o, 8 * D, BF16, 2)
        wroll = [view(wroll_o, D, F32, 4), view(wroll_o + 4 * D, D, F32, 4)]
        ln2qT = view(ln2qT_o, 8 * T, BF16, 2)
        oT = view(oT_o, 8 * T, BF16, 2)

        Csb = e(nc.sbuf_tensor("Csb", [128, T], BF16))
        Ssb = e(nc.sbuf_tensor("Ssb", [128, T], BF16))
        idsb = e(nc.sbuf_tensor("idsb", [128, 128], BF16))
        blkdsb = e(nc.sbuf_tensor("blkdsb", [128, 33], BF16))
        blkAsb = e(nc.sbuf_tensor("blkAsb", [1, 128], BF16))
        blkBsb = e(nc.sbuf_tensor("blkBsb", [1, 128], BF16))
        blk64sb = e(nc.sbuf_tensor("blk64sb", [1, 64], BF16))
        ones128b = e(nc.sbuf_tensor("ones128b", [1, 128], BF16))
        bosr = e(nc.sbuf_tensor("bosr", [1, D], BF16))
        r2f = e(nc.sbuf_tensor("r2f", [1, 512], F32))
        negmr = e(nc.sbuf_tensor("negmr", [128, 1], F32))
        negm2 = e(nc.sbuf_tensor("negm2", [128, 1], F32))
        negm3 = e(nc.sbuf_tensor("negm3", [128, 1], F32))
        ones128 = e(nc.sbuf_tensor("ones128", [1, 128], F32))
        c448 = e(nc.sbuf_tensor("c448", [1, 1], F32))
        bqsb = e(nc.sbuf_tensor("bqsb", [128, 8], F32))
        bksb = e(nc.sbuf_tensor("bksb", [128, 8], F32))
        bvsb = e(nc.sbuf_tensor("bvsb", [128, 8], F32))
        if T == 2048:
            fbsb = view(oT_o, D, F32, 4)
        else:
            fbsb = e(nc.sbuf_tensor("fbsb", [128, D], F32))[:, :]
        xb = [e(nc.sbuf_tensor("xb0", [128, D], F32)),
              e(nc.sbuf_tensor("xb1", [128, D], F32))]
        lnc = e(nc.sbuf_tensor("lnc", [128, D], BF16))
        lncB = e(nc.sbuf_tensor("lncB", [128, D], BF16))
        qbh = e(nc.sbuf_tensor("qbh", [128, TH], BF16))
        qsw = e(nc.sbuf_tensor("qsw", [128, TH], BF16))
        r1t = e(nc.sbuf_tensor("r1t", [128, TH], BF16))
        r2t = e(nc.sbuf_tensor("r2t", [128, TH], BF16))
        if T == 2048:
            vb = view(PT_o, T, BF16, 2)
            vb2 = view(PT_o + 2 * T, T, BF16, 2)
            qk_sw = view(PT_o + 2 * T, T, BF16, 2)
            qk_r1 = view(PT_o + 4 * T, T, BF16, 2)
            qk_r2 = view(PT_o + 6 * T, T, BF16, 2)
            qk_b = [view(PT_o + 8 * T, T, BF16, 2),
                    view(PT_o + 10 * T, T, BF16, 2)]
        else:
            vb = e(nc.sbuf_tensor("vb", [128, T], BF16))[:, :]
            vb2 = e(nc.sbuf_tensor("vb2", [128, T], BF16))[:, :]
            qk_sw = e(nc.sbuf_tensor("qk_sw", [128, T], BF16))[:, :]
            qk_r1 = e(nc.sbuf_tensor("qk_r1", [128, T], BF16))[:, :]
            qk_r2 = e(nc.sbuf_tensor("qk_r2", [128, T], BF16))[:, :]
            qk_b = [e(nc.sbuf_tensor("qk_b0", [128, T], BF16))[:, :],
                    e(nc.sbuf_tensor("qk_b1", [128, T], BF16))[:, :]]
        stat = e(nc.sbuf_tensor("stat", [128, 12], F32))
        mv = e(nc.sbuf_tensor("mv", [128, 2], F32))
        sdt = e(nc.sbuf_tensor("sdt", [128, 1], F32))
        rstd = e(nc.sbuf_tensor("rstd", [128, 1], F32))
        m2c = e(nc.sbuf_tensor("m2c", [33, 1], F32))
        mq2 = e(nc.sbuf_tensor("mq2", [1, NH], F32))
        mk2 = e(nc.sbuf_tensor("mk2", [1, NH], F32))
        mrow = e(nc.sbuf_tensor("mrow", [1, NH], F32))
        mpair = e(nc.sbuf_tensor("mpair", [1, NH // 2], F32))
        mh_bc = e(nc.sbuf_tensor("mh_bc", [128, NH // 2], F32))
        r2a = e(nc.sbuf_tensor("r2a", [1, QB], BF16))
        r2bb = e(nc.sbuf_tensor("r2bb", [1, QB], BF16))
        rfb = e(nc.sbuf_tensor("rfb", [128, QB], BF16))
        if T == 2048:
            vq3 = Ssb.ap().bitcast(F32)
            lnb = Csb.ap().bitcast(F32)
            zq2 = view(oT_o + 4 * D, D, F32, 4)
            cq2 = view(oT_o + 8 * D, D, F32, 4)
            vq32 = view(oT_o + 12 * D, D, F32, 4)
            lq2 = view(oT_o + 16 * D, D, BF16, 2)
            ffnb2 = view(oT_o + 16 * D + 2 * D, D, BF16, 2)
            lnb2 = view(oT_o + 16 * D + 4 * D, D, F32, 4)
        else:
            vq3 = e(nc.sbuf_tensor("vq3", [128, D], F32))[:, :]
            lnb = e(nc.sbuf_tensor("lnb", [128, D], F32))[:, :]
            zq2 = e(nc.sbuf_tensor("zq2", [128, D], F32))[:, :]
            cq2 = e(nc.sbuf_tensor("cq2", [128, D], F32))[:, :]
            vq32 = e(nc.sbuf_tensor("vq32", [128, D], F32))[:, :]
            lq2 = e(nc.sbuf_tensor("lq2", [128, D], BF16))[:, :]
            ffnb2 = e(nc.sbuf_tensor("ffnb2", [128, D], BF16))[:, :]
            lnb2 = e(nc.sbuf_tensor("lnb2", [128, D], F32))[:, :]
        attn = vq3
        lncm = e(nc.sbuf_tensor("lncm", [128, NT], F32))
        lncr = e(nc.sbuf_tensor("lncr", [128, NT], F32))
        rs2 = e(nc.sbuf_tensor("rs2", [128, NT], F32))
        amx = [e(nc.sbuf_tensor("amx0", [128, 1], F32)),
               e(nc.sbuf_tensor("amx1", [128, 1], F32))]
        wmx = [e(nc.sbuf_tensor("wmx0", [128, 1], F32)),
               e(nc.sbuf_tensor("wmx1", [128, 1], F32))]
        u1 = e(nc.sbuf_tensor("u1", [128, 1], F32))
        u2 = e(nc.sbuf_tensor("u2", [128, 1], F32))
        gu1 = e(nc.sbuf_tensor("gu1", [128, 1], F32))
        gu2 = e(nc.sbuf_tensor("gu2", [128, 1], F32))
        gtrt = e(nc.sbuf_tensor("gtrt", [128, 1], F32))
        trt = e(nc.sbuf_tensor("trt", [128, 1], F32))
        pad32 = e(nc.sbuf_tensor("pad32", [128, 32], F32))
        tr32 = e(nc.sbuf_tensor("tr32", [128, 32], F32))
        rowv = e(nc.sbuf_tensor("rowv", [1, 4], F32))
        zq = xb[0]
        cq = xb[1]
        lq = e(nc.sbuf_tensor("lq", [128, D], BF16))
        ffnb = e(nc.sbuf_tensor("ffnb", [128, D], BF16))
        if T == 2048:
            outb = [view(wbuf_o, D, F32, 4), view(wbuf_o + 4 * D, D, F32, 4)]
        else:
            outb = [e(nc.sbuf_tensor("outb0", [128, D], F32))[:, :],
                    e(nc.sbuf_tensor("outb1", [128, D], F32))[:, :]]
        scl = e(nc.sbuf_tensor("scl", [1, 8], F32))
        wscl = e(nc.sbuf_tensor("wscl", [1, 8], F32))
        dqt = e(nc.sbuf_tensor("dqt", [1, 1], F32))
        ccs = e(nc.sbuf_tensor("ccs", [1, 1], F32))
        epsb = e(nc.sbuf_tensor("epsb", [128, 1], F32))
        xs_bc = e(nc.sbuf_tensor("xs_bc", [128, 1], F32))
        ws_bc = e(nc.sbuf_tensor("ws_bc", [128, 1], F32))
        dq_bc = e(nc.sbuf_tensor("dq_bc", [128, 1], F32))
        zerob = e(nc.sbuf_tensor("zerob", [128, 1], F32))
        lnstT = [e(nc.sbuf_tensor("lnstT0", [128, 8], F32)),
                 e(nc.sbuf_tensor("lnstT1", [128, 8], F32)),
                 e(nc.sbuf_tensor("lnstT2", [128, 8], F32))]
        lnstS = [e(nc.sbuf_tensor("lnstS0", [128, 8], F32)),
                 e(nc.sbuf_tensor("lnstS1", [128, 8], F32))]
        lnstD = [e(nc.sbuf_tensor("lnstD0", [128, 8], F32)),
                 e(nc.sbuf_tensor("lnstD1", [128, 8], F32))]
        stP = [e(nc.sbuf_tensor("stP0", [128, 17], F32)),
               e(nc.sbuf_tensor("stP1", [128, 17], F32))]
        gu1b = e(nc.sbuf_tensor("gu1b", [128, 1], F32))
        gu2b = e(nc.sbuf_tensor("gu2b", [128, 1], F32))
        gtrtb = e(nc.sbuf_tensor("gtrtb", [128, 1], F32))
        if T == 2048:
            zqW = view(PT_o, D, F32, 4)
            cqW = view(PT_o + 4 * D, D, F32, 4)
            vqW = view(PT_o + 8 * D, D, F32, 4)
        else:
            zqW = e(nc.sbuf_tensor("zqW", [128, D], F32))[:, :]
            cqW = e(nc.sbuf_tensor("cqW", [128, D], F32))[:, :]
            vqW = e(nc.sbuf_tensor("vqW", [128, D], F32))[:, :]
        if T == 2048:
            scT = [view(PT_o + 12 * 1024, D, BF16, 2),
                   view(PT_o + 12 * 1024 + 2 * D, D, BF16, 2),
                   view(PT_o + 12 * 1024 + 8 * D, D, BF16, 2)]
            scS = [view(PT_o + 12 * 1024 + 4 * D, D, BF16, 2),
                   view(PT_o + 12 * 1024 + 6 * D, D, BF16, 2)]
        else:
            scT = [e(nc.sbuf_tensor("scT0", [128, D], BF16))[:, :],
                   e(nc.sbuf_tensor("scT1", [128, D], BF16))[:, :],
                   e(nc.sbuf_tensor("scT2", [128, D], BF16))[:, :]]
            scS = [e(nc.sbuf_tensor("scS0", [128, D], BF16))[:, :],
                   e(nc.sbuf_tensor("scS1", [128, D], BF16))[:, :]]

        ps32a = e(nc.psum_tensor("ps32a", [128, 2048], F32))
        ps32b = e(nc.psum_tensor("ps32b", [128, 2048], F32))
        psbf = ps32b[:, 0:512].bitcast(BF16)

        sems = {}
        for s in ("pe", "dve", "act", "gp", "cc") + DMA_SEMS:
            sems[s] = e(nc.semaphore(s))

        def dma(eng, o, i, waits=(), q="sd"):
            pre = (q, P.cnt[q]) if P.cnt[q] > 0 else None
            return P.op(eng, lambda en, o=o, i=i: en.dma_start(out=o, in_=i),
                        waits=list(waits) + [pre], inc=q, amt=16)

        def dve(fn, waits=(), inc="dve"):
            return P.op("vector", fn, waits=waits, inc=inc)

        def act(fn, waits=(), inc="act"):
            return P.op("scalar", fn, waits=waits, inc=inc)

        def pe(fn, waits=(), inc="pe"):
            return P.op("tensor", fn, waits=waits, inc=inc)

        def pe_noinc(fn, waits=()):
            return P.op("tensor", fn, waits=waits, inc=None)

        def gp(fn, waits=(), inc="gp"):
            return P.op("gpsimd", fn, waits=waits, inc=inc)

        tok_eps = P.op("vector", lambda v: v.memset(epsb[:, :], EPS), inc="dve")
        t_consts = []
        for dst, src in ((Csb, C_t), (Ssb, S_t), (idsb, id_t), (blkdsb, blkd_t),
                         (blkAsb, blkA_t), (blkBsb, blkB_t), (blk64sb, blk64_t),
                         (ones128b, ones128b_t),
                         (ones128, ones128_t), (c448, c448_t),
                         (bqsb, bq_d), (bksb, bk_d), (bvsb, bv_d)):
            t_consts.append(dma("sync", dst[:, :], src[:, :]))
        tok_consts = t_consts[-1]
        tok_bosr = dma("gpsimd", bosr[0:1, :], bo_d[0:1, :], q="gd")

        ln_norm_tok = [None, None]
        lnc_war = [None, None]
        ev1_tok = None
        ln1T_toks = []
        for tc in range(NT):
            par = tc % 2
            S = stP[par]
            lncP = lnc if par == 0 else lncB
            tx = dma("sync", xb[par][:, :], x_d[128 * tc:128 * (tc + 1), :],
                     waits=[ln_norm_tok[par]], q="sx%d" % par)
            t1 = dve(lambda v, b=xb[par], S=S: v.bn_stats(S[:, 0:6],
                                                          b[:, 0:512]),
                     waits=[tx])
            t2 = dve(lambda v, b=xb[par], S=S: v.bn_stats(S[:, 6:12],
                                                          b[:, 512:1024]),
                     waits=[tx])
            t3 = dve(lambda v, S=S: v.bn_aggr(S[:, 12:14], S[:, 0:12]),
                     waits=[t1, t2])
            t4 = act(lambda s, S=S: s.activation(S[:, 14:15], S[:, 13:14],
                                                 AF.Sqrt, bias=epsb[:, 0:1]),
                     waits=[t3, tok_eps])
            t5 = dve(lambda v, S=S: v.reciprocal(S[:, 15:16], S[:, 14:15]),
                     waits=[t4])
            t5b = dve(lambda v, S=S: v.tensor_scalar(
                S[:, 16:17], S[:, 12:13], S[:, 15:16], -1.0,
                ALU.mult, ALU.mult), waits=[t5])
            t6 = act(lambda s, b=xb[par], S=S, lncP=lncP: s.activation(
                lncP[:, :], b[:, :], AF.Identity, bias=S[:, 16:17],
                scale=S[:, 15:16]), waits=[t5b, lnc_war[par]])
            ln_norm_tok[par] = t6
            ttr = None
            for j in range(8):
                ttr = pe(lambda t, j=j, lncP=lncP: t.transpose(
                    psbf[:, 128 * j:128 * (j + 1)],
                    lncP[:, 128 * j:128 * (j + 1)],
                    idsb[:, :]), waits=[t6, ev1_tok] if j == 0 else [])
            lnc_war[par] = ttr
            ot = ln1T.rearrange("p (j t) -> p j t", j=8)[:, :, 128 * tc:128 * (tc + 1)]
            it = psbf.rearrange("p (j t) -> p j t", j=8)
            ev1_tok = dve(lambda v, ot=ot, it=it: v.tensor_copy(ot, it), waits=[ttr])
            ln1T_toks.append(ev1_tok)
        tok_ln1T = ln1T_toks[-1]

        def vtok_dst(j0, jn, oc, b):
            a = vtok.rearrange("p (tt x) -> p tt x", tt=NT)[
                :, j0:j0 + jn, 130 * oc:130 * oc + 130]
            a = a.rearrange("p tt (h i) -> p tt h i", h=2)
            return a[:, :, :, 32 * b:32 * b + 32]

        def vtok_src(jn, b):
            a = psbf.rearrange("p (t hh j b) -> p t hh j b",
                                    t=8, hh=2, j=32, b=2)
            return a[:, 0:jn, :, :, b]

        proj_state = {"evac": [None, None], "vtr": None,
                      "vb_war": [None, None], "pend_tr": None,
                      "e3": [None, None], "e4": None, "e5": None, "half": 0}

        def emit_proj(w_d, b_sb, dest_kind, wbuf_war):
            wtok = dma("gpsimd", wbuf[:, :], w_d[:, :], waits=[wbuf_war], q="gd")
            evac_tok = proj_state["evac"]
            write_toks = []
            mm = None
            for oc in range(NIC):
                oc_e5 = []
                for h2 in range(NHALF):
                    m = oc * NHALF + h2
                    pv = ps32a[:, (m % 2) * 1024:(m % 2) * 1024 + TH]
                    tcb = h2 * TH
                    for n in range((TH + 511) // 512):
                        nw = min(512, TH - 512 * n)
                        for ic in range(NIC):
                            w0 = []
                            if oc == 0 and h2 == 0 and n == 0 and ic == 0:
                                w0 = [wtok, tok_ln1T]
                            if n == 0 and ic == 0:
                                w0 = w0 + [evac_tok[m % 2]]
                            mm = P.op("tensor", lambda t, pv=pv, n=n, nw=nw, ic=ic,
                                      oc=oc, tcb=tcb: t.matmul(
                                          pv[:, 512 * n:512 * n + nw],
                                          wbuf[:, 1024 * ic + 128 * oc:1024 * ic + 128 * (oc + 1)],
                                          ln1T[:, T * ic + tcb + 512 * n:T * ic + tcb + 512 * n + nw],
                                          start=(ic == 0), stop=(ic == NIC - 1)),
                                      waits=w0,
                                      inc="pe" if ic == NIC - 1 else None)
                    hpar = proj_state["half"] % 2
                    proj_state["half"] += 1
                    qb_t = qbh if hpar == 0 else lq
                    e1 = act(lambda s, pv=pv, b=b_sb, oc=oc, qb_t=qb_t:
                             s.activation(
                                 qb_t[:, 0:TH], pv, AF.Identity,
                                 bias=b[:, oc:oc + 1]),
                             waits=[mm, proj_state["e3"][hpar]])
                    evac_tok[m % 2] = e1
                    esw = None
                    for si, (do, di) in enumerate(
                            ((0, 32), (32, 0), (64, 96), (96, 64))):
                        esw = dve(lambda v, do=do, di=di, qb_t=qb_t: v.tensor_copy(
                            qsw[do:do + 32, 0:TH], qb_t[di:di + 32, 0:TH]),
                            waits=[e1] + ([proj_state["e4"]] if si == 0 else []))
                    e3 = dve(lambda v, tcb=tcb, qb_t=qb_t: v.tensor_mul(
                        r1t[:, 0:TH], qb_t[:, 0:TH], Csb[:, tcb:tcb + TH]),
                        waits=[e1, proj_state["e5"]])
                    proj_state["e3"][hpar] = e3
                    e4 = dve(lambda v, tcb=tcb: v.tensor_mul(
                        r2t[:, 0:TH], qsw[:, 0:TH], Ssb[:, tcb:tcb + TH]),
                        waits=[esw])
                    proj_state["e4"] = e4
                    if dest_kind == "q":
                        dst = qT[:, T * oc + tcb:T * oc + tcb + TH]
                    elif dest_kind == "k":
                        dst = kT[:, T * oc + tcb:T * oc + tcb + TH]
                    else:
                        dst = (vb if oc % 2 == 0 else vb2)[:, tcb:tcb + TH]
                    e5 = dve(lambda v, dst=dst: v.tensor_add(
                        dst, r1t[:, 0:TH], r2t[:, 0:TH]),
                        waits=[e3, e4] + ([proj_state["vb_war"][oc % 2]]
                                          if dest_kind == "v" and h2 == 0 else []))
                    proj_state["e5"] = e5
                    oc_e5.append(e5)
                    write_toks.append(e5)
                    if dest_kind == "v" and h2 == 0 and proj_state["pend_tr"]:
                        proj_state["pend_tr"]()
                        proj_state["pend_tr"] = None
                if dest_kind == "v":
                    def do_tr(oc=oc, e5_tok=oc_e5[-1]):
                        vbp = vb if oc % 2 == 0 else vb2
                        ntile = T // 128
                        ttr_last = None
                        for batch in range((ntile + 7) // 8):
                            j0 = batch * 8
                            jn = min(8, ntile - j0)
                            ttr = None
                            for j in range(j0, j0 + jn):
                                ttr = pe(lambda t, j=j, vbp=vbp: t.transpose(
                                    psbf[:, 128 * (j % 8):128 * (j % 8) + 128],
                                    vbp[:, 128 * j:128 * j + 128], idsb[:, :]),
                                    waits=([e5_tok, proj_state["vtr"]]
                                           if j == j0 else []))
                            for b in range(2):
                                proj_state["vtr"] = dve(
                                    lambda v, b=b, j0=j0, jn=jn, oc=oc:
                                    v.tensor_copy(vtok_dst(j0, jn, oc, b),
                                                  vtok_src(jn, b)),
                                    waits=[ttr] if b == 0 else [])
                            ttr_last = ttr
                            write_toks.append(proj_state["vtr"])
                        proj_state["vb_war"][oc % 2] = ttr_last
                    proj_state["pend_tr"] = do_tr
            if dest_kind == "v" and proj_state["pend_tr"]:
                proj_state["pend_tr"]()
                proj_state["pend_tr"] = None
            return mm, write_toks, wtok

        vmm, v_toks, _ = emit_proj(wv_d, bvsb, "v", tok_consts)

        NW = (T + 511) // 512
        qk_state = {"evac": [None, None], "e3": [None, None], "e5": None,
                    "e4": None, "oc": 0}

        def emit_proj_fc(w_d, b_sb, dest, wbuf_war):
            wtok = dma("gpsimd", wbuf[:, :], w_d[:, :], waits=[wbuf_war], q="gd")
            last_mm = None
            toks = []
            for oc in range(NIC):
                par = qk_state["oc"] % 2
                qk_state["oc"] += 1
                pv = ps32a[:, 0:T] if par == 0 else ps32b[:, 0:T]
                mm = None
                for n in range(NW):
                    nw = min(512, T - 512 * n)
                    for ic in range(NIC):
                        w0 = []
                        if n == 0 and ic == 0:
                            w0 = [qk_state["evac"][par], wtok]
                            if qk_state["oc"] <= 2:
                                w0 += [v_toks[-1], proj_state["vtr"], tok_ln1T]
                        mm = P.op("tensor", lambda t, pv=pv, n=n, nw=nw, ic=ic,
                                  oc=oc: t.matmul(
                                      pv[:, 512 * n:512 * n + nw],
                                      wbuf[:, 1024 * ic + 128 * oc:1024 * ic + 128 * (oc + 1)],
                                      ln1T[:, T * ic + 512 * n:T * ic + 512 * n + nw],
                                      start=(ic == 0), stop=(ic == NIC - 1)),
                                  waits=w0,
                                  inc="pe" if ic == NIC - 1 else None)
                qb_t = qk_b[par]
                e1 = act(lambda s, pv=pv, b=b_sb, oc=oc, qb_t=qb_t: s.activation(
                    qb_t[:, :], pv, AF.Identity, bias=b[:, oc:oc + 1]),
                    waits=[mm, qk_state["e3"][par]])
                qk_state["evac"][par] = e1
                esw = None
                for si, (do, di) in enumerate(
                        ((0, 32), (32, 0), (64, 96), (96, 64))):
                    esw = dve(lambda v, do=do, di=di, qb_t=qb_t: v.tensor_copy(
                        qk_sw[do:do + 32, :], qb_t[di:di + 32, :]),
                        waits=[e1] + ([qk_state["e4"]] if si == 0 else []))
                e3 = dve(lambda v, qb_t=qb_t: v.tensor_mul(
                    qk_r1[:, :], qb_t[:, :], Csb[:, :]),
                    waits=[e1, qk_state["e5"]])
                qk_state["e3"][par] = e3
                e4 = dve(lambda v: v.tensor_mul(
                    qk_r2[:, :], qk_sw[:, :], Ssb[:, :]), waits=[esw])
                qk_state["e4"] = e4
                dst = (qT if dest == "q" else kT)[:, T * oc:T * (oc + 1)]
                e5 = dve(lambda v, dst=dst: v.tensor_add(
                    dst, qk_r1[:, :], qk_r2[:, :]), waits=[e3, e4])
                qk_state["e5"] = e5
                toks.append(e5)
            return last_mm if last_mm else mm, toks, wtok

        USE_FC = False
        if USE_FC:
            qmm, q_toks, _ = emit_proj_fc(wq_d, bqsb, "q", vmm)
            kmm, k_toks, _ = emit_proj_fc(wk_d, bksb, "k", qmm)
        else:
            qmm, q_toks, _ = emit_proj(wq_d, bqsb, "q", vmm)
            kmm, k_toks, _ = emit_proj(wk_d, bksb, "k", qmm)
        tok_vones = dve(lambda v: v.memset(
            vtok.rearrange("p (tt h i) -> p tt h i", tt=NT, h=NH)[:, :, :, 64:65],
            -1.0), waits=[])
        tok_wo = dma("gpsimd", wbuf[:, :], wo_d[:, :], waits=[kmm], q="gd")

        tok_q_all = q_toks[-1]
        tok_k_all = k_toks[-1]
        vb_war = proj_state["vb_war"][0]
        stat_war = proj_state["evac"][0]
        last_stat_red = None
        mrow_tok = None
        for (src, dstrow, wtok_all) in ((qT, mq2, tok_q_all), (kT, mk2, tok_k_all)):
            for oc in range(NIC):
                tsq = act(lambda s, src=src, oc=oc: s.activation(
                    vb[:, 0:T], src[:, T * oc:T * (oc + 1)], AF.Square),
                    waits=[wtok_all, v_toks[-1], vb_war])
                mmq = None
                for n in range((T + 511) // 512):
                    nw = min(512, T - 512 * n)
                    mmq = pe(lambda t, n=n, nw=nw: t.matmul(
                        ps32a[0:33, 512 * n:512 * n + nw],
                        blkdsb[:, :], vb[:, 512 * n:512 * n + nw],
                        start=True, stop=True),
                        waits=[tsq] + (([qk_state["evac"][0],
                                         qk_state["evac"][1],
                                         proj_state["evac"][0],
                                         proj_state["evac"][1]])
                                       if last_stat_red is None
                                       else [last_stat_red]))
                vb_war = mmq
                last_stat_red = dve(lambda v: v.tensor_reduce(
                    m2c[:, :], ps32a[0:33, 0:T], AX.X, ALU.max),
                    waits=[mmq, mrow_tok])
                mrow_tok = dve(lambda v, dstrow=dstrow, oc=oc: v.tensor_copy(
                    dstrow[0:1, 2 * oc:2 * oc + 1], m2c[0:1, 0:1]),
                    waits=[last_stat_red])
                mrow_tok = dve(lambda v, dstrow=dstrow, oc=oc: v.tensor_copy(
                    dstrow[0:1, 2 * oc + 1:2 * oc + 2], m2c[32:33, 0:1]),
                    waits=[last_stat_red])
        tprod = dve(lambda v: v.tensor_mul(mrow[:, :], mq2[:, :], mk2[:, :]),
                    waits=[mrow_tok])
        tsqr = act(lambda s: s.activation(mq2[:, :], mrow[:, :], AF.Sqrt),
                   waits=[tprod])
        tneg = dve(lambda v: v.tensor_scalar(
            mrow[:, :], mq2[:, :], -0.125, -1.0, ALU.mult, ALU.add), waits=[tsqr])
        tpair = dve(lambda v: v.tensor_tensor(
            mpair[:, :], mrow[0:1, 0:NH:2], mrow[0:1, 1:NH:2], ALU.min),
            waits=[tneg])
        tmh_mm = pe(lambda t: t.matmul(
            ps32a[0:128, 0:8], ones128[:, :], mpair[:, :], start=True, stop=True),
            waits=[tpair, last_stat_red])
        tok_mh = dve(lambda v: v.tensor_copy(mh_bc[:, :], ps32a[0:128, 0:8]),
                     waits=[tmh_mm])

        S2 = [ps32a[:, 0:S2W], ps32a[:, 1024:1024 + S2W]]
        av_off = [0, 512, 1024, 1536]
        o_last = None
        rope_hist = {-1: [tok_mh], -2: [tok_mh]}
        avb_prev = {}
        s2_free = {0: tok_mh, 1: tok_mh}
        last_exp = tok_mh
        prev_rfm = None
        prev_ow = None
        pending = None
        gi = 0

        def emit_epilogue_b(p):
            nonlocal prev_rfm, prev_ow, o_last
            offA, offB = p["offA"], p["offB"]
            pe_noinc(lambda t, offA=offA: t.matmul(
                ps32b[0:64, offA:offA + QB], blk64sb[:, :], r2a[0:1, 0:QB],
                start=True, stop=True),
                waits=[p["rc1"], p["rc2"], p["evA"], p["evB"]])
            rfm = pe(lambda t, offA=offA: t.matmul(
                ps32b[64:128, offA:offA + QB], blk64sb[:, :], r2bb[0:1, 0:QB],
                start=True, stop=True, tile_position=(0, 64)), waits=[])
            prev_rfm = rfm
            erf = dve(lambda v, offA=offA: v.tensor_copy(
                rfb[:, 0:QB], ps32b[0:128, offA:offA + QB]),
                waits=[rfm, prev_ow])
            esw = None
            for (do, di) in ((0, 32), (32, 0), (64, 96), (96, 64)):
                esw = dve(lambda v, do=do, di=di: v.tensor_copy(
                    qsw[do:do + 32, 0:QB], qbh[di:di + 32, 0:QB]),
                    waits=[p["evA"], p["evB"]])
            m1 = dve(lambda v, qc0=p["qc0"]: v.tensor_mul(
                r1t[:, 0:QB], qbh[:, 0:QB], Csb[:, qc0:qc0 + QB]),
                waits=[p["evA"], p["evB"]])
            m2 = dve(lambda v, qc0=p["qc0"]: v.tensor_mul(
                r2t[:, 0:QB], qsw[:, 0:QB], Ssb[:, qc0:qc0 + QB]),
                waits=[esw])
            m3 = dve(lambda v: v.tensor_add(qsw[:, 0:QB], r1t[:, 0:QB],
                                            r2t[:, 0:QB]), waits=[m1, m2])
            ow = dve(lambda v, hp=p["hp"], qc0=p["qc0"]: v.tensor_mul(
                oT[:, T * hp + qc0:T * hp + qc0 + QB], qsw[:, 0:QB],
                rfb[:, 0:QB]), waits=[m3, erf])
            rope_hist[p["gi"]] = [p["evA"], p["evB"], p["rc1"], p["rc2"],
                                  ow, erf]
            prev_ow = ow
            o_last = ow

        for qb in range(NQ):
            qc0 = QB * qb
            for hp in range(NH // 2):
                par = gi % 2
                offA = av_off[2 * par]
                offB = av_off[2 * par + 1]
                avA = ps32b[0:65, offA:offA + QB]
                avB = ps32b[0:65, offB:offB + QB]
                new_exp = {}
                new_avb = {}
                av_args = None
                for kc in range(KC):
                    s2 = S2[kc % 2]
                    w0 = [s2_free[kc % 2]]
                    if gi == 0 and kc == 0:
                        w0 += [tok_mh, tok_q_all, tok_k_all]
                    pe_noinc(lambda t, s2=s2, hp=hp, kc=kc, qc0=qc0: t.matmul(
                        s2[:, 0:QB],
                        kT[0:64, T * hp + 128 * kc:T * hp + 128 * (kc + 1)],
                        qT[0:64, T * hp + qc0:T * hp + qc0 + QB],
                        start=True, stop=True, tile_position=(0, 0)),
                        waits=w0)
                    smm = pe(lambda t, s2=s2, hp=hp, kc=kc, qc0=qc0: t.matmul(
                        s2[:, QB:2 * QB],
                        kT[64:128, T * hp + 128 * kc:T * hp + 128 * (kc + 1)],
                        qT[64:128, T * hp + qc0:T * hp + qc0 + QB],
                        start=True, stop=True, tile_position=(64, 0)),
                        waits=[])
                    pt_out = PT.rearrange("p (h x) -> p h x", h=2)[
                        :, :, QB * kc:QB * (kc + 1)]
                    s2r = s2.rearrange("p (h x) -> p h x", h=2)
                    et = act(lambda s, pt_out=pt_out, s2r=s2r, hp=hp: s.activation(
                        pt_out, s2r, AF.Exp, bias=mh_bc[:, hp:hp + 1], scale=0.125),
                        waits=[smm, avb_prev.get(kc)])
                    new_exp[kc] = et
                    s2_free[kc % 2] = et

                    if kc == min(6, KC - 1) and pending is not None:
                        emit_epilogue_b(pending)
                        pending = None

                    def emit_av(kav):
                        avw = [new_exp[kav]]
                        if kav == 0:
                            avw = avw + rope_hist[gi - 2]
                        pe_noinc(lambda t, kav=kav, hp=hp, avA=avA: t.matmul(
                            avA, vtok[:, (65 * NH) * kav + 65 * (2 * hp):(65 * NH) * kav + 65 * (2 * hp) + 65],
                            PT[:, QB * kav:QB * (kav + 1)],
                            start=(kav == 0), stop=(kav == KC - 1)), waits=avw)
                        return P.op("tensor", lambda t, kav=kav, hp=hp, avB=avB:
                                    t.matmul(
                                        avB, vtok[:, (65 * NH) * kav + 65 * (2 * hp + 1):(65 * NH) * kav + 65 * (2 * hp + 1) + 65],
                                        PT[:, KC * QB + QB * kav:KC * QB + QB * (kav + 1)],
                                        start=(kav == 0), stop=(kav == KC - 1)),
                                    waits=[], inc="pe")
                    if kc >= 1:
                        new_avb[kc - 1] = emit_av(kc - 1)
                new_avb[KC - 1] = emit_av(KC - 1)
                av_tokB = new_avb[KC - 1]
                avb_prev = new_avb
                last_exp = new_exp[KC - 1]
                evA = dve(lambda v, avA=avA: v.tensor_copy(
                    qbh[0:64, 0:QB], avA[0:64, 0:QB]),
                    waits=[av_tokB, prev_rfm, prev_ow])
                evB = dve(lambda v, avB=avB: v.tensor_copy(
                    qbh[64:128, 0:QB], avB[0:64, 0:QB]), waits=[av_tokB])
                rc1 = dve(lambda v, avA=avA: v.reciprocal(
                    r2a[0:1, 0:QB], avA[64:65, 0:QB]), waits=[av_tokB])
                rc2 = dve(lambda v, avB=avB: v.reciprocal(
                    r2bb[0:1, 0:QB], avB[64:65, 0:QB]), waits=[av_tokB])
                pending = {"gi": gi, "hp": hp, "qc0": qc0, "offA": offA,
                           "offB": offB, "rc1": rc1, "rc2": rc2,
                           "evA": evA, "evB": evB}
                gi += 1
        emit_epilogue_b(pending)
        pending = None
        prev_rf_evac = None

        tok_pad = dve(lambda v: v.memset(pad32[:, :], 0.0), waits=[])
        tok_zerob = dve(lambda v: v.memset(zerob[:, :], 0.0), waits=[])

        def maxtree(src, toks):
            tok = dve(lambda v, src=src: v.tensor_copy(pad32[:, 0:1], src[:, 0:1]),
                      waits=list(toks) + [tok_pad])
            tok = dve(lambda v: v.transpose(tr32[:, :], pad32[:, :]), waits=[tok])
            for j, row in enumerate((0, 32, 64, 96)):
                tok = dve(lambda v, j=j, row=row: v.reduce_max(
                    rowv[0:1, j:j + 1], tr32[row:row + 1, 0:32], AX.X),
                    waits=[tok])
            tok = dve(lambda v: v.reduce_max(trt[0:1, 0:1], rowv[0:1, 0:4], AX.X),
                      waits=[tok])
            return tok

        RD = 1.0 / D

        def stats_math(C, pA, pB, mean_out=None):
            mo = mean_out if mean_out is not None else C[:, 2:3]
            m1 = dve(lambda v, C=C, mo=mo: v.tensor_scalar_mul(
                mo, C[:, 0:1], RD), waits=[pA])
            m2 = dve(lambda v, C=C, mo=mo: v.tensor_mul(
                C[:, 4:5], mo, mo), waits=[m1])
            m3 = dve(lambda v, C=C: v.tensor_scalar_mul(
                C[:, 3:4], C[:, 1:2], RD), waits=[pB])
            m4 = dve(lambda v, C=C: v.tensor_sub(
                C[:, 3:4], C[:, 3:4], C[:, 4:5]), waits=[m2, m3])
            return m1, m4

        def velt(dst, srcf32, waits, cqx=None, vq3x=None):
            cqa = cqx if cqx is not None else cq.ap()[:, :]
            vqa = vq3x if vq3x is not None else vq3
            t1 = dve(lambda v, s=srcf32, cqa=cqa: v.tensor_single_scalar(
                cqa.bitcast(I32), s.bitcast(I32), 0x7F800000,
                ALU.bitwise_and), waits=waits)
            t2 = dve(lambda v, cqa=cqa, vqa=vqa: v.tensor_scalar(
                vqa, cqa, VELT_K, VELT_MIN, ALU.mult, ALU.max),
                waits=[t1])
            t3 = dve(lambda v, s=srcf32, cqa=cqa, vqa=vqa: v.tensor_tensor(
                cqa, s, vqa, ALU.add), waits=[t2])
            t4 = dve(lambda v, dst=dst, cqa=cqa, vqa=vqa: v.tensor_tensor(
                dst, cqa, vqa, ALU.subtract), waits=[t3])
            return t4

        wmx_tok = None
        wroll_war = [None, None]
        velt_war = None
        wq_toks = []
        tokw = None
        tok_ws = None
        tok_wsi = None
        twsb = None
        bc_mm2 = None

        def emit_wstep(i):
            nonlocal wmx_tok, velt_war, tokw, tok_ws, tok_wsi, twsb, bc_mm2
            if i < NIC:
                wc = i
                wpar = wc % 2
                tw = dma("sync", wroll[wpar][:, :],
                         wf_d[:, 1024 * wc:1024 * (wc + 1)],
                         waits=[wroll_war[wpar]] + ([k_toks[-1], o_last,
                                                    v_toks[-1]]
                                                   if wc <= 1 else []),
                         q="sw%d" % wpar)
                rw = dve(lambda v, wpar=wpar: v.reduce_max(
                    u1[:, :], wroll[wpar][:, :], AX.X,
                    apply_absolute_value=True),
                    waits=[tw, wmx_tok])
                wroll_war[wpar] = rw
                if wc == 0:
                    wmx_tok = dve(lambda v: v.tensor_copy(wmx[0][:, :],
                                                          u1[:, :]),
                                  waits=[rw])
                else:
                    wmx_tok = dve(lambda v, wc=wc: v.tensor_max(
                        wmx[wc % 2][:, :], wmx[1 - wc % 2][:, :], u1[:, :]),
                        waits=[rw, wmx_tok])
            else:
                wc = i - NIC
                wpar = wc % 2
                if wc == 0:
                    wmx_fin = wmx[(NIC - 1) % 2]
                    tokw = maxtree(wmx_fin, [wmx_tok])
                    tokw = dve(lambda v: v.tensor_scalar_max(
                        wscl[0:1, 1:2], trt[0:1, 0:1], 1e-12), waits=[tokw])
                    tokw = dve(lambda v: v.reciprocal(wscl[0:1, 5:6],
                                                      wscl[0:1, 1:2]),
                               waits=[tokw])
                    tokw = dve(lambda v: v.tensor_scalar_mul(
                        wscl[0:1, 2:3], wscl[0:1, 5:6], FP8MAX), waits=[tokw])
                    tok_ws = dve(lambda v: v.tensor_scalar_min(
                        wscl[0:1, 3:4], wscl[0:1, 2:3], FP8MAX), waits=[tokw])
                    tok_wsi = dve(lambda v: v.reciprocal(wscl[0:1, 4:5],
                                                         wscl[0:1, 3:4]),
                                  waits=[tok_ws])
                    bc_mm2 = pe(lambda t: t.matmul(
                        ps32b[0:128, 1540:1541], ones128[:, :], wscl[0:1, 3:4],
                        start=True, stop=True), waits=[tok_ws, o_last])
                    twsb = dve(lambda v: v.tensor_copy(ws_bc[:, :],
                                                       ps32b[0:128, 1540:1541]),
                               waits=[bc_mm2])
                tw = dma("sync", wroll[wpar][:, :],
                         wf_d[:, 1024 * wc:1024 * (wc + 1)],
                         waits=[wroll_war[wpar]], q="sw%d" % wpar)
                tz = dve(lambda v, wpar=wpar: v.tensor_scalar_mul(
                    zqW, wroll[wpar][:, :], ws_bc[:, 0:1]),
                    waits=[tw, twsb, velt_war])
                wroll_war[wpar] = tz
                tq = velt(wffq[:, 1024 * wc:1024 * (wc + 1)], zqW, [tz],
                          cqx=cqW, vq3x=vqW)
                velt_war = tq
                wq_toks.append(tq)

        y_toks = []
        prev_q3 = None
        oproj_evac = [None, None, None]
        amx_tok = None
        sA_war = [None, None]
        gu_war = [None, None]
        out_war = [k_toks[-1], o_last]
        x2_tok = [ln_norm_tok[0], ln_norm_tok[1]]
        p4s = {"s5": None, "mm": None}

        pvs3 = [ps32a[:, 0:1024], ps32a[:, 1024:2048], ps32b[:, 0:1024]]

        def emit_p4_head(tc):
            par = tc % 2
            p3i = tc % 3
            tx = dma("sync", xb[par][:, :], x_d[128 * tc:128 * (tc + 1), :],
                     waits=[x2_tok[par]], q="sx%d" % par)
            pv = pvs3[p3i]
            mm = None
            for n in range(2):
                for dc in range(NIC):
                    w0 = []
                    if tc == 0 and n == 0 and dc == 0:
                        w0 = [last_exp, tok_wo, o_last, prev_rf_evac]
                    if tc == 2 and n == 0 and dc == 0:
                        w0 = [o_last]
                    if n == 0 and dc == 0:
                        w0 = w0 + [oproj_evac[p3i]]
                    mm = P.op("tensor", lambda t, pv=pv, n=n, dc=dc, tc=tc: t.matmul(
                        pv[:, 512 * n:512 * n + 512],
                        oT[:, T * dc + 128 * tc:T * dc + 128 * (tc + 1)],
                        wbuf[:, 1024 * dc + 512 * n:1024 * dc + 512 * n + 512],
                        start=(dc == 0), stop=False),
                        waits=w0, inc=None)
                mm = P.op("tensor", lambda t, pv=pv, n=n: t.matmul(
                    pv[:, 512 * n:512 * n + 512], ones128b[:, :],
                    bosr[0:1, 512 * n:512 * n + 512],
                    start=False, stop=True),
                    waits=[tok_bosr] if tc == 0 and n == 0 else [],
                    inc="pe")
            p4s["mm"] = mm
            C = lnstT[p3i]
            pA = act(lambda s, pv=pv, C=C, p3i=p3i: s.activation(
                scT[p3i], pv, AF.Identity,
                bias=zerob[:, 0:1], accum_out=C[:, 0:1]),
                waits=[mm, tok_zerob] + ([o_last] if tc <= 1 else []))
            pB = act(lambda s, pv=pv, C=C, p3i=p3i: s.activation(
                scT[p3i], pv, AF.Square,
                accum_out=C[:, 1:2]), waits=[pA])
            _, m4 = stats_math(C, pA, pB)
            t4 = act(lambda s, C=C: s.activation(C[:, 5:6], C[:, 3:4], AF.Sqrt,
                                                 bias=epsb[:, 0:1]),
                     waits=[m4])
            t5 = dve(lambda v, C=C: v.reciprocal(C[:, 6:7], C[:, 5:6]),
                     waits=[t4])
            t5b = dve(lambda v, C=C: v.tensor_scalar(
                C[:, 7:8], C[:, 2:3], C[:, 6:7], -1.0,
                ALU.mult, ALU.mult), waits=[t5])
            t6 = act(lambda s, pv=pv, C=C: s.activation(
                lnb[:, :], pv, AF.Identity, bias=C[:, 7:8],
                scale=C[:, 6:7]), waits=[t5b])
            oproj_evac[p3i] = t6
            ty = dve(lambda v, tc=tc, par=par: v.tensor_add(
                yv[:, D * tc:D * (tc + 1)], xb[par][:, :], lnb[:, :]),
                waits=[t6, tx] + (out_war if tc <= 1 else []))
            x2_tok[par] = ty
            y_toks.append(ty)

        def emit_p4_tail(tc):
            nonlocal amx_tok, prev_q3
            par = tc % 2
            ty = y_toks[tc]
            Cs = lnstS[par]
            pAs = act(lambda s, tc=tc, Cs=Cs, par=par: s.activation(
                scS[par], yv[:, D * tc:D * (tc + 1)], AF.Identity,
                bias=zerob[:, 0:1], accum_out=Cs[:, 0:1]),
                waits=[ty, sA_war[par]])
            pBs = act(lambda s, tc=tc, Cs=Cs, par=par: s.activation(
                scS[par], yv[:, D * tc:D * (tc + 1)], AF.Square,
                accum_out=Cs[:, 1:2]), waits=[pAs])
            sA_war[par] = pBs
            sm, m4s = stats_math(Cs, pAs, pBs,
                                 mean_out=lncm[:, tc:tc + 1])
            s4 = act(lambda s, Cs=Cs: s.activation(
                Cs[:, 5:6], Cs[:, 3:4], AF.Sqrt, bias=epsb[:, 0:1]),
                waits=[m4s])
            s5 = dve(lambda v, tc=tc, Cs=Cs: v.reciprocal(
                lncr[:, tc:tc + 1], Cs[:, 5:6]), waits=[s4])
            p4s["s5"] = s5
            guP, gu2P, gtP = ([gu1, gu2, gtrt] if par == 0
                              else [gu1b, gu2b, gtrtb])
            r1 = dve(lambda v, tc=tc, guP=guP: v.reduce_max(
                guP[:, :], yv[:, D * tc:D * (tc + 1)], AX.X),
                waits=[ty, gu_war[par]])
            r2_ = dve(lambda v, tc=tc, gu2P=gu2P: v.reduce_max(
                gu2P[:, :], yv[:, D * tc:D * (tc + 1)], AX.X, negate=True),
                waits=[ty])
            q1 = dve(lambda v, tc=tc, guP=guP: v.tensor_scalar(
                guP[:, :], guP[:, :], lncm[:, tc:tc + 1], lncr[:, tc:tc + 1],
                ALU.subtract, ALU.mult), waits=[r1, sm, s5])
            q2 = dve(lambda v, tc=tc, gu2P=gu2P: v.tensor_scalar(
                gu2P[:, :], gu2P[:, :], lncm[:, tc:tc + 1], lncr[:, tc:tc + 1],
                ALU.add, ALU.mult), waits=[r2_, sm, s5])
            q3 = dve(lambda v, guP=guP, gu2P=gu2P, gtP=gtP: v.tensor_max(
                gtP[:, :], guP[:, :], gu2P[:, :]), waits=[q1, q2])
            gu_war[par] = q3
            prev_q3 = q3
            if tc == 0:
                amx_tok = dve(lambda v, gtP=gtP: v.tensor_copy(
                    amx[0][:, :], gtP[:, :]), waits=[q3])
            else:
                amx_tok = dve(lambda v, tc=tc, gtP=gtP: v.tensor_max(
                    amx[tc % 2][:, :], amx[1 - tc % 2][:, :], gtP[:, :]),
                    waits=[q3, amx_tok])
            gu_war[par] = amx_tok

        for tc in range(NT):
            emit_p4_head(tc)
            if tc >= 1:
                emit_p4_tail(tc - 1)
            if tc < 2 * NIC:
                emit_wstep(tc)
        emit_p4_tail(NT - 1)
        for i in range(min(NT, 2 * NIC), 2 * NIC):
            emit_wstep(i)
        amx_fin = amx[(NT - 1) % 2]
        tok_fb = dma("sync", fbsb, fb_d[:, :], waits=[p4s["mm"]])

        tok = maxtree(amx_fin, [wmx_tok, amx_tok, tokw])
        tok = dve(lambda v: v.tensor_copy(scl[0:1, 0:1], trt[0:1, 0:1]),
                  waits=[tok])
        tgd = dma("gpsimd", cc_in[0:1, 0:1], scl[0:1, 0:1], waits=[tok], q="gc")
        tcc = P.op("gpsimd", lambda g: g.collective_compute(
            "AllReduce", ALU.max, replica_groups=[list(range(NCORES))],
            ins=[cc_in.ap().opt()], outs=[cc_out.ap().opt()]),
            waits=[tgd], inc="cc", amt=1)
        tgd2 = dma("gpsimd", ccs[0:1, 0:1], cc_out[0:1, 0:1], waits=[tcc], q="gc")
        tok = dve(lambda v: v.tensor_scalar_max(scl[0:1, 1:2], ccs[0:1, 0:1],
                                                1e-12), waits=[tgd2])
        tok = dve(lambda v: v.reciprocal(scl[0:1, 5:6], scl[0:1, 1:2]),
                  waits=[tok])
        tok = dve(lambda v: v.tensor_scalar_mul(scl[0:1, 2:3], scl[0:1, 5:6],
                                                FP8MAX), waits=[tok])
        tok_xs = dve(lambda v: v.tensor_scalar_min(scl[0:1, 3:4], scl[0:1, 2:3],
                                                   FP8MAX), waits=[tok])
        tok_xsi = dve(lambda v: v.reciprocal(scl[0:1, 4:5], scl[0:1, 3:4]),
                      waits=[tok_xs])
        tok_dq = dve(lambda v: v.tensor_mul(dqt[:, :], scl[0:1, 4:5],
                                            wscl[0:1, 4:5]),
                     waits=[tok_xsi, tok_wsi])
        bc_mm1 = pe(lambda t: t.matmul(ps32b[0:128, 1541:1542], ones128[:, :],
                                       scl[0:1, 3:4], start=True, stop=True),
                    waits=[tok_xs, bc_mm2])
        bc_mm3 = pe(lambda t: t.matmul(ps32b[0:128, 1542:1543], ones128[:, :],
                                       dqt[:, :], start=True, stop=True),
                    waits=[tok_dq, bc_mm1])
        txsb = dve(lambda v: v.tensor_copy(xs_bc[:, :], ps32b[0:128, 1541:1542]),
                   waits=[bc_mm1])
        tdqb = dve(lambda v: v.tensor_copy(dq_bc[:, :], ps32b[0:128, 1542:1543]),
                   waits=[bc_mm3])
        trs2 = dve(lambda v: v.tensor_scalar_mul(rs2[:, 0:NT], lncr[:, 0:NT],
                                                 xs_bc[:, 0:1]),
                   waits=[txsb, p4s["s5"]])
        tfbr = dve(lambda v: v.tensor_scalar(
            bosr[0:1, :], fbsb[0:1, :], scl[0:1, 3:4], wscl[0:1, 3:4],
            ALU.mult, ALU.mult), waits=[tok_fb, tok_xs, tok_ws])

        qsets = [(zq.ap()[:, :], cq.ap()[:, :], vq3, lq.ap()[:, :],
                  ffnb.ap()[:, :], lnb, negm2),
                 (zq2, cq2, vq32, lq2, ffnb2, lnb2, negm3)]
        p6 = {"ev6": [None, None],
              "tvq_war": [y_toks[NT - 2], y_toks[NT - 1]],
              "d6_war": [None, None], "ffn_evac": [None, None],
              "out_dma": [None, None],
              "ln2q_war": [o_last, wq_toks[-1], oproj_evac[0],
                           oproj_evac[1], oproj_evac[2],
                           sA_war[0], sA_war[1]],
              "mm": [None, None], "d7_war": [None, None],
              "pB_war": [None, None]}
        final_toks = []

        def emit_quant(tc):
            par = tc % 2
            zqp, cqp, vq3p, lqp, ffnp, lnbp, ngp = qsets[par]
            pb = dve(lambda v, tc=tc, ngp=ngp: v.tensor_scalar(
                ngp[:, 0:1], lncm[:, tc:tc + 1],
                rs2[:, tc:tc + 1], -1.0, ALU.mult, ALU.mult),
                waits=[trs2, p6["tvq_war"][par]])
            tvq = act(lambda s, tc=tc, zqp=zqp, ngp=ngp: s.activation(
                zqp, yv[:, D * tc:D * (tc + 1)], AF.Identity,
                bias=ngp[:, 0:1], scale=rs2[:, tc:tc + 1]),
                waits=[pb, y_toks[tc]])
            tq = velt(lqp, zqp, [tvq], cqx=cqp, vq3x=vq3p)
            p6["tvq_war"][par] = tq
            ttr = None
            for j in range(8):
                ttr = pe(lambda t, j=j, lqp=lqp: t.transpose(
                    psbf[:, 128 * j:128 * (j + 1)], lqp[:, 128 * j:128 * (j + 1)],
                    idsb[:, :]),
                    waits=[tq, p6["ev6"][1 - par]] + p6["ln2q_war"]
                    if j == 0 else [])
                p6["ln2q_war"] = []
            ot = ln2qT.rearrange("p (j t) -> p j t", j=8)[
                :, :, 128 * tc:128 * (tc + 1)]
            it = psbf.rearrange("p (j t) -> p j t", j=8)
            p6["ev6"][par] = dve(lambda v, ot=ot, it=it: v.tensor_copy(ot, it),
                                 waits=[ttr])
            return p6["ev6"][par]

        def emit_ffn_mm(tc):
            par = tc % 2
            pv = ps32a[:, 1024 * par:1024 * par + 1024]
            mm = None
            for n in range(2):
                for ic in range(NIC):
                    w0 = []
                    if tc == 0 and n == 0 and ic == 0:
                        w0 = [tdqb, txsb, wq_toks[-1]]
                    if n == 0 and ic == 0:
                        w0 = w0 + [p6["ffn_evac"][par], ev_list[tc]]
                    mm = P.op("tensor", lambda t, pv=pv, n=n, ic=ic, tc=tc:
                              t.matmul(
                                  pv[:, 512 * n:512 * n + 512],
                                  ln2qT[:, T * ic + 128 * tc:T * ic + 128 * (tc + 1)],
                                  wffq[:, 1024 * ic + 512 * n:1024 * ic + 512 * n + 512],
                                  start=(ic == 0), stop=False),
                              waits=w0, inc=None)
                mm = P.op("tensor", lambda t, pv=pv, n=n: t.matmul(
                    pv[:, 512 * n:512 * n + 512], ones128b[:, :],
                    bosr[0:1, 512 * n:512 * n + 512],
                    start=False, stop=True),
                    waits=[tfbr] if tc == 0 and n == 0 else [],
                    inc="pe")
            p6["mm"][par] = mm

        def emit_ffn_rest(tc):
            par = tc % 2
            zqp, cqp, vq3p, lqp, ffnp, lnbp, ngp = qsets[par]
            pv = ps32a[:, 1024 * par:1024 * par + 1024]
            Cd = lnstD[par]
            f1 = act(lambda s, pv=pv, ffnp=ffnp, Cd=Cd: s.activation(
                ffnp, pv, AF.Identity, bias=zerob[:, 0:1],
                scale=dq_bc[:, 0:1], accum_out=Cd[:, 0:1]),
                waits=[p6["mm"][par], tdqb, tok_zerob, p6["d6_war"][par]])
            p6["ffn_evac"][par] = f1
            pBd = act(lambda s, ffnp=ffnp, Cd=Cd, par=par: s.activation(
                wroll[par][:, :], ffnp, AF.Square,
                accum_out=Cd[:, 1:2]), waits=[f1, p6["pB_war"][par]])
            p6["pB_war"][par] = pBd
            _, m4d = stats_math(Cd, f1, pBd)
            d4 = act(lambda s, Cd=Cd: s.activation(
                Cd[:, 5:6], Cd[:, 3:4], AF.Sqrt, bias=epsb[:, 0:1]),
                waits=[m4d])
            d5 = dve(lambda v, Cd=Cd: v.reciprocal(Cd[:, 6:7], Cd[:, 5:6]),
                     waits=[d4])
            d5b = dve(lambda v, Cd=Cd: v.tensor_scalar(
                Cd[:, 7:8], Cd[:, 2:3], Cd[:, 6:7], -1.0,
                ALU.mult, ALU.mult), waits=[d5])
            d6 = act(lambda s, ffnp=ffnp, lnbp=lnbp, Cd=Cd: s.activation(
                lnbp, ffnp, AF.Identity, bias=Cd[:, 7:8],
                scale=Cd[:, 6:7]), waits=[d5b, p6["d7_war"][par]])
            p6["d6_war"][par] = d6

        def emit_ffn_out(tc):
            par = tc % 2
            zqp, cqp, vq3p, lqp, ffnp, lnbp, ngp = qsets[par]
            d7 = dve(lambda v, tc=tc, par=par, lnbp=lnbp: v.tensor_add(
                outb[par][:, :], yv[:, D * tc:D * (tc + 1)], lnbp),
                waits=[p6["d6_war"][par], p6["out_dma"][par]])
            p6["d7_war"][par] = d7
            od = dma("sync", out_d[128 * tc:128 * (tc + 1), :], outb[par][:, :],
                     waits=[d7], q="so%d" % par)
            p6["out_dma"][par] = od
            final_toks.append(od)

        ev_list = {}
        ev_list[0] = emit_quant(0)
        for tc in range(1, NT):
            emit_ffn_mm(tc - 1)
            ev_list[tc] = emit_quant(tc)
            emit_ffn_rest(tc - 1)
            if tc >= 2:
                emit_ffn_out(tc - 2)
        emit_ffn_mm(NT - 1)
        emit_ffn_rest(NT - 1)
        emit_ffn_out(NT - 2)
        emit_ffn_out(NT - 1)

        P.op("sync", lambda s: s.nop(),
             waits=[final_toks[-1], final_toks[-2] if NT > 1 else None])

        with nc.allow_low_precision(reason="bf16 operands by design (blessed)"), \
                nc.Block() as block:
            def make_runner(ename):
                def runner(en):
                    for fn, w, inc, amt in P.stream[ename]:
                        for s, v in w:
                            en.wait_ge(sems[s], v)
                        ins = fn(en)
                        if inc is not None and ins is not None:
                            ins.then_inc(sems[inc], amt)
                return runner
            block.sync(make_runner("sync"))
            block.tensor(make_runner("tensor"))
            block.vector(make_runner("vector"))
            block.scalar(make_runner("scalar"))
            block.gpsimd(make_runner("gpsimd"))
    return nc


def _host_prep(L0_attn_x, L0_attn_Wq, L0_attn_bq, L0_attn_Wk, L0_attn_bk,
               L0_attn_Wv, L0_attn_bv, L0_attn_Wo, L0_attn_bo,
               L0_ffn_weight, L0_ffn_bias, T):
    perm = np.concatenate([np.concatenate([np.arange(64 * h, 64 * (h + 1), 2),
                                           np.arange(64 * h + 1, 64 * (h + 1), 2)])
                           for h in range(NH)])

    def lay_lhsT(Wp):
        A = np.ascontiguousarray(Wp.T)
        return np.ascontiguousarray(
            A.reshape(8, 128, D).transpose(1, 0, 2).reshape(128, 8 * D))

    def lay_rhs(W):
        A = np.ascontiguousarray(W.T)
        return np.ascontiguousarray(
            A.reshape(8, 128, D).transpose(1, 0, 2).reshape(128, 8 * D))

    wq = lay_lhsT(np.asarray(L0_attn_Wq)[perm, :])
    wk = lay_lhsT(np.asarray(L0_attn_Wk)[perm, :])
    wv = lay_lhsT(np.asarray(L0_attn_Wv)[perm, :])
    wo = lay_rhs(np.asarray(L0_attn_Wo))
    wf = lay_rhs(np.asarray(L0_ffn_weight))
    bq = np.ascontiguousarray(np.asarray(L0_attn_bq)[perm].reshape(8, 128).T)
    bk = np.ascontiguousarray(np.asarray(L0_attn_bk)[perm].reshape(8, 128).T)
    bv = np.ascontiguousarray(np.asarray(L0_attn_bv)[perm].reshape(8, 128).T)
    bo = np.ascontiguousarray(np.broadcast_to(np.asarray(L0_attn_bo), (128, D)))
    fb = np.ascontiguousarray(np.broadcast_to(np.asarray(L0_ffn_bias), (128, D)))
    x = np.asarray(L0_attn_x)
    in_maps = []
    for b in range(NCORES):
        in_maps.append({
            "x": np.ascontiguousarray(x[b, :T]), "wq": wq, "wk": wk, "wv": wv,
            "wo": wo, "wf": wf, "bq": bq, "bk": bk, "bv": bv, "bo": bo, "fb": fb,
        })
    return in_maps


def kernel(L0_attn_x, L0_attn_Wq, L0_attn_bq, L0_attn_Wk, L0_attn_bk,
           L0_attn_Wv, L0_attn_bv, L0_attn_Wo, L0_attn_bo,
           L0_ffn_weight, L0_ffn_bias):
    T = np.asarray(L0_attn_x).shape[1]
    in_maps = _host_prep(L0_attn_x, L0_attn_Wq, L0_attn_bq, L0_attn_Wk,
                         L0_attn_bk, L0_attn_Wv, L0_attn_bv, L0_attn_Wo,
                         L0_attn_bo, L0_ffn_weight, L0_ffn_bias, T)
    nc = build(T)
    res = run_bass_kernel_spmd(nc, in_maps, core_ids=list(range(NCORES)))
    out = np.stack([r["out"] for r in res.results], axis=0)
    return out.astype(np.float32)
